# version 17
# baseline (speedup 1.0000x reference)
import numpy as np
import ml_dtypes
import concourse.bass as bass
import concourse.mybir as mybir
from contextlib import ExitStack

F32 = mybir.dt.float32
BF16 = mybir.dt.bfloat16
I8 = mybir.dt.int8
AF = mybir.ActivationFunctionType
ALU = mybir.AluOpType
AX = mybir.AxisListType.X

S, C, D, HH, II = 512, 8, 1024, 256, 2816
R = C * S
NKD, NTC, NIC, NHC = 8, 4, 22, 2
LR = 1e-3
COEF = 2.0 / (4096.0 * 1024.0)
EPS = 1e-5
INV_SQRT_2PI = 0.3989422804014327
TINY = 2.0 ** -40
ENGINES = ("sync", "vector", "scalar", "tensor", "gpsimd")
V_SA, V_CA1, V_GA, V_SB, V_CB1, V_GB = range(6)


class Sem:
    def __init__(self, es, nc, name):
        self.h = es.enter_context(nc.semaphore(name))
        self.n = 0


class Prog:
    def __init__(self):
        self.q = {e: [] for e in ENGINES}

    def op(self, eng, fn, waits=(), inc=None, inc_n=1):
        if inc is not None:
            inc.n += inc_n
        def run(e, fn=fn, waits=tuple(waits), inc=inc, inc_n=inc_n):
            for s, v in waits:
                e.wait_ge(s.h, v)
            ins = fn(e)
            if inc is not None:
                ins.then_inc(inc.h, inc_n)
        self.q[eng].append(run)
        return inc.n if inc is not None else None

    def wait(self, eng, sem, value=None):
        v = sem.n if value is None else value
        self.q[eng].append(lambda e, s=sem, v=v: e.wait_ge(s.h, v))


def build_nc(n_steps=8, do_ffn=True, dbg=()):
    dbg = set(dbg)
    nc = bass.Bass("TRN2", target_bir_lowering=False)

    def din(name, shape, dt=F32):
        return nc.dram_tensor(name, shape, dt, kind="ExternalInput").ap()

    x_in = din("x", [C, S, D])
    tw_in = [din(n, [D, D]) for n in ("tk", "tv", "tq")]
    w1_in = din("W1", [HH, D]); b1_in = din("b1", [HH])
    w2_in = din("W2", [D, HH]); b2_in = din("b2", [D])
    clipx_in = din("clip_x", [1]); clipd_in = din("clip_down", [1])
    wg_in = din("wg", [II, D]); wu_in = din("wu", [II, D]); wd_in = din("wd", [D, II])
    c_in = din("cvec", [C, D]); wmod_in = din("wmod", [D, 6 * D])
    idf_in = din("identf", [128, 128]); idb_in = din("identb", [128, 128], BF16)
    out_hbm = nc.dram_tensor("out", [C, S, D], F32, kind="ExternalOutput").ap()

    def dint(name, shape, dt=F32):
        return nc.dram_tensor(name, shape, dt).ap()

    x2d = dint("x2d", [R, D])
    h2d = dint("h2d", [NKD, 128, R])
    Pd = dint("Pd", [NIC, 128, R])
    pwd = dint("pwd", [3, NKD, 2, 128, D], BF16)
    igtd = dint("igtd", [NKD, 128, II], BF16)
    iutd = dint("iutd", [NKD, 128, II], BF16)
    idtd = dint("idtd", [NIC, 128, D], BF16)
    rowsd = dint("rowsd", [4, C, D])
    sdrowd = dint("sdrowd", [1, D])
    gin_d = dint("gin", [128, 4106]); gout_d = dint("gout", [128, 4106])
    mx1i = dint("mx1i", [1, 8]); mx1o = dint("mx1o", [1, 8])
    mx2i = dint("mx2i", [1, 8]); mx2o = dint("mx2o", [1, 8])
    RG = [list(range(8))]

    dbg_names = []
    es = ExitStack()

    def sb(name, cols, dt=F32):
        return es.enter_context(nc.sbuf_tensor(name, [128, cols], dt))

    pb = [es.enter_context(nc.psum_tensor(f"pb{i}", [128, 512], F32)) for i in range(6)]
    pbb = [es.enter_context(nc.psum_tensor(f"pbb{i}", [128, 1024], BF16)) for i in range(2)]

    sST = Sem(es, nc, "sST")
    sT = Sem(es, nc, "sT"); sV = Sem(es, nc, "sV"); sA = Sem(es, nc, "sA")
    sG = Sem(es, nc, "sG"); sCC = Sem(es, nc, "sCC")
    sINIT = Sem(es, nc, "sINIT")
    sWM = [Sem(es, nc, "sWM0"), Sem(es, nc, "sWM1")]
    sWP = Sem(es, nc, "sWP")
    sPW = [Sem(es, nc, "sPW0"), Sem(es, nc, "sPW1")]
    sX = Sem(es, nc, "sX")
    sBC = Sem(es, nc, "sBC")
    sGO = [Sem(es, nc, f"sGO{i}") for i in range(4)]
    sRW = Sem(es, nc, "sRW")
    sPWD = Sem(es, nc, "sPWD")
    sX2 = Sem(es, nc, "sX2")
    sH2S = [Sem(es, nc, "sH2S0"), Sem(es, nc, "sH2S1")]
    sGS = [Sem(es, nc, "sGS0"), Sem(es, nc, "sGS1")]
    ALL_DMA_SEMS = [sST, sINIT, sWM[0], sWM[1], sWP, sPW[0], sPW[1], sX, sBC,
                    sGO[0], sGO[1], sGO[2], sGO[3],
                    sRW, sPWD, sX2, sH2S[0], sH2S[1],
                    sGS[0], sGS[1]]
    ESEM = {"vector": sV, "scalar": sA, "gpsimd": sG, "tensor": sT}

    P = Prog()

    IDF = sb("IDF", 128)
    IDB = sb("IDB", 128, BF16)
    ONESROW = sb("ONESROW", 128)
    CB49 = sb("CB49", 2)
    CLIPS = sb("CLIPS", 2)
    MAXH = sb("MAXH", 256)
    SMALL = sb("SMALL", 32)
    BCA = sb("BCA", 1024)
    es_ttt = ExitStack()

    def sbt(name, cols, dt=F32):
        return es_ttt.enter_context(nc.sbuf_tensor(name, [128, cols], dt))

    MODF = sbt("MODF", 384)
    W1T_m = sbt("W1T_m", 2048); W2_m = sbt("W2_m", 2048)
    B1C = sbt("B1C", 2); B2C = sbt("B2C", 8)
    W1TH = sbt("W1TH", 2048, BF16); W1TL = sbt("W1TL", 2048, BF16)
    W2H = sbt("W2H", 2048, BF16); W2L = sbt("W2L", 2048, BF16)
    W2TH = sbt("W2TH", 2048, BF16); W2TL = sbt("W2TL", 2048, BF16)
    PW = [sbt(f"PW{i}", 2048, BF16) for i in range(2)]
    GBIA = sbt("GBIA", 16)
    GSTG = [sbt(f"GSTG{i}", 512) for i in range(2)]
    GOUT = [sbt(f"GOUT{i}", 512) for i in range(4)]
    H2S = [sbt(f"H2S{i}", 512) for i in range(2)]
    ZTS = [sbt(f"ZTS{i}", 1024, BF16) for i in range(2)]

    def modcol(vec, kd, b):
        i = vec * 64 + b * 8 + kd
        return MODF[:, i:i + 1]

    def modset(vec, b):
        i = vec * 64 + b * 8
        return MODF[:, i:i + 8]

    def load(out_ap, in_ap, sem, eng="sync", waits=(), slow=False):
        if slow:
            return P.op(eng, lambda e, o=out_ap, i=in_ap: e.dma_start(
                out=o, in_=i, allow_slow_non_contiguous=True),
                waits=waits, inc=sem, inc_n=16)
        return P.op(eng, lambda e, o=out_ap, i=in_ap: e.dma_start(out=o, in_=i),
                    waits=waits, inc=sem, inc_n=16)

    def store(out_ap, in_ap, sem, eng="gpsimd", waits=()):
        return P.op(eng, lambda e, o=out_ap, i=in_ap: e.dma_start(out=o, in_=i),
                    waits=waits, inc=sem, inc_n=16)

    def ew(eng, fn, waits=()):
        sw = [(ESEM[eng], ESEM[eng].n)]
        return P.op(eng, fn, waits=list(waits) + sw, inc=ESEM[eng])

    def dump(name, ap_, waits):
        if name not in dbg:
            return
        t = nc.dram_tensor("dbg_" + name, list(ap_.shape), ap_.dtype,
                           kind="ExternalOutput").ap()
        dbg_names.append(name)
        P.op("gpsimd", lambda e, o=t, i=ap_: e.dma_start(out=o, in_=i),
             waits=waits, inc=sST, inc_n=16)

    class Banks:
        def __init__(self, n):
            self.n = n
            self.guard = [None] * n
            self.rr = 0

        def acq(self, pick=None):
            i = self.rr if pick is None else pick
            if pick is None:
                self.rr = (self.rr + 1) % self.n
            if self.guard[i] is not None:
                s, v = self.guard[i]
                P.wait("tensor", s, v)
                self.guard[i] = None
            return i

        def rel(self, i, sem, count):
            self.guard[i] = (sem, count)

    BK = Banks(6)
    BKB = Banks(2)

    def mm_group(bank_ap, mms, waits=()):
        n = len(mms)
        for j, (lt, rh) in enumerate(mms):
            P.op("tensor",
                 lambda e, o=bank_ap, lt=lt, rh=rh, st=(j == 0), sp=(j == n - 1):
                 e.matmul(o, lt, rh, start=st, stop=sp),
                 waits=waits if j == 0 else (),
                 inc=sT if j == n - 1 else None)
        return sT.n

    def tr_group(pairs, ident, waits=()):
        n = len(pairs)
        for j, (o, i) in enumerate(pairs):
            P.op("tensor", lambda e, o=o, i=i, idn=ident: e.transpose(o, i, idn),
                 waits=waits if j == 0 else (),
                 inc=sT if j == n - 1 else None)
        return sT.n

    def copy_pass(eng, dst, src, waits=()):
        if eng == "scalar":
            return ew(eng, lambda e, d=dst, s=src: e.activation(d, s, AF.Copy),
                      waits=waits)
        return ew(eng, lambda e, d=dst, s=src: e.tensor_copy(d, s), waits=waits)

    def split_hi_lo(src_ap, hi_ap, lo_ap, waits_hi=(), waits_lo=()):
        a = copy_pass("scalar", hi_ap, src_ap, waits=waits_hi)
        v = ew("vector", lambda e, d=lo_ap, s=src_ap, h=hi_ap: e.tensor_sub(d, s, h),
               waits=[(sA, a)] + list(waits_lo))
        return a, v

    es_pro = ExitStack()

    def sbp(name, cols, dt=F32):
        return es_pro.enter_context(nc.sbuf_tensor(name, [128, cols], dt))

    load(IDF[:], idf_in, sINIT)
    load(IDB[:], idb_in, sINIT)
    ew("vector", lambda e: e.memset(ONESROW[0:1, :], 1.0))
    ew("vector", lambda e: e.memset(CB49[:, 0:1], 127.49))
    ew("vector", lambda e: e.memset(CB49[:, 1:2], 255.98))
    ew("vector", lambda e: e.memset(MAXH[:], 0.0))

    W1N = sbp("W1N", 2048)
    CT = sbp("CT", 1024)
    SILCT = sbp("SILCT", 64)
    load(W1N[:].rearrange("p (c d) -> p c d", c=2),
         w1_in.rearrange("(c p) d -> p c d", p=128), sINIT)
    load(W2_m[:].rearrange("p (c h) -> p c h", c=8),
         w2_in.rearrange("(c p) h -> p c h", p=128), sINIT)
    load(CLIPS[0:1, 0:1], clipx_in.rearrange("(o x) -> o x", o=1), sINIT)
    load(CLIPS[0:1, 1:2], clipd_in.rearrange("(o x) -> o x", o=1), sINIT)
    load(B1C[:], b1_in.rearrange("(c p) -> p c", p=128), sINIT, slow=True)
    load(B2C[:], b2_in.rearrange("(c p) -> p c", p=128), sINIT, slow=True)
    load(CT[0:8, :], c_in, sINIT)
    vINIT = sINIT.n

    for g in range(4):
        bk = BK.acq()
        tiles = []
        for j in range(4):
            hc, kd = divmod(g * 4 + j, 8)
            tiles.append((pb[bk][:, j * 128:(j + 1) * 128],
                          W1N[:, hc * 1024 + kd * 128: hc * 1024 + (kd + 1) * 128]))
        tt = tr_group(tiles, IDF[:], waits=[(sINIT, vINIT)])
        vv = None
        for j in range(4):
            hc, kd = divmod(g * 4 + j, 8)
            dst = W1T_m[:, kd * 256 + hc * 128: kd * 256 + (hc + 1) * 128]
            vv = copy_pass("vector", dst, pb[bk][:, j * 128:(j + 1) * 128],
                           waits=[(sT, tt)] if j == 0 else ())
        BK.rel(bk, sV, vv)
    v_w1t = sV.n

    def resplit_params(waits_w1, waits_w2):
        split_hi_lo(W1T_m[:], W1TH[:], W1TL[:], waits_hi=waits_w1)
        split_hi_lo(W2_m[:], W2H[:], W2L[:], waits_hi=waits_w2)
        for g in range(4):
            bk = BK.acq()
            tiles = []
            for j in range(4):
                c2, hc = divmod(g * 4 + j, 2)
                tiles.append((pb[bk][:, j * 128:(j + 1) * 128],
                              W2_m[:, c2 * 256 + hc * 128: c2 * 256 + (hc + 1) * 128]))
            tt = tr_group(tiles, IDF[:], waits=waits_w2 if g == 0 else ())
            vv = None
            for j in range(4):
                c2, hc = divmod(g * 4 + j, 2)
                sl = slice(hc * 1024 + c2 * 128, hc * 1024 + (c2 + 1) * 128)
                src = pb[bk][:, j * 128:(j + 1) * 128]
                ah = copy_pass("scalar", W2TH[:, sl], src,
                               waits=[(sT, tt)] if j == 0 else ())
                vv = ew("vector", lambda e, d=W2TL[:, sl], s=src, h=W2TH[:, sl]:
                        e.tensor_sub(d, s, h), waits=[(sA, ah)])
            BK.rel(bk, sV, vv)
        return sA.n, sV.n

    resplit_params([(sV, v_w1t)], [(sINIT, vINIT)])

    aC = ew("scalar", lambda e: e.activation(CT[0:8, :], CT[0:8, :], AF.Silu),
            waits=[(sINIT, vINIT)])
    bk = BK.acq()
    tS = tr_group([(pb[bk][:, kd * 8:(kd + 1) * 8],
                    CT[0:8, kd * 128:(kd + 1) * 128]) for kd in range(NKD)],
                  IDF[0:8, 0:8], waits=[(sA, aC), (sINIT, vINIT)])
    vS = copy_pass("vector", SILCT[:], pb[bk][:, 0:64], waits=[(sT, tS)])
    BK.rel(bk, sV, vS)

    WMC = [sbp(f"WMC{i}", 4096) for i in range(2)]
    MODT = sbp("MODT", 6144)
    wm_guard = {}
    vMT = None
    for msl in range(12):
        waits = [(sT, wm_guard[msl - 2])] if msl >= 2 else ()
        ldw = load(WMC[msl % 2][:].rearrange("p (kd n) -> p kd n", kd=NKD),
                   wmod_in[:, msl * 512:(msl + 1) * 512]
                   .rearrange("(kd p) n -> p kd n", p=128), sWM[msl % 2],
                   waits=waits)
        bk = BK.acq()
        mms = [(SILCT[:, kd * 8:(kd + 1) * 8],
                WMC[msl % 2][:, kd * 512:(kd + 1) * 512]) for kd in range(NKD)]
        tmm = mm_group(pb[bk][0:8, :], mms, waits=[(sWM[msl % 2], ldw), (sV, vS)])
        wm_guard[msl] = tmm
        vMT = copy_pass("vector", MODT[0:8, msl * 512:(msl + 1) * 512],
                        pb[bk][0:8, :], waits=[(sT, tmm)])
        BK.rel(bk, sV, vMT)
    vMOD = None
    for j in range(48):
        vec, kd = divmod(j, 8)
        bk = BK.acq()
        tt = tr_group([(pb[bk][:, 0:8], MODT[0:8, j * 128:(j + 1) * 128])],
                      IDF[0:8, 0:8], waits=[(sV, vMT)] if j == 0 else ())
        dst = MODF[:, vec * 64:(vec + 1) * 64].rearrange(
            "p (b k) -> p b k", b=8, k=8)[:, :, kd:kd + 1]
        vMOD = ew("vector", lambda e, d=dst, s=pb[bk][:, 0:8].unsqueeze(2):
                  e.tensor_copy(d, s), waits=[(sT, tt)])
        BK.rel(bk, sV, vMOD)
    dump("modf", MODF[:], [(sV, vMOD)])

    for vec in (V_CA1, V_CB1):
        sl = slice(vec * 64, (vec + 1) * 64)
        vMOD = ew("vector", lambda e, sl=sl: e.tensor_scalar(
            MODF[:, sl], MODF[:, sl], 1.0, None, ALU.add), waits=[(sV, vMOD)])
    GVECS = (V_GA, V_GB)
    def gsl(vec):
        return slice(vec * 64, (vec + 1) * 64)
    for i, vec in enumerate(GVECS):
        vMOD = ew("vector", lambda e, sl=gsl(vec), t=CT[:, i * 256:i * 256 + 64]:
                  e.tensor_scalar(t, MODF[:, sl], 0.0, 2.0, ALU.is_ge, ALU.mult),
                  waits=[(sV, vMOD)] if i == 0 else ())
    for i, vec in enumerate(GVECS):
        vMOD = ew("vector", lambda e, t=CT[:, i * 256:i * 256 + 64]:
                  e.tensor_scalar(t, t, -1.0, None, ALU.add))
    for i, vec in enumerate(GVECS):
        aM = ew("scalar", lambda e, sl=gsl(vec), t=CT[:, i * 256 + 64:i * 256 + 128]:
                e.activation(t, MODF[:, sl], AF.Abs),
                waits=[(sV, vMOD)] if i == 0 else ())
    for i, vec in enumerate(GVECS):
        vMOD = ew("vector", lambda e, t=CT[:, i * 256 + 64:i * 256 + 128]:
                  e.tensor_scalar(t, t, 0.25, None, ALU.max),
                  waits=[(sA, aM)] if i == 0 else ())
    for i, vec in enumerate(GVECS):
        vMOD = ew("vector", lambda e, sl=gsl(vec), t=CT[:, i * 256:i * 256 + 64],
                  t2=CT[:, i * 256 + 64:i * 256 + 128]:
                  e.tensor_mul(MODF[:, sl], t2, t))
    vMODF = vMOD
    dump("modp", MODF[:], [(sV, vMODF)])

    ROWS = sbp("ROWS", 128)
    for ri, vec in enumerate((V_GA, V_CB1, V_SB, V_GB)):
        for b in range(C):
            bk = BK.acq()
            tt = tr_group([(pb[bk][0:8, 0:128], modset(vec, b))], IDF[:],
                          waits=[(sV, vMODF)])
            vv = copy_pass("vector", ROWS[0:8, :], pb[bk][0:8, 0:128],
                           waits=[(sT, tt), (sRW, sRW.n)])
            BK.rel(bk, sV, vv)
            store(rowsd[ri, b, :].rearrange("(a f) -> a f", a=8), ROWS[0:8, :],
                  sRW, waits=[(sV, vv)])
    stROWS = sRW.n

    TKN = sbp("TKN", 1024)
    TKH = sbp("TKH", 1024, BF16)
    TKL = sbp("TKL", 1024, BF16)
    PWS = sbp("PWS", 2048, BF16)
    st_pws_prev = None
    prev_ah = prev_vl = None
    for w in range(3):
        for mh in range(NKD):
            wts = [(sPWD, st_pws_prev)] if st_pws_prev else []
            ldw_ = ([(sA, prev_ah), (sV, prev_vl)] if prev_ah is not None else [])
            ld = load(TKN[:], tw_in[w][mh * 128:(mh + 1) * 128, :], sWP, waits=ldw_)
            ah, vl = split_hi_lo(TKN[:], TKH[:], TKL[:],
                                 waits_hi=[(sWP, ld)] + wts)
            prev_ah, prev_vl = ah, vl
            lastev = {"scalar": None, "vector": None}
            for hl, src in ((0, TKH), (1, TKL)):
                for g in range(2):
                    bk = BKB.acq()
                    tiles = [(pbb[bk][:, j * 128:(j + 1) * 128],
                              src[:, (g * 4 + j) * 128:(g * 4 + j + 1) * 128])
                             for j in range(4)]
                    wt = [(sA, ah), (sV, vl)] if (hl == 0 and g == 0) else ()
                    tt = tr_group(tiles, IDB[:], waits=wt)
                    dst = PWS[:, hl * 1024 + g * 512: hl * 1024 + (g + 1) * 512]
                    eng = "scalar" if (g + hl) % 2 == 0 else "vector"
                    vv = copy_pass(eng, dst, pbb[bk][:, 0:512], waits=[(sT, tt)])
                    lastev[eng] = vv
                    BKB.rel(bk, ESEM[eng], vv)
            wts2 = [(ESEM[k], v) for k, v in lastev.items() if v is not None]
            store(pwd[w, mh, 0, :, :], PWS[:, 0:1024], sPWD, waits=wts2)
            st_pws_prev = store(pwd[w, mh, 1, :, :], PWS[:, 1024:2048], sPWD)
    stPW = sPWD.n

    es_pro.close()
    XT = sbt("XT", 4096)
    STAT = sbt("STAT", 64)
    TMPA = sbt("TMPA", 4096)
    T512 = sbt("T512", 512)
    H1H = sbt("H1H", 4096, BF16); H1L = sbt("H1L", 4096, BF16)
    ZH = sbt("ZH", 4096, BF16); ZL = sbt("ZL", 4096, BF16)
    QH = sbt("QH", 4096, BF16); QL = sbt("QL", 4096, BF16)
    AFB = sbt("AFB", 1024)
    HFF = sbt("HFF", 1024)
    T1K = sbt("T1K", 1024); T2K = sbt("T2K", 1024)
    HH_ = sbt("HH_", 1024, BF16); HL_ = sbt("HL_", 1024, BF16)
    HTH = sbt("HTH", 1024, BF16); HTL = sbt("HTL", 1024, BF16)
    GAH = sbt("GAH", 1024, BF16); GAL = sbt("GAL", 1024, BF16)
    GATH = sbt("GATH", 1024, BF16); GATL = sbt("GATL", 1024, BF16)

    def load_x(t, barrier=False, extra=()):
        w = (([(s, s.n) for s in (sV, sA, sT, sG, sINIT, sWM[0], sWM[1], sWP,
                                  sPWD, sRW)])
             if barrier else []) + list(extra)
        return load(XT[:].rearrange("p (tc d) -> p tc d", tc=NTC),
                    x_in[t].rearrange("(tc p) d -> p tc d", p=128), sX, waits=w)

    def stat_a(tc):
        return STAT[:, tc * 16 + 14: tc * 16 + 15]

    def stat_b(tc):
        return STAT[:, tc * 16 + 15: tc * 16 + 16]

    def emit_ln(src, waits, sqtmp=None):
        sq = sqtmp if sqtmp is not None else T1K
        v = None
        for tc in range(NTC):
            b = tc * 16
            xs = src[:, tc * 1024:(tc + 1) * 1024]
            v = ew("vector", lambda e, b=b, xs=xs: e.reduce_sum(
                STAT[:, b:b + 1], xs, axis=AX),
                waits=waits if tc == 0 else ())
            a = ew("scalar", lambda e, b=b, xs=xs: e.activation(
                sq[:], xs, AF.Square, accum_out=STAT[:, b + 1:b + 2]),
                waits=waits if tc == 0 else ())
        vS1, aS1 = sV.n, sA.n
        def stage(fn, wt=()):
            vv = None
            for tc in range(NTC):
                vv = ew("vector", lambda e, b=tc * 16: fn(e, b),
                        waits=wt if tc == 0 else ())
            return vv
        stage(lambda e, b: e.tensor_scalar(
            STAT[:, b + 2:b + 3], STAT[:, b:b + 1], 1.0 / D, None, ALU.mult),
            wt=[(sV, vS1), (sA, aS1)])
        stage(lambda e, b: e.tensor_scalar(
            STAT[:, b + 3:b + 4], STAT[:, b + 1:b + 2], 1.0 / D, None, ALU.mult))
        stage(lambda e, b: e.scalar_tensor_tensor(
            STAT[:, b + 4:b + 5], STAT[:, b + 2:b + 3], 1.0,
            STAT[:, b + 2:b + 3], ALU.mult, ALU.mult))
        stage(lambda e, b: e.tensor_sub(
            STAT[:, b + 3:b + 4], STAT[:, b + 3:b + 4], STAT[:, b + 4:b + 5]))
        v = stage(lambda e, b: e.tensor_scalar(
            STAT[:, b + 3:b + 4], STAT[:, b + 3:b + 4], EPS, None, ALU.add))
        aq = None
        for tc in range(NTC):
            aq = ew("scalar", lambda e, b=tc * 16: e.activation(
                STAT[:, b + 5:b + 6], STAT[:, b + 3:b + 4], AF.Sqrt),
                waits=[(sV, v)] if tc == 0 else ())
        stage(lambda e, b: e.reciprocal(
            STAT[:, b + 6:b + 7], STAT[:, b + 5:b + 6]), wt=[(sA, aq)])
        stage(lambda e, b: e.scalar_tensor_tensor(
            STAT[:, b + 7:b + 8], STAT[:, b + 6:b + 7], 1.0,
            STAT[:, b + 6:b + 7], ALU.mult, ALU.mult))
        stage(lambda e, b: e.tensor_mul(
            STAT[:, b + 8:b + 9], STAT[:, b + 7:b + 8], STAT[:, b + 3:b + 4]))
        stage(lambda e, b: e.tensor_scalar(
            STAT[:, b + 9:b + 10], STAT[:, b + 8:b + 9], -0.5, 1.5,
            ALU.mult, ALU.add))
        stage(lambda e, b: e.tensor_mul(
            STAT[:, b + 14:b + 15], STAT[:, b + 6:b + 7], STAT[:, b + 9:b + 10]))
        v = stage(lambda e, b: e.scalar_tensor_tensor(
            STAT[:, b + 15:b + 16], STAT[:, b + 2:b + 3], -1.0,
            STAT[:, b + 14:b + 15], ALU.mult, ALU.mult))
        return v
        return v

    def emit_h1(t, waits_x, vln):
        vu = None
        for tc in range(NTC):
            vu = ew("vector", lambda e, tc=tc: e.tensor_scalar(
                TMPA[:, tc * 1024:(tc + 1) * 1024], XT[:, tc * 1024:(tc + 1) * 1024],
                stat_a(tc), stat_b(tc), ALU.mult, ALU.add),
                waits=([(sV, vln)] + list(waits_x)) if tc == 0 else ())
        dump(f"u_dbg_{t}", TMPA[:], [(sV, vu)])
        dump(f"stat_dbg_{t}", STAT[:], [(sV, vu)])
        vh = None
        tlast = None
        for kd in range(NKD):
            bk = BK.acq()
            tiles = [(pb[bk][:, tc * 128:(tc + 1) * 128],
                      TMPA[:, tc * 1024 + kd * 128: tc * 1024 + (kd + 1) * 128])
                     for tc in range(NTC)]
            tlast = tr_group(tiles, IDF[:], waits=[(sV, vu)] if kd == 0 else ())
            ca1 = modcol(V_CA1, kd, t); sa_ = modcol(V_SA, kd, t)
            dsth = H1H[:, kd * 512:(kd + 1) * 512]
            dstl = H1L[:, kd * 512:(kd + 1) * 512]
            vh = ew("vector", lambda e, d=dsth, s=pb[bk][:], a=ca1, b=sa_:
                    e.tensor_scalar(d, s, a, b, ALU.mult, ALU.add),
                    waits=[(sT, tlast)])
            vh = ew("vector", lambda e, s=pb[bk][:], a=ca1, b=sa_:
                    e.tensor_scalar(T512[:], s, a, b, ALU.mult, ALU.add))
            vh = ew("vector", lambda e, d=dstl, h=dsth:
                    e.tensor_sub(d, T512[:], h))
            BK.rel(bk, sV, vh)
        return vh, sT.n

    pw_state = {"k": 0, "guards": {}}

    def pw_issue(w, mh):
        k = pw_state["k"]; pw_state["k"] += 1
        waits = []
        if k >= 2:
            waits.append((sT, pw_state["guards"][k - 2]))
        else:
            waits.append((sPWD, stPW))
        sl = PW[k % 2]
        load(sl[:, 0:1024], pwd[w, mh, 0, :, :], sPW[k % 2], waits=waits)
        ldv = load(sl[:, 1024:2048], pwd[w, mh, 1, :, :], sPW[k % 2])
        return k, sl, ldv

    def emit_proj(t, which, vh1, y_extra_waits=()):
        dst = {0: (ZH, ZL), 2: (QH, QL)}
        for w in which:
            for mh in range(NKD):
                k, sl, ldv = pw_issue(w, mh)
                bk = BK.acq()
                mms = []
                for kd in range(NKD):
                    lt_h = sl[:, kd * 128:(kd + 1) * 128]
                    lt_l = sl[:, 1024 + kd * 128: 1024 + (kd + 1) * 128]
                    rh_h = H1H[:, kd * 512:(kd + 1) * 512]
                    rh_l = H1L[:, kd * 512:(kd + 1) * 512]
                    mms += [(lt_h, rh_h), (lt_h, rh_l), (lt_l, rh_h)]
                tmm = mm_group(pb[bk][:], mms, waits=[(sPW[k % 2], ldv), (sV, vh1)])
                pw_state["guards"][k] = tmm
                if w == 1:
                    vv = copy_pass("vector", TMPA[:, mh * 512:(mh + 1) * 512],
                                   pb[bk][:],
                                   waits=([(sT, tmm)] + list(y_extra_waits))
                                   if mh == 0 else [(sT, tmm)])
                else:
                    dh, dl = dst[w]
                    oh = dh[:, mh * 512:(mh + 1) * 512]
                    ol = dl[:, mh * 512:(mh + 1) * 512]
                    ah = copy_pass("scalar", oh, pb[bk][:], waits=[(sT, tmm)])
                    vv = ew("vector", lambda e, d=ol, s=pb[bk][:], h=oh:
                            e.tensor_sub(d, s, h), waits=[(sA, ah)])
                BK.rel(bk, sV, vv)
        return sV.n, sA.n

    def emit_tm(srcs, dsts, nmh, waits0):
        rowlen = nmh * 128
        for hl in range(2):
            src, dstt = srcs[hl], dsts[hl]
            for tc in range(NTC):
                ngroups = max(1, nmh // 4)
                for g in range(ngroups):
                    gw = min(4, nmh)
                    bk = BKB.acq()
                    tiles = [(pbb[bk][:, j * 128:(j + 1) * 128],
                              src[:, (g * 4 + j) * 512 + tc * 128:
                                  (g * 4 + j) * 512 + (tc + 1) * 128])
                             for j in range(gw)]
                    tt = tr_group(tiles, IDB[:],
                                  waits=waits0 if (hl == 0 and tc == 0 and g == 0) else ())
                    dstv = dstt[:, tc * rowlen + g * 512: tc * rowlen + g * 512 + gw * 128]
                    eng = "scalar" if (tc + hl) % 2 == 0 else "vector"
                    vv = copy_pass(eng, dstv, pbb[bk][:, 0:gw * 128], waits=[(sT, tt)])
                    BKB.rel(bk, ESEM[eng], vv)
        return sV.n, sA.n

    ld_x0 = load_x(0, barrier=True)
    vln = emit_ln(XT[:], [(sX, ld_x0), (sV, vMODF)])
    vh1, _th1 = emit_h1(0, [(sX, ld_x0)], vln)
    dump("h1h_0", H1H[:], [(sV, vh1)])
    dump("h1l_0", H1L[:], [(sV, vh1)])
    emit_proj(0, [0, 1, 2], vh1)
    vproj, aproj = sV.n, sA.n
    dump("zh_0", ZH[:], [(sV, vproj), (sA, aproj)])
    dump("y_0", TMPA[:], [(sV, vproj), (sA, aproj)])
    dump("qh_0", QH[:], [(sV, vproj), (sA, aproj)])

    prev_vh2 = None
    prev_th1 = _th1
    h2s_slot_guard = {}
    for t in range(n_steps):
        for mh in range(NHC):
            bk = BK.acq()
            mms = []
            for kd in range(NKD):
                lt_h = W1TH[:, kd * 256 + mh * 128: kd * 256 + (mh + 1) * 128]
                lt_l = W1TL[:, kd * 256 + mh * 128: kd * 256 + (mh + 1) * 128]
                rh_h = ZH[:, kd * 512:(kd + 1) * 512]
                rh_l = ZL[:, kd * 512:(kd + 1) * 512]
                mms += [(lt_h, rh_h), (lt_h, rh_l), (lt_l, rh_h)]
            tA = mm_group(pb[bk][:], mms, waits=[(sV, vproj), (sA, aproj)])
            b1c = B1C[:, mh:mh + 1]
            aH = ew("scalar", lambda e, d=HFF[:, mh * 512:(mh + 1) * 512],
                    s=pb[bk][:], b=b1c: e.activation(d, s, AF.Gelu, bias=b),
                    waits=[(sT, tA)])
            vA = ew("vector", lambda e, d=AFB[:, mh * 512:(mh + 1) * 512],
                    s=pb[bk][:], b=b1c: e.tensor_scalar(d, s, b, None, ALU.add),
                    waits=[(sA, aH)])
            BK.rel(bk, sV, vA)
        aH_all, vA_all = sA.n, sV.n
        aHH, vHL = split_hi_lo(HFF[:], HH_[:], HL_[:], waits_hi=[(sA, aH_all)])
        dump(f"A_{t}", AFB[:], [(sV, vA_all)])
        dump(f"H_{t}", HFF[:], [(sA, aH_all)])
        emit_tm((HH_, HL_), (HTH, HTL), NHC, [(sA, aHH), (sV, vHL)])

        agp0 = ew("scalar", lambda e: e.activation(T1K[:], AFB[:], AF.Abs),
                  waits=[(sV, vA_all)])
        vgp = ew("vector", lambda e: e.tensor_scalar(
            T1K[:], T1K[:], TINY, None, ALU.is_lt), waits=[(sA, agp0)])
        vgp = ew("vector", lambda e: e.tensor_add(T2K[:], AFB[:], T1K[:]))
        vgp = ew("vector", lambda e: e.reciprocal(T2K[:], T2K[:]))
        vgp = ew("vector", lambda e: e.tensor_mul(T2K[:], T2K[:], HFF[:]),
                 waits=[(sA, aHH)])
        vgp = ew("vector", lambda e: e.tensor_scalar(
            HFF[:], T2K[:], -1.0, 0.5, ALU.mult, ALU.add))
        vgp = ew("vector", lambda e: e.tensor_mul(HFF[:], HFF[:], T1K[:]))
        vgp = ew("vector", lambda e: e.tensor_add(T2K[:], T2K[:], HFF[:]))
        agp = ew("scalar", lambda e: e.activation(T1K[:], AFB[:], AF.Square),
                 waits=[(sV, vgp)])
        agp = ew("scalar", lambda e: e.activation(T1K[:], T1K[:], AF.Exp, scale=-0.5))
        vgp = ew("vector", lambda e: e.scalar_tensor_tensor(
            T1K[:], AFB[:], INV_SQRT_2PI, T1K[:], ALU.mult, ALU.mult),
            waits=[(sA, agp)])
        vgp = ew("vector", lambda e: e.tensor_add(T2K[:], T2K[:], T1K[:]))
        dump(f"gp_{t}", T2K[:], [(sV, vgp)])

        vGr = aGr = None
        for md2 in range(NKD):
            bk = BK.acq()
            mms = []
            for kh in range(NHC):
                lt_h = W2TH[:, kh * 1024 + md2 * 128: kh * 1024 + (md2 + 1) * 128]
                lt_l = W2TL[:, kh * 1024 + md2 * 128: kh * 1024 + (md2 + 1) * 128]
                rh_h = HH_[:, kh * 512:(kh + 1) * 512]
                rh_l = HL_[:, kh * 512:(kh + 1) * 512]
                mms += [(lt_h, rh_h), (lt_h, rh_l), (lt_l, rh_h)]
            sl = slice(md2 * 512, (md2 + 1) * 512)
            mms += [(IDB[:], ZH[:, sl]), (IDB[:], ZL[:, sl])]
            tR = mm_group(pb[bk][:], mms,
                          waits=[(sA, aHH), (sV, vHL)] if md2 == 0 else ())
            b2c = B2C[:, md2:md2 + 1]
            vE = ew("vector", lambda e, s=pb[bk][:], b=b2c, y=TMPA[:, sl]:
                    e.scalar_tensor_tensor(T512[:], s, b, y, ALU.add, ALU.subtract),
                    waits=[(sT, tR)])
            aGr = ew("scalar", lambda e, d=H1H[:, sl],
                     acc=GBIA[:, 2 + md2:3 + md2]:
                     e.activation(d, T512[:], AF.Copy, scale=COEF, accum_out=acc),
                     waits=[(sV, vE)])
            vGr = ew("vector", lambda e, d=H1L[:, sl], h=H1H[:, sl]:
                     e.scalar_tensor_tensor(d, T512[:], COEF, h, ALU.mult,
                                            ALU.subtract),
                     waits=[(sA, aGr)])
            BK.rel(bk, sV, vGr)
        vGr_all, aGr_all = sV.n, sA.n
        dump(f"Gr_{t}", H1H[:], [(sV, vGr_all), (sA, aGr_all)])

        vga = None
        for mh in range(NHC):
            bk = BK.acq()
            mms = []
            for kd2 in range(NKD):
                lt_h = W2H[:, kd2 * 256 + mh * 128: kd2 * 256 + (mh + 1) * 128]
                lt_l = W2L[:, kd2 * 256 + mh * 128: kd2 * 256 + (mh + 1) * 128]
                rh_h = H1H[:, kd2 * 512:(kd2 + 1) * 512]
                rh_l = H1L[:, kd2 * 512:(kd2 + 1) * 512]
                mms += [(lt_h, rh_h), (lt_h, rh_l), (lt_l, rh_h)]
            tGh = mm_group(pb[bk][:], mms,
                           waits=[(sV, vGr_all), (sA, aGr_all)] if mh == 0 else ())
            sl = slice(mh * 512, (mh + 1) * 512)
            vga = ew("vector", lambda e, sl=sl, s=pb[bk][:],
                     acc=GBIA[:, mh:mh + 1]:
                     e.scalar_tensor_tensor(T1K[:, sl], s, 1.0, T2K[:, sl],
                                            ALU.mult, ALU.mult, accum_out=acc),
                     waits=[(sT, tGh), (sV, vgp)])
            BK.rel(bk, sV, vga)
        aga, vga = split_hi_lo(T1K[:], GAH[:], GAL[:], waits_hi=[(sV, vga)])
        dump(f"Ga_{t}", T1K[:], [(sV, vga)])
        emit_tm((GAH, GAL), (GATH, GATL), NHC, [(sA, aga), (sV, vga)])
        vtm_all, atm_all = sV.n, sA.n

        zts_guard = {}
        gst_guard = {}
        gsti = 0
        for md in range(NKD):
            zsl = ZTS[md % 2]
            zwaits = [(sT, zts_guard[md - 2])] if md >= 2 else ()
            evs = {}
            for hl, src in ((0, ZH), (1, ZL)):
                bk = BKB.acq()
                tiles = [(pbb[bk][:, tc * 128:(tc + 1) * 128],
                          src[:, md * 512 + tc * 128: md * 512 + (tc + 1) * 128])
                         for tc in range(NTC)]
                wt = list(zwaits) + ([(sV, vtm_all), (sA, atm_all)]
                                     if md == 0 and hl == 0 else [])
                tt = tr_group(tiles, IDB[:], waits=wt)
                eng = "scalar" if hl == 0 else "vector"
                vv = copy_pass(eng, zsl[:, hl * 512:(hl + 1) * 512],
                               pbb[bk][:, 0:512], waits=[(sT, tt)])
                evs[eng] = vv
                BKB.rel(bk, ESEM[eng], vv)
            bk = BK.acq()
            mms = []
            for tc in range(NTC):
                lt_h = zsl[:, tc * 128:(tc + 1) * 128]
                lt_l = zsl[:, 512 + tc * 128: 512 + (tc + 1) * 128]
                rh_h = GATH[:, tc * 256:(tc + 1) * 256]
                rh_l = GATL[:, tc * 256:(tc + 1) * 256]
                mms += [(lt_h, rh_h), (lt_h, rh_l), (lt_l, rh_h)]
            wmm = [(ESEM[k], v) for k, v in evs.items()]
            tg = mm_group(pb[bk][:, 0:256], mms, waits=wmm)
            zts_guard[md] = tg
            gw = [(sGS[gsti % 2], gst_guard[gsti - 2])] if gsti >= 2 else []
            vg = copy_pass("vector", GSTG[gsti % 2][:, 0:256], pb[bk][:, 0:256],
                           waits=[(sT, tg)] + gw)
            BK.rel(bk, sV, vg)
            gst_guard[gsti] = store(gin_d[:, md * 256:(md + 1) * 256],
                                    GSTG[gsti % 2][:, 0:256], sGS[gsti % 2],
                                    waits=[(sV, vg)])
            gsti += 1

        emit_tm((H1H, H1L), (ZH, ZL), NKD, [(sV, vGr_all), (sA, aGr_all)])
        vgrt_all, agrt_all = sV.n, sA.n
        for md2 in range(NKD):
            bk = BK.acq()
            mms = []
            for tc in range(NTC):
                lt_h = ZH[:, tc * 1024 + md2 * 128: tc * 1024 + (md2 + 1) * 128]
                lt_l = ZL[:, tc * 1024 + md2 * 128: tc * 1024 + (md2 + 1) * 128]
                rh_h = HTH[:, tc * 256:(tc + 1) * 256]
                rh_l = HTL[:, tc * 256:(tc + 1) * 256]
                mms += [(lt_h, rh_h), (lt_h, rh_l), (lt_l, rh_h)]
            tg = mm_group(pb[bk][:, 0:256], mms,
                          waits=[(sV, vgrt_all), (sA, agrt_all)] if md2 == 0 else ())
            gw = [(sGS[gsti % 2], gst_guard[gsti - 2])] if gsti >= 2 else []
            vg = copy_pass("vector", GSTG[gsti % 2][:, 0:256], pb[bk][:, 0:256],
                           waits=[(sT, tg)] + gw)
            BK.rel(bk, sV, vg)
            gst_guard[gsti] = store(gin_d[:, 2048 + md2 * 256: 2048 + (md2 + 1) * 256],
                                    GSTG[gsti % 2][:, 0:256], sGS[gsti % 2],
                                    waits=[(sV, vg)])
            gsti += 1
        store(gin_d[:, 4096:4106], GBIA[:, 0:10], sGS[0],
              waits=[(sV, vga), (sA, aGr_all)])

        if f"grads_{t}" in dbg:
            tdbg = nc.dram_tensor(f"dbg_grads_{t}", [128, 4106], F32,
                                  kind="ExternalOutput").ap()
            dbg_names.append(f"grads_{t}")
            P.op("gpsimd", lambda e, o=tdbg: e.dma_start(out=o, in_=gin_d[:, :]),
                 waits=[(sGS[0], sGS[0].n), (sGS[1], sGS[1].n)], inc=sST, inc_n=16)
        ccv = P.op("gpsimd", lambda e: e.collective_compute(
            "AllReduce", ALU.add, replica_groups=RG,
            ins=[gin_d.opt()], outs=[gout_d.opt()]),
            waits=[(sGS[0], sGS[0].n), (sGS[1], sGS[1].n)], inc=sCC)
        if f"gback_{t}" in dbg:
            tdbg = nc.dram_tensor(f"dbg_gback_{t}", [128, 4106], F32,
                                  kind="ExternalOutput").ap()
            dbg_names.append(f"gback_{t}")
            P.op("gpsimd", lambda e, o=tdbg: e.dma_start(out=o, in_=gout_d[:, :]),
                 waits=[(sCC, ccv)], inc=sST, inc_n=16)

        th1n = None
        if t + 1 < n_steps:
            ldx = load_x(t + 1, barrier=False,
                         extra=([(sV, prev_vh2)] if prev_vh2 else []) +
                         [(sX2, sX2.n)])
            vln2 = emit_ln(XT[:], [(sX, ldx), (sV, sV.n)])
            vh1n, th1n = emit_h1(t + 1, [(sX, ldx)], vln2)
            emit_proj(t + 1, [0], vh1n)

        go_guard = {}
        for ch in range(9):
            gw = [(sCC, ccv)]
            if ch >= 4:
                gw.append((sV, go_guard[ch - 4]))
            if ch < 8:
                ldg = load(GOUT[ch % 4][:], gout_d[:, ch * 512:(ch + 1) * 512],
                           sGO[ch % 4], waits=gw)
                if ch < 4:
                    dst = W1T_m[:, ch * 512:(ch + 1) * 512]
                else:
                    dst = W2_m[:, (ch - 4) * 512:(ch - 3) * 512]
                vv = ew("vector", lambda e, d=dst, g=GOUT[ch % 4][:]:
                        e.scalar_tensor_tensor(d, g, -LR, d, ALU.mult, ALU.add),
                        waits=[(sGO[ch % 4], ldg)])
                go_guard[ch] = vv
                csl = slice(ch * 512 % 2048, ch * 512 % 2048 + 512)
                if ch < 4:
                    split_hi_lo(W1T_m[:, csl], W1TH[:, csl], W1TL[:, csl],
                                waits_hi=[(sV, vv)])
            else:
                ldg = load(GBIA[:, 0:10], gout_d[:, 4096:4106], sGO[0], waits=gw)
                vv = ew("vector", lambda e: e.scalar_tensor_tensor(
                    B1C[:], GBIA[:, 0:2], -LR, B1C[:], ALU.mult, ALU.add),
                    waits=[(sGO[0], ldg)])
                vv = ew("vector", lambda e: e.scalar_tensor_tensor(
                    B2C[:], GBIA[:, 2:10], -LR, B2C[:], ALU.mult, ALU.add))
        vUPD = sV.n
        split_hi_lo(W2_m[:], W2H[:], W2L[:], waits_hi=[(sV, vUPD)])
        for g in range(4):
            bk = BK.acq()
            tiles = []
            for j in range(4):
                c2, hc = divmod(g * 4 + j, 2)
                tiles.append((pb[bk][:, j * 128:(j + 1) * 128],
                              W2_m[:, c2 * 256 + hc * 128:
                                   c2 * 256 + (hc + 1) * 128]))
            tt = tr_group(tiles, IDF[:], waits=[(sV, vUPD)] if g == 0 else ())
            vv2 = None
            for j in range(4):
                c2, hc = divmod(g * 4 + j, 2)
                sl = slice(hc * 1024 + c2 * 128, hc * 1024 + (c2 + 1) * 128)
                srcp = pb[bk][:, j * 128:(j + 1) * 128]
                ah = copy_pass("scalar", W2TH[:, sl], srcp,
                               waits=[(sT, tt)] if j == 0 else ())
                vv2 = ew("vector", lambda e, d=W2TL[:, sl], s=srcp, h=W2TH[:, sl]:
                         e.tensor_sub(d, s, h), waits=[(sA, ah)])
            BK.rel(bk, sV, vv2)
        aUP, vUP = sA.n, sV.n
        dump(f"W1m_{t}", W1T_m[:], [(sV, vUPD)])
        dump(f"W2m_{t}", W2_m[:], [(sV, vUPD)])

        ldxr = load_x(t, barrier=False,
                      extra=[(sT, th1n if th1n is not None else prev_th1),
                             (sX2, sX2.n)])
        if th1n is not None:
            prev_th1 = th1n

        for mh in range(NHC):
            bk = BK.acq()
            mms = []
            for kd in range(NKD):
                lt_h = W1TH[:, kd * 256 + mh * 128: kd * 256 + (mh + 1) * 128]
                lt_l = W1TL[:, kd * 256 + mh * 128: kd * 256 + (mh + 1) * 128]
                rh_h = QH[:, kd * 512:(kd + 1) * 512]
                rh_l = QL[:, kd * 512:(kd + 1) * 512]
                mms += [(lt_h, rh_h), (lt_h, rh_l), (lt_l, rh_h)]
            tA2 = mm_group(pb[bk][:], mms, waits=[(sA, aUP), (sV, vUP)])
            aH2 = ew("scalar", lambda e, d=HFF[:, mh * 512:(mh + 1) * 512],
                     s=pb[bk][:], b=B1C[:, mh:mh + 1]:
                     e.activation(d, s, AF.Gelu, bias=b), waits=[(sT, tA2)])
            BK.rel(bk, sA, aH2)
        aH2_all = sA.n
        aH2h, vH2l = split_hi_lo(HFF[:], HH_[:], HL_[:], waits_hi=[(sA, aH2_all)])
        for md2 in range(NKD):
            bk = BK.acq()
            mms = []
            for kh in range(NHC):
                lt_h = W2TH[:, kh * 1024 + md2 * 128: kh * 1024 + (md2 + 1) * 128]
                lt_l = W2TL[:, kh * 1024 + md2 * 128: kh * 1024 + (md2 + 1) * 128]
                rh_h = HH_[:, kh * 512:(kh + 1) * 512]
                rh_l = HL_[:, kh * 512:(kh + 1) * 512]
                mms += [(lt_h, rh_h), (lt_h, rh_l), (lt_l, rh_h)]
            sl = slice(md2 * 512, (md2 + 1) * 512)
            mms += [(IDB[:], QH[:, sl]), (IDB[:], QL[:, sl])]
            tO = mm_group(pb[bk][:], mms,
                          waits=[(sA, aH2h), (sV, vH2l)] if md2 == 0 else ())
            vO = ew("vector", lambda e, d=TMPA[:, sl], s=pb[bk][:],
                    b=B2C[:, md2:md2 + 1]:
                    e.tensor_scalar(d, s, b, None, ALU.add), waits=[(sT, tO)])
            BK.rel(bk, sV, vO)
        vATT = sV.n
        dump(f"attn_{t}", TMPA[:], [(sV, vATT)])

        ldga = load(BCA[:], rowsd[0, t, :].rearrange("(o d) -> o d", o=1)
                    .to_broadcast([128, D]), sBC,
                    waits=[(sRW, stROWS)] + ([(sV, prev_vh2)] if prev_vh2 else []))
        vx2 = None
        for tc in range(NTC):
            for g in range(2):
                bk = BK.acq()
                tiles = [(pb[bk][:, j * 128:(j + 1) * 128],
                          TMPA[:, (g * 4 + j) * 512 + tc * 128:
                               (g * 4 + j) * 512 + (tc + 1) * 128])
                         for j in range(4)]
                tt = tr_group(tiles, IDF[:],
                              waits=[(sV, vATT)] if (tc == 0 and g == 0) else ())
                dsl = slice(tc * 1024 + g * 512, tc * 1024 + (g + 1) * 512)
                vx2 = ew("vector", lambda e, s=pb[bk][:], b=BCA[:, g * 512:(g + 1) * 512]:
                         e.tensor_mul(T512[:], s, b),
                         waits=[(sT, tt), (sBC, ldga), (sX, ldxr)])
                vx2 = ew("vector", lambda e, d=XT[:, dsl]:
                         e.tensor_add(d, d, T512[:]))
                BK.rel(bk, sV, vx2)
        vX2 = sV.n
        store(x2d[t * S:(t + 1) * S, :].rearrange("(tc p) d -> p tc d", p=128),
              XT[:].rearrange("p (tc d) -> p tc d", tc=NTC), sX2,
              waits=[(sV, vX2)])
        dump(f"x2_{t}", XT[:], [(sV, vX2)])

        vln3 = emit_ln(XT[:], [(sV, vX2)])
        vu2 = None
        for tc in range(NTC):
            vu2 = ew("vector", lambda e, tc=tc: e.tensor_scalar(
                TMPA[:, tc * 1024:(tc + 1) * 1024], XT[:, tc * 1024:(tc + 1) * 1024],
                stat_a(tc), stat_b(tc), ALU.mult, ALU.add),
                waits=[(sV, vln3)] if tc == 0 else ())
        vh2 = None
        for kd in range(NKD):
            bk = BK.acq()
            tiles = [(pb[bk][:, tc * 128:(tc + 1) * 128],
                      TMPA[:, tc * 1024 + kd * 128: tc * 1024 + (kd + 1) * 128])
                     for tc in range(NTC)]
            tt = tr_group(tiles, IDF[:], waits=[(sV, vu2)] if kd == 0 else ())
            cb1 = modcol(V_CB1, kd, t); sb_ = modcol(V_SB, kd, t)
            hw = ([(sH2S[kd % 2], h2s_slot_guard[kd % 2])]
                  if kd % 2 in h2s_slot_guard else [])
            vh2 = ew("vector", lambda e, s=pb[bk][:], a=cb1, b=sb_,
                     d=H2S[kd % 2][:]:
                     e.tensor_scalar(d, s, a, b, ALU.mult, ALU.add),
                     waits=[(sT, tt)] + hw)
            vh2 = ew("vector", lambda e, d=H2S[kd % 2][:], c=t * 8 + kd:
                     e.reduce_max(MAXH[:, c:c + 1], d, axis=AX,
                                  apply_absolute_value=True))
            BK.rel(bk, sV, vh2)
            h2s_slot_guard[kd % 2] = store(
                h2d[kd, :, t * S:(t + 1) * S], H2S[kd % 2][:],
                sH2S[kd % 2], waits=[(sV, vh2)])
            dump(f"h2_{t}_{kd}", H2S[kd % 2][:], [(sV, vh2)])
        vH2 = sV.n
        prev_vh2 = vH2

        if t + 1 < n_steps:
            emit_proj(t + 1, [1], vh1n)
            emit_proj(t + 1, [2], vh1n)
            vproj, aproj = sV.n, sA.n


    es_ttt.close()

    es_ffn = ExitStack()

    def sbf(name, cols, dt=F32):
        return es_ffn.enter_context(nc.sbuf_tensor(name, [128, cols], dt))

    if do_ffn:
        sMXS = Sem(es, nc, "sMXS"); sMXL = Sem(es, nc, "sMXL")
        sFW = [Sem(es, nc, f"sFW{i}") for i in range(4)]
        sFS = [Sem(es, nc, "sFS0"), Sem(es, nc, "sFS1")]
        sIG = Sem(es, nc, "sIG")
        sH2L = [Sem(es, nc, "sH2L0"), Sem(es, nc, "sH2L1")]
        sPS = Sem(es, nc, "sPS"); sPL = Sem(es, nc, "sPL")
        sX2L = [Sem(es, nc, "sX2L0"), Sem(es, nc, "sX2L1")]
        sOUT = [Sem(es, nc, "sOUT0"), Sem(es, nc, "sOUT1")]
        ALL_DMA_SEMS.extend([sMXS, sMXL, sFW[2], sFW[3],
                             sFW[0], sFW[1], sFS[0], sFS[1], sIG,
                             sH2L[0], sH2L[1], sPS, sPL, sX2L[0], sX2L[1],
                             sOUT[0], sOUT[1]])

        FMISC = sbf("FMISC", 256)
        SXC = sbf("SXC", 8)
        SGC = sbf("SGC", 24); SUC = sbf("SUC", 24); SDC = sbf("SDC", 8)
        SXG = sbf("SXG", 24); SXU = sbf("SXU", 24)
        SDB = sbf("SDB", 1024)
        QV = [sbf(f"QV{i}", 512) for i in range(2)]
        QF = [sbf(f"QF{i}", 512) for i in range(2)]
        QJ = [sbf(f"QJ{i}", 512, I8) for i in range(2)]
        QC = [sbf(f"QC{i}", 512, BF16) for i in range(2)]
        SILG = sbf("SILG", 512)

        ttt_join = [(s_, s_.n) for s_ in
                    (sV, sA, sT, sG, sX2, sH2S[0], sH2S[1], sGS[0], sGS[1],
                     sX, sBC, sGO[0], sGO[1], sPW[0], sPW[1], sWM[0], sWM[1],
                     sWP, sINIT, sRW, sPWD) if True]

        def global_max(cols, mxi, mxo, fc, xw=()):
            v1 = ew("vector", lambda e: e.reduce_max(
                MAXH[:, 250:251], MAXH[:, cols[0]:cols[1]], axis=AX),
                waits=[(sV, sV.n)])
            bk = BK.acq()
            t1 = tr_group([(pb[bk][0:1, 0:128], MAXH[:, 250:251])], IDF[:],
                          waits=[(sV, v1)])
            a1 = copy_pass("scalar", FMISC[0:1, fc:fc + 128], pb[bk][0:1, 0:128],
                           waits=[(sT, t1)] + list(xw))
            BK.rel(bk, sA, a1)
            v2 = ew("vector", lambda e: e.reduce_max(
                FMISC[0:1, fc + 128:fc + 129], FMISC[0:1, fc:fc + 128], axis=AX),
                waits=[(sA, a1)])
            g1 = ew("gpsimd", lambda e: e.tensor_scalar(
                FMISC[0:1, fc + 136:fc + 144], ONESROW[0:1, 0:8],
                FMISC[0:1, fc + 128:fc + 129], None, ALU.mult),
                waits=[(sV, v2)])
            st1 = store(mxi, FMISC[0:1, fc + 136:fc + 144], sMXS, waits=[(sG, g1)])
            cc1 = P.op("gpsimd", lambda e: e.collective_compute(
                "AllReduce", ALU.max, replica_groups=RG,
                ins=[mxi.opt()], outs=[mxo.opt()]),
                waits=[(sMXS, st1)], inc=sCC)
            ld1 = load(FMISC[0:1, fc + 144:fc + 152], mxo, sMXL, waits=[(sCC, cc1)])
            return FMISC[0:1, fc + 144:fc + 145], ld1

        def scale_pair(gmax_ap, ld1, clip_col, sxc0, fc):
            a1 = ew("scalar", lambda e: e.activation(
                FMISC[0:1, fc:fc + 1], CLIPS[0:1, clip_col:clip_col + 1],
                AF.Sigmoid), waits=[(sINIT, vINIT)])
            v1 = ew("vector", lambda e: e.tensor_mul(
                FMISC[0:1, fc + 1:fc + 2], gmax_ap, FMISC[0:1, fc:fc + 1]),
                waits=[(sA, a1), (sMXL, ld1)])
            a2 = ew("scalar", lambda e: e.activation(
                FMISC[0:1, fc + 2:fc + 3], FMISC[0:1, fc + 1:fc + 2],
                AF.Copy, scale=1.0 / 127.0), waits=[(sV, v1)])
            v2 = ew("vector", lambda e: e.reciprocal(
                FMISC[0:1, fc + 3:fc + 4], FMISC[0:1, fc + 2:fc + 3]),
                waits=[(sA, a2)])
            v3 = ew("vector", lambda e: e.tensor_scalar(
                FMISC[0:1, fc + 4:fc + 5], FMISC[0:1, fc + 3:fc + 4],
                -1.0, None, ALU.mult))
            bk = BK.acq()
            tb = mm_group(pb[bk][:, 0:3],
                          [(ONESROW[0:1, 0:128], FMISC[0:1, fc + 2:fc + 5])],
                          waits=[(sV, v3)])
            ab = copy_pass("scalar", SXC[:, sxc0:sxc0 + 3], pb[bk][:, 0:3],
                           waits=[(sT, tb)])
            BK.rel(bk, sA, ab)
            return ab

        gm1, ldm1 = global_max((0, 64), mx1i, mx1o, 0)
        ab1 = scale_pair(gm1, ldm1, 0, 0, 160)

        es_fp = ExitStack()

        def sbq(name, cols, dt=F32):
            return es_fp.enter_context(nc.sbuf_tensor(name, [128, cols], dt))

        WQN = [sbq(f"WQN{i}", 2816) for i in range(4)]
        WQV = [sbq(f"WQV{i}", 2816) for i in range(2)]
        WQF = sbq("WQF", 2816); WQD = sbq("WQD", 2816)
        WQJ = sbq("WQJ", 2816, I8); WQCB = sbq("WQCB", 2816, BF16)
        WQB = sbq("WQB", 2816, BF16)
        WQS = [sbq(f"WQS{i}", 2816, BF16) for i in range(2)]
        RMX = sbq("RMX", 64); RQ1 = sbq("RQ1", 64); RQC = sbq("RQC", 64)

        fp_chunks = ([("g", c, wg_in, D, SGC, igtd) for c in range(NIC)] +
                     [("u", c, wu_in, D, SUC, iutd) for c in range(NIC)] +
                     [("d", c, wd_in, II, SDC, idtd) for c in range(NKD)])
        NCH = len(fp_chunks)
        wq_guard = {}
        ws_guard = {}
        prev_vc1 = None
        prev_tt = None
        prev_vd = None
        av_fp = {}
        for cx in range(NCH + 2):
            if 0 <= cx - 1 < NCH:
                j = cx - 1
                kindj, cj, w_inj, cwj, _, _ = fp_chunks[j]
                vrq = ew("vector", lambda e, j=j: e.reciprocal(
                    RQ1[:, j:j + 1], RMX[:, j:j + 1]))
                ew("scalar", lambda e, j=j: e.activation(
                    RQC[:, j:j + 1], RQ1[:, j:j + 1], AF.Copy, scale=127.0),
                    waits=[(sV, vrq)])
                av_fp[j] = ew(
                    "scalar", lambda e, s=WQN[j % 4][:, 0:cwj], cwj=cwj, j=j:
                    e.activation(WQV[j % 2][:, 0:cwj], s, AF.Copy,
                                 scale=RQC[:, j:j + 1]),
                    waits=([(sV, prev_vd)] if prev_vd else []))
            ci = cx - 2
            if ci >= 0:
                kind, c, w_in, cw, scol, dstd = fp_chunks[ci]
                p2 = ci % 2
                vj = ew("vector", lambda e, cw=cw, p2=p2: e.tensor_copy(
                    WQJ[:, 0:cw], WQV[p2][:, 0:cw]),
                    waits=[(sA, av_fp[ci])])
                vd = ew("vector", lambda e, cw=cw, p2=p2: e.tensor_sub(
                    WQD[:, 0:cw], WQV[p2][:, 0:cw], WQJ[:, 0:cw]))
                prev_vd = vd
                vc1 = ew("vector", lambda e, cw=cw: e.scalar_tensor_tensor(
                    WQF[:, 0:cw], WQD[:, 0:cw], 0.5, WQJ[:, 0:cw],
                    ALU.is_ge, ALU.add))
                vc2 = ew("vector", lambda e, cw=cw: e.tensor_scalar(
                    WQCB[:, 0:cw], WQD[:, 0:cw], -0.5, -1.0,
                    ALU.is_le, ALU.mult))
                vq = ew("vector", lambda e, cw=cw: e.tensor_add(
                    WQB[:, 0:cw], WQF[:, 0:cw], WQCB[:, 0:cw]),
                    waits=([(sT, prev_tt)] if prev_tt else []))
                wq_guard[ci] = vq
                ntile = cw // 128
                sw = [(sFS[ci % 2], ws_guard[ci - 2])] if ci >= 2 else []
                for g0 in range(0, ntile, 4):
                    gw_ = min(4, ntile - g0)
                    bkb = BKB.acq()
                    tiles = [(pbb[bkb][:, j * 128:(j + 1) * 128],
                              WQB[:, (g0 + j) * 128:(g0 + j + 1) * 128])
                             for j in range(gw_)]
                    tt = tr_group(tiles, IDB[:],
                                  waits=([(sV, vq)] + sw) if g0 == 0 else ())
                    evv = copy_pass("scalar",
                                    WQS[ci % 2][:, g0 * 128:(g0 + gw_) * 128],
                                    pbb[bkb][:, 0:gw_ * 128], waits=[(sT, tt)])
                    BKB.rel(bkb, sA, evv)
                    prev_tt = tt
                store(dstd[0:ntile, :, c * 128:(c + 1) * 128]
                      .rearrange("k p f -> p k f"),
                      WQS[ci % 2][:, 0:ntile * 128]
                      .rearrange("p (k f) -> p k f", k=ntile), sFS[ci % 2],
                      waits=[(sV, sV.n), (sA, sA.n)])
                ws_guard[ci] = sFS[ci % 2].n
            if cx < NCH:
                kind, c, w_in, cw, scol, dstd = fp_chunks[cx]
                sl4 = cx % 4
                lw = (list(ttt_join) if cx < 4 else
                      [(sV, wq_guard[cx - 4])])
                load(WQN[sl4][0:64, 0:cw], w_in[c * 128:c * 128 + 64, :],
                     sFW[sl4], waits=lw)
                ldn = load(WQN[sl4][64:128, 0:cw],
                           w_in[c * 128 + 64:(c + 1) * 128, :], sFW[sl4])
                vr = ew("vector", lambda e, s=WQN[sl4][:, 0:cw], cx=cx:
                        e.reduce_max(RMX[:, cx:cx + 1], s, axis=AX,
                                     apply_absolute_value=True),
                        waits=[(sFW[sl4], ldn)])
                wq_guard.setdefault(cx, vr)
                ew("scalar", lambda e, d=scol[:, c:c + 1], cx=cx:
                   e.activation(d, RMX[:, cx:cx + 1], AF.Copy,
                                scale=1.0 / 127.0), waits=[(sV, vr)])
        bk = BK.acq()
        tsd = tr_group([(pb[bk][0:8, 0:128], SDC[:, 0:8])], IDF[:],
                       waits=[(sA, sA.n), (sV, sV.n)])
        asd = copy_pass("scalar", FMISC[0:8, 0:128], pb[bk][0:8, 0:128],
                        waits=[(sT, tsd)])
        BK.rel(bk, sA, asd)
        stsd = store(sdrowd.rearrange("o (a f) -> (o a) f", a=8), FMISC[0:8, 0:128],
                     sMXS, waits=[(sA, asd)])

        qst = {"i": 0, "last_vq": [None, None]}

        def quant_slab(src_ap, dst_ap, nrcol, waits_src=()):
            i = qst["i"] % 2; qst["i"] += 1
            gw0 = [(sV, qst["last_vq"][i])] if qst["last_vq"][i] else []
            av = ew("scalar", lambda e: e.activation(QF[i][:], src_ap, AF.Relu,
                                                     bias=CB49[:, 0:1],
                                                     scale=nrcol),
                    waits=list(waits_src) + gw0)
            av2 = ew("scalar", lambda e: e.activation(QV[i][:], QF[i][:],
                                                      AF.Relu,
                                                      bias=CB49[:, 1:2],
                                                      scale=-1.0))
            vj = ew("vector", lambda e: e.tensor_scalar(
                QJ[i][:], QV[i][:], -128.49, None, ALU.add),
                waits=[(sA, av2)])
            vd = ew("vector", lambda e: e.scalar_tensor_tensor(
                QV[i][:], QV[i][:], -128.49, QJ[i][:], ALU.add, ALU.subtract))
            vc1 = ew("vector", lambda e: e.scalar_tensor_tensor(
                QF[i][:], QV[i][:], 0.5, QJ[i][:], ALU.is_ge, ALU.add))
            vc2 = ew("vector", lambda e: e.tensor_scalar(
                QC[i][:], QV[i][:], -0.5, -1.0, ALU.is_le, ALU.mult))
            vq = ew("vector", lambda e: e.tensor_add(dst_ap, QF[i][:], QC[i][:]))
            qst["last_vq"][i] = vq
            return vq

        es_fp.close()
        PTILE = sbf("PTILE", NIC * 512)
        es_f1 = ExitStack()
        IGT = es_f1.enter_context(nc.sbuf_tensor("IGT", [128, NKD * II], BF16))
        IUT = es_f1.enter_context(nc.sbuf_tensor("IUT", [128, NKD * II], BF16))
        H2T = [es_f1.enter_context(nc.sbuf_tensor(f"H2T{i}", [128, 4096], F32))
               for i in range(2)]
        IQF = [es_f1.enter_context(nc.sbuf_tensor(f"IQF{i}", [128, 4096], BF16))
               for i in range(2)]
        for kd in range(NKD):
            load(IGT[:, kd * II:(kd + 1) * II], igtd[kd, :, :], sIG,
                 waits=[(sFS[0], sFS[0].n), (sFS[1], sFS[1].n)] if kd == 0 else ())
            load(IUT[:, kd * II:(kd + 1) * II], iutd[kd, :, :], sIG)
        vIG = sIG.n
        vsxg = ew("vector", lambda e: e.tensor_scalar(
            SXG[:, 0:NIC], SGC[:, 0:NIC], SXC[:, 0:1], None, ALU.mult),
            waits=[(sA, ab1), (sV, sV.n)])
        vsxu = ew("vector", lambda e: e.tensor_scalar(
            SXU[:, 0:NIC], SUC[:, 0:NIC], SXC[:, 0:1], None, ALU.mult))

        h2l_guard = {}
        iqf_guard = {}
        ps_tile_done = {}
        for tau in range(C):
            sl_h = tau % 2
            hw_ = ([(sV, h2l_guard[tau - 2])] if tau >= 2 else list(ttt_join))
            ldh = None
            for kd in range(NKD):
                ldh = load(H2T[sl_h][:, kd * 512:(kd + 1) * 512],
                           h2d[kd, :, tau * S:(tau + 1) * S], sH2L[sl_h],
                           waits=hw_ if kd == 0 else ())
            vq = None
            for kd in range(NKD):
                wsrc = [(sH2L[sl_h], ldh)] if kd == 0 else ()
                wsrc = list(wsrc)
                if tau >= 2 and kd == 0:
                    wsrc.append((sT, iqf_guard[tau - 2]))
                vq = quant_slab(H2T[sl_h][:, kd * 512:(kd + 1) * 512],
                                IQF[tau % 2][:, kd * 512:(kd + 1) * 512],
                                SXC[:, 2:3], waits_src=wsrc)
            h2l_guard[tau] = vq
            pw_ = [(sPS, ps_tile_done[tau - 1])] if tau >= 1 else []
            stp = None
            prev_vp = None
            for m in range(NIC):
                bk = BK.acq()
                mms = [(IGT[:, kd * II + m * 128: kd * II + (m + 1) * 128],
                        IQF[tau % 2][:, kd * 512:(kd + 1) * 512])
                       for kd in range(NKD)]
                tg = mm_group(pb[bk][:], mms,
                              waits=[(sV, vq), (sIG, vIG)] if m == 0 else ())
                asil = ew("scalar", lambda e, s=pb[bk][:], m=m: e.activation(
                    SILG[:], s, AF.Silu, scale=SXG[:, m:m + 1]),
                    waits=([(sT, tg), (sV, vsxg)] if m == 0 else [(sT, tg)]) +
                    ([(sV, prev_vp)] if prev_vp else []))
                BK.rel(bk, sA, asil)
                bk2 = BK.acq()
                mms = [(IUT[:, kd * II + m * 128: kd * II + (m + 1) * 128],
                        IQF[tau % 2][:, kd * 512:(kd + 1) * 512])
                       for kd in range(NKD)]
                tu = mm_group(pb[bk2][:], mms)
                vp = ew("vector", lambda e, s=pb[bk2][:], m=m,
                        d=PTILE[:, m * 512:(m + 1) * 512]:
                        e.scalar_tensor_tensor(d, s, SXU[:, m:m + 1], SILG[:],
                                               ALU.mult, ALU.mult),
                        waits=([(sT, tu), (sA, asil), (sV, vsxu)] + pw_)
                        if m == 0 else [(sT, tu), (sA, asil)])
                BK.rel(bk2, sV, vp)
                prev_vp = vp
                vmx = ew("vector", lambda e, d=PTILE[:, m * 512:(m + 1) * 512],
                         c=64 + tau * NIC + m:
                         e.reduce_max(MAXH[:, c:c + 1], d, axis=AX,
                                      apply_absolute_value=True))
                stp = store(Pd[m, :, tau * S:(tau + 1) * S],
                            PTILE[:, m * 512:(m + 1) * 512], sPS,
                            waits=[(sV, vmx)])
            iqf_guard[tau] = sT.n
            ps_tile_done[tau] = sPS.n

        gm2, ldm2 = global_max((64, 240), mx2i, mx2o, 32, xw=[(sMXS, stsd)])
        ab2 = scale_pair(gm2, ldm2, 1, 3, 192)

        es_f1.close()
        IDT = sbf("IDT", NIC * 1024, BF16)
        HQ2 = [sbf("HQFA", NIC * 512, BF16), sbf("HQFB", NIC * 512, BF16)]
        X2L = [sbf("X2TA", 4096), sbf("X2TB", 4096)]
        CBL = [sbf("CBTA", 1024), sbf("CBTB", 1024)]
        BCL = [BCA, sbf("BCF1", 1024)]
        OUT5 = [sbf(f"OUT5{i}", 512) for i in range(2)]
        T512F = sbf("T512F", 512)
        for ki in range(NIC):
            load(IDT[:, ki * 1024:(ki + 1) * 1024], idtd[ki, :, :], sIG,
                 waits=[(sT, sT.n)] if ki == 0 else ())
        ldsdb = load(SDB[:], sdrowd.rearrange("o d -> o d", o=1)
                     .to_broadcast([128, D]), sIG, waits=[(sMXS, stsd)])
        vIDT = sIG.n
        f2bar = [(sT, sT.n), (sV, sV.n), (sA, sA.n), (sIG, sIG.n),
                 (sH2L[0], sH2L[0].n), (sH2L[1], sH2L[1].n)]

        groupsF2 = [(tc, dg) for tc in range(NTC) for dg in range(2)]
        x2_guard = {}
        out_done = {}
        out5_guard = {}
        gstate = {}
        ldx2s = {}
        ldp_tot = {}
        vqs = {}
        EV_AT = (2, 5, 8, 11, 14, 17, 20)

        def f2_group(tau_, j, waits=()):
            tc, dg = groupsF2[j]
            bk = BK.acq()
            mms = [(HQ2[tau_ % 2][:, ki * 512 + tc * 128:
                                  ki * 512 + (tc + 1) * 128],
                    IDT[:, ki * 1024 + dg * 512:ki * 1024 + (dg + 1) * 512])
                   for ki in range(NIC)]
            tdn = mm_group(pb[bk][:], mms, waits=waits)
            return (bk, tdn)

        def f2_evac(tau_, j, bk, tdn):
            tc, dg = groupsF2[j]
            w1 = [(sT, tdn)] + ([(sX2L[tau_ % 2], ldx2s[tau_])]
                                if j == 0 else [])
            ew("vector", lambda e, s=pb[bk][:], dg=dg, p=tau_ % 2:
               e.tensor_mul(T512F[:], s, CBL[p][:, dg * 512:(dg + 1) * 512]),
               waits=w1)
            BK.rel(bk, sV, sV.n)
            sl = j % 2
            vo2 = ew("vector", lambda e, tc=tc, dg=dg, sl=sl, p=tau_ % 2:
                     e.tensor_add(OUT5[sl][:],
                                  X2L[p][:, tc * 1024 + dg * 512:
                                         tc * 1024 + (dg + 1) * 512],
                                  T512F[:]),
                     waits=([(sOUT[sl], out5_guard[sl])]
                            if sl in out5_guard else []))
            out5_guard[sl] = store(
                out_hbm[tau_][tc * 128:(tc + 1) * 128, dg * 512:(dg + 1) * 512],
                OUT5[sl][:], sOUT[sl], waits=[(sV, vo2)])

        def f2_consume(tau_, j):
            bk, tdn = gstate[tau_][j]
            f2_evac(tau_, j, bk, tdn)
            if j < 2:
                gstate[tau_].append(f2_group(tau_, 6 + j))

        ldp = None
        for m in range(NIC):
            ldp = load(PTILE[:, m * 512:(m + 1) * 512], Pd[m, :, 0:S], sPL,
                       waits=[(sPS, sPS.n), (sV, sV.n)] if m == 0 else ())
        ldp_tot[0] = ldp

        for tau in range(C):
            par = tau % 2
            xw = ([(sV, x2_guard[tau - 2])] if tau >= 2 else list(f2bar))
            ldx2s[tau] = load(
                X2L[par][:].rearrange("p (tc d) -> p tc d", tc=NTC),
                x2d[tau * S:(tau + 1) * S, :]
                .rearrange("(tc p) d -> p tc d", p=128), sX2L[par],
                waits=(xw + [(sX2, sX2.n)]) if tau == 0 else xw)
            ldgb = load(BCL[par][:], rowsd[3, tau, :]
                        .rearrange("(o d) -> o d", o=1)
                        .to_broadcast([128, D]), sBC, waits=list(xw))
            ew("vector", lambda e, par=par: e.scalar_tensor_tensor(
                CBL[par][:], BCL[par][:], SXC[:, 3:4], SDB[:],
                ALU.mult, ALU.mult),
                waits=[(sBC, ldgb), (sIG, vIDT), (sA, ab2)])
            vql = []
            for m in range(NIC):
                wsrc = []
                if m == 0:
                    wsrc = [(sPL, ldp_tot[tau])] + (
                        [(sT, out_done[tau - 2])] if tau >= 2 else list(f2bar))
                vq = quant_slab(PTILE[:, m * 512:(m + 1) * 512],
                                HQ2[par][:, m * 512:(m + 1) * 512],
                                SXC[:, 5:6], waits_src=wsrc)
                vql.append(vq)
                if tau >= 1 and m in EV_AT:
                    f2_consume(tau - 1, EV_AT.index(m))
            if tau >= 1:
                f2_consume(tau - 1, 7)
                out_done[tau - 1] = sT.n
                x2_guard[tau - 1] = sV.n
            vqs[tau] = vql
            if tau + 1 < C:
                ldp = None
                for m in range(NIC):
                    ldp = load(PTILE[:, m * 512:(m + 1) * 512],
                               Pd[m, :, (tau + 1) * S:(tau + 2) * S], sPL,
                               waits=[(sV, vql[m])])
                ldp_tot[tau + 1] = ldp
            gstate[tau] = [f2_group(tau, j,
                                    waits=[(sV, vql[-1])] if j == 0 else ())
                           for j in range(6)]
        for j in range(8):
            f2_consume(C - 1, j)
    es_ffn.close()

    with nc.Block() as blk:
        @blk.sync
        def _(e):
            for f in P.q["sync"]:
                f(e)

        @blk.vector
        def _(e):
            for f in P.q["vector"]:
                f(e)

        @blk.scalar
        def _(e):
            for f in P.q["scalar"]:
                f(e)

        @blk.tensor
        def _(e):
            for f in P.q["tensor"]:
                f(e)

        @blk.gpsimd
        def _(e):
            for f in P.q["gpsimd"]:
                f(e)
            for s_ in ALL_DMA_SEMS:
                if s_.n:
                    e.wait_ge(s_.h, s_.n)

    es.close()
    return nc, dbg_names


def make_inputs_per_core(inputs, core):
    x = np.ascontiguousarray(inputs["L0_attn_in_seq"][:, core * S:(core + 1) * S, :])
    eye = np.eye(128, dtype=np.float32)
    return {
        "x": x,
        "tk": np.asarray(inputs["L0_attn_t_k"]),
        "tv": np.asarray(inputs["L0_attn_t_v"]),
        "tq": np.asarray(inputs["L0_attn_t_q"]),
        "W1": np.asarray(inputs["L0_attn_W1"]), "b1": np.asarray(inputs["L0_attn_b1"]),
        "W2": np.asarray(inputs["L0_attn_W2"]), "b2": np.asarray(inputs["L0_attn_b2"]),
        "clip_x": np.asarray(inputs["L0_ffn_clip_x"]),
        "clip_down": np.asarray(inputs["L0_ffn_clip_down"]),
        "wg": np.asarray(inputs["L0_ffn_w_gate"]),
        "wu": np.asarray(inputs["L0_ffn_w_up"]),
        "wd": np.asarray(inputs["L0_ffn_w_down"]),
        "cvec": np.asarray(inputs["L0_c"]), "wmod": np.asarray(inputs["L0_wmod"]),
        "identf": eye,
        "identb": eye.astype(ml_dtypes.bfloat16),
    }


def kernel(**inputs):
    from concourse.bass_utils import run_bass_kernel_spmd
    nc, _ = build_nc(n_steps=8, do_ffn=True, dbg=())
    in_maps = [make_inputs_per_core(inputs, c) for c in range(8)]
    res = run_bass_kernel_spmd(nc, in_maps, core_ids=list(range(8)))
    out = np.empty((C, 4096, D), dtype=np.float32)
    for c in range(8):
        out[:, c * S:(c + 1) * S, :] = np.asarray(res.results[c]["out"])
    return out


# version 18
# speedup vs baseline: 1.0035x; 1.0035x over previous
import numpy as np
import ml_dtypes
import concourse.bass as bass
import concourse.mybir as mybir
from contextlib import ExitStack

F32 = mybir.dt.float32
BF16 = mybir.dt.bfloat16
I8 = mybir.dt.int8
AF = mybir.ActivationFunctionType
ALU = mybir.AluOpType
AX = mybir.AxisListType.X

S, C, D, HH, II = 512, 8, 1024, 256, 2816
R = C * S
NKD, NTC, NIC, NHC = 8, 4, 22, 2
LR = 1e-3
COEF = 2.0 / (4096.0 * 1024.0)
EPS = 1e-5
INV_SQRT_2PI = 0.3989422804014327
TINY = 2.0 ** -40
ENGINES = ("sync", "vector", "scalar", "tensor", "gpsimd")
V_SA, V_CA1, V_GA, V_SB, V_CB1, V_GB = range(6)


class Sem:
    def __init__(self, es, nc, name):
        self.h = es.enter_context(nc.semaphore(name))
        self.n = 0


class Prog:
    def __init__(self):
        self.q = {e: [] for e in ENGINES}

    def op(self, eng, fn, waits=(), inc=None, inc_n=1):
        if inc is not None:
            inc.n += inc_n
        def run(e, fn=fn, waits=tuple(waits), inc=inc, inc_n=inc_n):
            for s, v in waits:
                e.wait_ge(s.h, v)
            ins = fn(e)
            if inc is not None:
                ins.then_inc(inc.h, inc_n)
        self.q[eng].append(run)
        return inc.n if inc is not None else None

    def wait(self, eng, sem, value=None):
        v = sem.n if value is None else value
        self.q[eng].append(lambda e, s=sem, v=v: e.wait_ge(s.h, v))


def build_nc(n_steps=8, do_ffn=True, dbg=()):
    dbg = set(dbg)
    nc = bass.Bass("TRN2", target_bir_lowering=False)

    def din(name, shape, dt=F32):
        return nc.dram_tensor(name, shape, dt, kind="ExternalInput").ap()

    x_in = din("x", [C, S, D])
    tw_in = [din(n, [D, D]) for n in ("tk", "tv", "tq")]
    w1_in = din("W1", [HH, D]); b1_in = din("b1", [HH])
    w2_in = din("W2", [D, HH]); b2_in = din("b2", [D])
    clipx_in = din("clip_x", [1]); clipd_in = din("clip_down", [1])
    wg_in = din("wg", [II, D]); wu_in = din("wu", [II, D]); wd_in = din("wd", [D, II])
    c_in = din("cvec", [C, D]); wmod_in = din("wmod", [D, 6 * D])
    idf_in = din("identf", [128, 128]); idb_in = din("identb", [128, 128], BF16)
    out_hbm = nc.dram_tensor("out", [C, S, D], F32, kind="ExternalOutput").ap()

    def dint(name, shape, dt=F32):
        return nc.dram_tensor(name, shape, dt).ap()

    x2d = dint("x2d", [R, D])
    h2d = dint("h2d", [NKD, 128, R])
    Pd = dint("Pd", [NIC, 128, R])
    pwd = dint("pwd", [3, NKD, 2, 128, D], BF16)
    igtd = dint("igtd", [NKD, 128, II], BF16)
    iutd = dint("iutd", [NKD, 128, II], BF16)
    idtd = dint("idtd", [NIC, 128, D], BF16)
    rowsd = dint("rowsd", [4, C, D])
    sdrowd = dint("sdrowd", [1, D])
    gin_d = dint("gin", [128, 4106]); gout_d = dint("gout", [128, 4106])
    mx1i = dint("mx1i", [1, 8]); mx1o = dint("mx1o", [1, 8])
    mx2i = dint("mx2i", [1, 8]); mx2o = dint("mx2o", [1, 8])
    RG = [list(range(8))]

    dbg_names = []
    es = ExitStack()

    def sb(name, cols, dt=F32):
        return es.enter_context(nc.sbuf_tensor(name, [128, cols], dt))

    pb = [es.enter_context(nc.psum_tensor(f"pb{i}", [128, 512], F32)) for i in range(6)]
    pbb = [es.enter_context(nc.psum_tensor(f"pbb{i}", [128, 1024], BF16)) for i in range(2)]

    sST = Sem(es, nc, "sST")
    sT = Sem(es, nc, "sT"); sV = Sem(es, nc, "sV"); sA = Sem(es, nc, "sA")
    sG = Sem(es, nc, "sG"); sCC = Sem(es, nc, "sCC")
    sINIT = Sem(es, nc, "sINIT")
    sWM = [Sem(es, nc, "sWM0"), Sem(es, nc, "sWM1")]
    sWP = Sem(es, nc, "sWP")
    sPW = [Sem(es, nc, "sPW0"), Sem(es, nc, "sPW1")]
    sX = Sem(es, nc, "sX")
    sBC = Sem(es, nc, "sBC")
    sGO = [Sem(es, nc, f"sGO{i}") for i in range(4)]
    sRW = Sem(es, nc, "sRW")
    sPWD = Sem(es, nc, "sPWD")
    sX2 = Sem(es, nc, "sX2")
    sH2S = [Sem(es, nc, "sH2S0"), Sem(es, nc, "sH2S1")]
    sGS = [Sem(es, nc, "sGS0"), Sem(es, nc, "sGS1")]
    ALL_DMA_SEMS = [sST, sINIT, sWM[0], sWM[1], sWP, sPW[0], sPW[1], sX, sBC,
                    sGO[0], sGO[1], sGO[2], sGO[3],
                    sRW, sPWD, sX2, sH2S[0], sH2S[1],
                    sGS[0], sGS[1]]
    ESEM = {"vector": sV, "scalar": sA, "gpsimd": sG, "tensor": sT}

    P = Prog()

    IDF = sb("IDF", 128)
    IDB = sb("IDB", 128, BF16)
    ONESROW = sb("ONESROW", 128)
    CB49 = sb("CB49", 2)
    CLIPS = sb("CLIPS", 2)
    MAXH = sb("MAXH", 256)
    SMALL = sb("SMALL", 32)
    es_ttt = ExitStack()

    def sbt(name, cols, dt=F32):
        return es_ttt.enter_context(nc.sbuf_tensor(name, [128, cols], dt))

    MODF = sbt("MODF", 384)
    W1T_m = sbt("W1T_m", 2048); W2_m = sbt("W2_m", 2048)
    B1C = sbt("B1C", 2); B2C = sbt("B2C", 8)
    W1TH = sbt("W1TH", 2048, BF16); W1TL = sbt("W1TL", 2048, BF16)
    W2H = sbt("W2H", 2048, BF16); W2L = sbt("W2L", 2048, BF16)
    W2TH = sbt("W2TH", 2048, BF16); W2TL = sbt("W2TL", 2048, BF16)
    PW = [sbt(f"PW{i}", 2048, BF16) for i in range(2)]
    XT = sbt("XT", 4096)
    STAT = sbt("STAT", 64)
    TMPA = sbt("TMPA", 4096)
    T512 = sbt("T512", 512)
    T1K = sbt("T1K", 1024)
    H1H = sbt("H1H", 4096, BF16); H1L = sbt("H1L", 4096, BF16)
    TKN = sbt("TKN", 1024)
    TKH = sbt("TKH", 1024, BF16)
    TKL = sbt("TKL", 1024, BF16)
    PWS = sbt("PWS", 2048, BF16)
    GBIA = sbt("GBIA", 16)
    GSTG = [sbt(f"GSTG{i}", 512) for i in range(2)]
    GOUT = [sbt(f"GOUT{i}", 512) for i in range(4)]
    H2S = [sbt(f"H2S{i}", 512) for i in range(2)]
    ZTS = [sbt(f"ZTS{i}", 1024, BF16) for i in range(2)]

    def modcol(vec, kd, b):
        i = vec * 64 + b * 8 + kd
        return MODF[:, i:i + 1]

    def modset(vec, b):
        i = vec * 64 + b * 8
        return MODF[:, i:i + 8]

    def load(out_ap, in_ap, sem, eng="sync", waits=(), slow=False):
        if slow:
            return P.op(eng, lambda e, o=out_ap, i=in_ap: e.dma_start(
                out=o, in_=i, allow_slow_non_contiguous=True),
                waits=waits, inc=sem, inc_n=16)
        return P.op(eng, lambda e, o=out_ap, i=in_ap: e.dma_start(out=o, in_=i),
                    waits=waits, inc=sem, inc_n=16)

    def store(out_ap, in_ap, sem, eng="gpsimd", waits=()):
        return P.op(eng, lambda e, o=out_ap, i=in_ap: e.dma_start(out=o, in_=i),
                    waits=waits, inc=sem, inc_n=16)

    def ew(eng, fn, waits=()):
        sw = [(ESEM[eng], ESEM[eng].n)]
        return P.op(eng, fn, waits=list(waits) + sw, inc=ESEM[eng])

    def dump(name, ap_, waits):
        if name not in dbg:
            return
        t = nc.dram_tensor("dbg_" + name, list(ap_.shape), ap_.dtype,
                           kind="ExternalOutput").ap()
        dbg_names.append(name)
        P.op("gpsimd", lambda e, o=t, i=ap_: e.dma_start(out=o, in_=i),
             waits=waits, inc=sST, inc_n=16)

    class Banks:
        def __init__(self, n):
            self.n = n
            self.guard = [None] * n
            self.rr = 0

        def acq(self, pick=None):
            i = self.rr if pick is None else pick
            if pick is None:
                self.rr = (self.rr + 1) % self.n
            if self.guard[i] is not None:
                s, v = self.guard[i]
                P.wait("tensor", s, v)
                self.guard[i] = None
            return i

        def rel(self, i, sem, count):
            self.guard[i] = (sem, count)

    BK = Banks(6)
    BKB = Banks(2)

    def mm_group(bank_ap, mms, waits=()):
        n = len(mms)
        for j, (lt, rh) in enumerate(mms):
            P.op("tensor",
                 lambda e, o=bank_ap, lt=lt, rh=rh, st=(j == 0), sp=(j == n - 1):
                 e.matmul(o, lt, rh, start=st, stop=sp),
                 waits=waits if j == 0 else (),
                 inc=sT if j == n - 1 else None)
        return sT.n

    def tr_group(pairs, ident, waits=()):
        n = len(pairs)
        for j, (o, i) in enumerate(pairs):
            P.op("tensor", lambda e, o=o, i=i, idn=ident: e.transpose(o, i, idn),
                 waits=waits if j == 0 else (),
                 inc=sT if j == n - 1 else None)
        return sT.n

    def copy_pass(eng, dst, src, waits=()):
        if eng == "scalar":
            return ew(eng, lambda e, d=dst, s=src: e.activation(d, s, AF.Copy),
                      waits=waits)
        return ew(eng, lambda e, d=dst, s=src: e.tensor_copy(d, s), waits=waits)

    def split_hi_lo(src_ap, hi_ap, lo_ap, waits_hi=(), waits_lo=()):
        a = copy_pass("scalar", hi_ap, src_ap, waits=waits_hi)
        v = ew("vector", lambda e, d=lo_ap, s=src_ap, h=hi_ap: e.tensor_sub(d, s, h),
               waits=[(sA, a)] + list(waits_lo))
        return a, v

    es_pro = ExitStack()

    def sbp(name, cols, dt=F32):
        return es_pro.enter_context(nc.sbuf_tensor(name, [128, cols], dt))

    load(IDF[:], idf_in, sINIT)
    load(IDB[:], idb_in, sINIT)
    ew("vector", lambda e: e.memset(ONESROW[0:1, :], 1.0))
    ew("vector", lambda e: e.memset(CB49[:, 0:1], 127.49))
    ew("vector", lambda e: e.memset(CB49[:, 1:2], 255.98))
    ew("vector", lambda e: e.memset(MAXH[:], 0.0))

    W1N = sbp("W1N", 2048)
    CT = sbp("CT", 1024)
    SILCT = sbp("SILCT", 64)
    load(W1N[:].rearrange("p (c d) -> p c d", c=2),
         w1_in.rearrange("(c p) d -> p c d", p=128), sINIT)
    load(W2_m[:].rearrange("p (c h) -> p c h", c=8),
         w2_in.rearrange("(c p) h -> p c h", p=128), sINIT)
    load(CLIPS[0:1, 0:1], clipx_in.rearrange("(o x) -> o x", o=1), sINIT)
    load(CLIPS[0:1, 1:2], clipd_in.rearrange("(o x) -> o x", o=1), sINIT)
    load(B1C[:], b1_in.rearrange("(c p) -> p c", p=128), sINIT, slow=True)
    load(B2C[:], b2_in.rearrange("(c p) -> p c", p=128), sINIT, slow=True)
    load(CT[0:8, :], c_in, sINIT)
    vINIT = sINIT.n

    for g in range(4):
        bk = BK.acq()
        tiles = []
        for j in range(4):
            hc, kd = divmod(g * 4 + j, 8)
            tiles.append((pb[bk][:, j * 128:(j + 1) * 128],
                          W1N[:, hc * 1024 + kd * 128: hc * 1024 + (kd + 1) * 128]))
        tt = tr_group(tiles, IDF[:], waits=[(sINIT, vINIT)])
        vv = None
        for j in range(4):
            hc, kd = divmod(g * 4 + j, 8)
            dst = W1T_m[:, kd * 256 + hc * 128: kd * 256 + (hc + 1) * 128]
            vv = copy_pass("vector", dst, pb[bk][:, j * 128:(j + 1) * 128],
                           waits=[(sT, tt)] if j == 0 else ())
        BK.rel(bk, sV, vv)
    v_w1t = sV.n

    def resplit_params(waits_w1, waits_w2):
        split_hi_lo(W1T_m[:], W1TH[:], W1TL[:], waits_hi=waits_w1)
        split_hi_lo(W2_m[:], W2H[:], W2L[:], waits_hi=waits_w2)
        for g in range(4):
            bk = BK.acq()
            tiles = []
            for j in range(4):
                c2, hc = divmod(g * 4 + j, 2)
                tiles.append((pb[bk][:, j * 128:(j + 1) * 128],
                              W2_m[:, c2 * 256 + hc * 128: c2 * 256 + (hc + 1) * 128]))
            tt = tr_group(tiles, IDF[:], waits=waits_w2 if g == 0 else ())
            vv = None
            for j in range(4):
                c2, hc = divmod(g * 4 + j, 2)
                sl = slice(hc * 1024 + c2 * 128, hc * 1024 + (c2 + 1) * 128)
                src = pb[bk][:, j * 128:(j + 1) * 128]
                ah = copy_pass("scalar", W2TH[:, sl], src,
                               waits=[(sT, tt)] if j == 0 else ())
                vv = ew("vector", lambda e, d=W2TL[:, sl], s=src, h=W2TH[:, sl]:
                        e.tensor_sub(d, s, h), waits=[(sA, ah)])
            BK.rel(bk, sV, vv)
        return sA.n, sV.n

    resplit_params([(sV, v_w1t)], [(sINIT, vINIT)])

    aC = ew("scalar", lambda e: e.activation(CT[0:8, :], CT[0:8, :], AF.Silu),
            waits=[(sINIT, vINIT)])
    bk = BK.acq()
    tS = tr_group([(pb[bk][:, kd * 8:(kd + 1) * 8],
                    CT[0:8, kd * 128:(kd + 1) * 128]) for kd in range(NKD)],
                  IDF[0:8, 0:8], waits=[(sA, aC), (sINIT, vINIT)])
    vS = copy_pass("vector", SILCT[:], pb[bk][:, 0:64], waits=[(sT, tS)])
    BK.rel(bk, sV, vS)

    WMC = [sbp(f"WMC{i}", 4096) for i in range(2)]
    MODT = sbp("MODT", 6144)
    wm_guard = {}
    vMT = None
    for msl in range(12):
        waits = [(sT, wm_guard[msl - 2])] if msl >= 2 else ()
        ldw = load(WMC[msl % 2][:].rearrange("p (kd n) -> p kd n", kd=NKD),
                   wmod_in[:, msl * 512:(msl + 1) * 512]
                   .rearrange("(kd p) n -> p kd n", p=128), sWM[msl % 2],
                   waits=waits)
        bk = BK.acq()
        mms = [(SILCT[:, kd * 8:(kd + 1) * 8],
                WMC[msl % 2][:, kd * 512:(kd + 1) * 512]) for kd in range(NKD)]
        tmm = mm_group(pb[bk][0:8, :], mms, waits=[(sWM[msl % 2], ldw), (sV, vS)])
        wm_guard[msl] = tmm
        vMT = copy_pass("vector", MODT[0:8, msl * 512:(msl + 1) * 512],
                        pb[bk][0:8, :], waits=[(sT, tmm)])
        BK.rel(bk, sV, vMT)
    vMOD = None
    for j in range(48):
        vec, kd = divmod(j, 8)
        bk = BK.acq()
        tt = tr_group([(pb[bk][:, 0:8], MODT[0:8, j * 128:(j + 1) * 128])],
                      IDF[0:8, 0:8], waits=[(sV, vMT)] if j == 0 else ())
        dst = MODF[:, vec * 64:(vec + 1) * 64].rearrange(
            "p (b k) -> p b k", b=8, k=8)[:, :, kd:kd + 1]
        vMOD = ew("vector", lambda e, d=dst, s=pb[bk][:, 0:8].unsqueeze(2):
                  e.tensor_copy(d, s), waits=[(sT, tt)])
        BK.rel(bk, sV, vMOD)
    dump("modf", MODF[:], [(sV, vMOD)])

    for vec in (V_CA1, V_CB1):
        sl = slice(vec * 64, (vec + 1) * 64)
        vMOD = ew("vector", lambda e, sl=sl: e.tensor_scalar(
            MODF[:, sl], MODF[:, sl], 1.0, None, ALU.add), waits=[(sV, vMOD)])
    GVECS = (V_GA, V_GB)
    def gsl(vec):
        return slice(vec * 64, (vec + 1) * 64)
    for i, vec in enumerate(GVECS):
        vMOD = ew("vector", lambda e, sl=gsl(vec), t=CT[:, i * 256:i * 256 + 64]:
                  e.tensor_scalar(t, MODF[:, sl], 0.0, 2.0, ALU.is_ge, ALU.mult),
                  waits=[(sV, vMOD)] if i == 0 else ())
    for i, vec in enumerate(GVECS):
        vMOD = ew("vector", lambda e, t=CT[:, i * 256:i * 256 + 64]:
                  e.tensor_scalar(t, t, -1.0, None, ALU.add))
    for i, vec in enumerate(GVECS):
        aM = ew("scalar", lambda e, sl=gsl(vec), t=CT[:, i * 256 + 64:i * 256 + 128]:
                e.activation(t, MODF[:, sl], AF.Abs),
                waits=[(sV, vMOD)] if i == 0 else ())
    for i, vec in enumerate(GVECS):
        vMOD = ew("vector", lambda e, t=CT[:, i * 256 + 64:i * 256 + 128]:
                  e.tensor_scalar(t, t, 0.25, None, ALU.max),
                  waits=[(sA, aM)] if i == 0 else ())
    for i, vec in enumerate(GVECS):
        vMOD = ew("vector", lambda e, sl=gsl(vec), t=CT[:, i * 256:i * 256 + 64],
                  t2=CT[:, i * 256 + 64:i * 256 + 128]:
                  e.tensor_mul(MODF[:, sl], t2, t))
    vMODF = vMOD
    dump("modp", MODF[:], [(sV, vMODF)])

    ROWS = sbp("ROWS", 128)
    for ri, vec in enumerate((V_GA, V_CB1, V_SB, V_GB)):
        for b in range(C):
            bk = BK.acq()
            tt = tr_group([(pb[bk][0:8, 0:128], modset(vec, b))], IDF[:],
                          waits=[(sV, vMODF)])
            vv = copy_pass("vector", ROWS[0:8, :], pb[bk][0:8, 0:128],
                           waits=[(sT, tt), (sRW, sRW.n)])
            BK.rel(bk, sV, vv)
            store(rowsd[ri, b, :].rearrange("(a f) -> a f", a=8), ROWS[0:8, :],
                  sRW, waits=[(sV, vv)])
    stROWS = sRW.n

    st_pws_prev = None
    pwst = {}
    prev_ah = prev_vl = None
    for w in range(3):
        for mh in range(NKD):
            wts = [(sPWD, st_pws_prev)] if st_pws_prev else []
            ldw_ = ([(sA, prev_ah), (sV, prev_vl)] if prev_ah is not None else [])
            ld = load(TKN[:], tw_in[w][mh * 128:(mh + 1) * 128, :], sWP, waits=ldw_)
            ah, vl = split_hi_lo(TKN[:], TKH[:], TKL[:],
                                 waits_hi=[(sWP, ld)] + wts)
            prev_ah, prev_vl = ah, vl
            lastev = {"scalar": None, "vector": None}
            for hl, src in ((0, TKH), (1, TKL)):
                for g in range(2):
                    bk = BKB.acq()
                    tiles = [(pbb[bk][:, j * 128:(j + 1) * 128],
                              src[:, (g * 4 + j) * 128:(g * 4 + j + 1) * 128])
                             for j in range(4)]
                    wt = [(sA, ah), (sV, vl)] if (hl == 0 and g == 0) else ()
                    tt = tr_group(tiles, IDB[:], waits=wt)
                    dst = PWS[:, hl * 1024 + g * 512: hl * 1024 + (g + 1) * 512]
                    eng = "scalar" if (g + hl) % 2 == 0 else "vector"
                    vv = copy_pass(eng, dst, pbb[bk][:, 0:512], waits=[(sT, tt)])
                    lastev[eng] = vv
                    BKB.rel(bk, ESEM[eng], vv)
            wts2 = [(ESEM[k], v) for k, v in lastev.items() if v is not None]
            store(pwd[w, mh, 0, :, :], PWS[:, 0:1024], sPWD, waits=wts2)
            st_pws_prev = store(pwd[w, mh, 1, :, :], PWS[:, 1024:2048], sPWD)
            pwst[(w, mh)] = st_pws_prev
    stPW = sPWD.n

    es_pro.close()
    ZH = sbt("ZH", 4096, BF16); ZL = sbt("ZL", 4096, BF16)
    BCA = sbt("BCA", 1024)
    QH = sbt("QH", 4096, BF16); QL = sbt("QL", 4096, BF16)
    AFB = sbt("AFB", 1024)
    HFF = sbt("HFF", 1024)
    T2K = sbt("T2K", 1024)
    HH_ = sbt("HH_", 1024, BF16); HL_ = sbt("HL_", 1024, BF16)
    HTH = sbt("HTH", 1024, BF16); HTL = sbt("HTL", 1024, BF16)
    GAH = sbt("GAH", 1024, BF16); GAL = sbt("GAL", 1024, BF16)
    GATH = sbt("GATH", 1024, BF16); GATL = sbt("GATL", 1024, BF16)

    def load_x(t, barrier=False, extra=()):
        w = (([(s, s.n) for s in (sV, sA, sT, sG, sINIT, sWM[0], sWM[1], sWP,
                                  sPWD, sRW)])
             if barrier else []) + list(extra)
        return load(XT[:].rearrange("p (tc d) -> p tc d", tc=NTC),
                    x_in[t].rearrange("(tc p) d -> p tc d", p=128), sX, waits=w)

    def stat_a(tc):
        return STAT[:, tc * 16 + 14: tc * 16 + 15]

    def stat_b(tc):
        return STAT[:, tc * 16 + 15: tc * 16 + 16]

    def emit_ln(src, waits, sqtmp=None):
        sq = sqtmp if sqtmp is not None else T1K
        v = None
        for tc in range(NTC):
            b = tc * 16
            xs = src[:, tc * 1024:(tc + 1) * 1024]
            v = ew("vector", lambda e, b=b, xs=xs: e.reduce_sum(
                STAT[:, b:b + 1], xs, axis=AX),
                waits=waits if tc == 0 else ())
            a = ew("scalar", lambda e, b=b, xs=xs: e.activation(
                sq[:], xs, AF.Square, accum_out=STAT[:, b + 1:b + 2]),
                waits=waits if tc == 0 else ())
        vS1, aS1 = sV.n, sA.n
        def stage(fn, wt=()):
            vv = None
            for tc in range(NTC):
                vv = ew("vector", lambda e, b=tc * 16: fn(e, b),
                        waits=wt if tc == 0 else ())
            return vv
        stage(lambda e, b: e.tensor_scalar(
            STAT[:, b + 2:b + 3], STAT[:, b:b + 1], 1.0 / D, None, ALU.mult),
            wt=[(sV, vS1), (sA, aS1)])
        stage(lambda e, b: e.tensor_scalar(
            STAT[:, b + 3:b + 4], STAT[:, b + 1:b + 2], 1.0 / D, None, ALU.mult))
        stage(lambda e, b: e.scalar_tensor_tensor(
            STAT[:, b + 4:b + 5], STAT[:, b + 2:b + 3], 1.0,
            STAT[:, b + 2:b + 3], ALU.mult, ALU.mult))
        stage(lambda e, b: e.tensor_sub(
            STAT[:, b + 3:b + 4], STAT[:, b + 3:b + 4], STAT[:, b + 4:b + 5]))
        v = stage(lambda e, b: e.tensor_scalar(
            STAT[:, b + 3:b + 4], STAT[:, b + 3:b + 4], EPS, None, ALU.add))
        aq = None
        for tc in range(NTC):
            aq = ew("scalar", lambda e, b=tc * 16: e.activation(
                STAT[:, b + 5:b + 6], STAT[:, b + 3:b + 4], AF.Sqrt),
                waits=[(sV, v)] if tc == 0 else ())
        stage(lambda e, b: e.reciprocal(
            STAT[:, b + 6:b + 7], STAT[:, b + 5:b + 6]), wt=[(sA, aq)])
        stage(lambda e, b: e.scalar_tensor_tensor(
            STAT[:, b + 7:b + 8], STAT[:, b + 6:b + 7], 1.0,
            STAT[:, b + 6:b + 7], ALU.mult, ALU.mult))
        stage(lambda e, b: e.tensor_mul(
            STAT[:, b + 8:b + 9], STAT[:, b + 7:b + 8], STAT[:, b + 3:b + 4]))
        stage(lambda e, b: e.tensor_scalar(
            STAT[:, b + 9:b + 10], STAT[:, b + 8:b + 9], -0.5, 1.5,
            ALU.mult, ALU.add))
        stage(lambda e, b: e.tensor_mul(
            STAT[:, b + 14:b + 15], STAT[:, b + 6:b + 7], STAT[:, b + 9:b + 10]))
        v = stage(lambda e, b: e.scalar_tensor_tensor(
            STAT[:, b + 15:b + 16], STAT[:, b + 2:b + 3], -1.0,
            STAT[:, b + 14:b + 15], ALU.mult, ALU.mult))
        return v
        return v

    def emit_h1(t, waits_x, vln):
        vu = None
        for tc in range(NTC):
            vu = ew("vector", lambda e, tc=tc: e.tensor_scalar(
                TMPA[:, tc * 1024:(tc + 1) * 1024], XT[:, tc * 1024:(tc + 1) * 1024],
                stat_a(tc), stat_b(tc), ALU.mult, ALU.add),
                waits=([(sV, vln)] + list(waits_x)) if tc == 0 else ())
        dump(f"u_dbg_{t}", TMPA[:], [(sV, vu)])
        dump(f"stat_dbg_{t}", STAT[:], [(sV, vu)])
        vh = None
        tlast = None
        for kd in range(NKD):
            bk = BK.acq()
            tiles = [(pb[bk][:, tc * 128:(tc + 1) * 128],
                      TMPA[:, tc * 1024 + kd * 128: tc * 1024 + (kd + 1) * 128])
                     for tc in range(NTC)]
            tlast = tr_group(tiles, IDF[:], waits=[(sV, vu)] if kd == 0 else ())
            ca1 = modcol(V_CA1, kd, t); sa_ = modcol(V_SA, kd, t)
            dsth = H1H[:, kd * 512:(kd + 1) * 512]
            dstl = H1L[:, kd * 512:(kd + 1) * 512]
            vh = ew("vector", lambda e, d=dsth, s=pb[bk][:], a=ca1, b=sa_:
                    e.tensor_scalar(d, s, a, b, ALU.mult, ALU.add),
                    waits=[(sT, tlast)])
            vh = ew("vector", lambda e, s=pb[bk][:], a=ca1, b=sa_:
                    e.tensor_scalar(T512[:], s, a, b, ALU.mult, ALU.add))
            vh = ew("vector", lambda e, d=dstl, h=dsth:
                    e.tensor_sub(d, T512[:], h))
            BK.rel(bk, sV, vh)
        return vh, sT.n

    pw_state = {"k": 0, "guards": {}}

    def pw_issue(w, mh):
        k = pw_state["k"]; pw_state["k"] += 1
        waits = [(sPWD, pwst[(w, mh)])]
        if k >= 2:
            waits.append((sT, pw_state["guards"][k - 2]))
        sl = PW[k % 2]
        load(sl[:, 0:1024], pwd[w, mh, 0, :, :], sPW[k % 2], waits=waits)
        ldv = load(sl[:, 1024:2048], pwd[w, mh, 1, :, :], sPW[k % 2])
        return k, sl, ldv

    def emit_proj(t, which, vh1, y_extra_waits=()):
        dst = {0: (ZH, ZL), 2: (QH, QL)}
        for w in which:
            for mh in range(NKD):
                k, sl, ldv = pw_issue(w, mh)
                bk = BK.acq()
                mms = []
                for kd in range(NKD):
                    lt_h = sl[:, kd * 128:(kd + 1) * 128]
                    lt_l = sl[:, 1024 + kd * 128: 1024 + (kd + 1) * 128]
                    rh_h = H1H[:, kd * 512:(kd + 1) * 512]
                    rh_l = H1L[:, kd * 512:(kd + 1) * 512]
                    mms += [(lt_h, rh_h), (lt_h, rh_l), (lt_l, rh_h)]
                tmm = mm_group(pb[bk][:], mms, waits=[(sPW[k % 2], ldv), (sV, vh1)])
                pw_state["guards"][k] = tmm
                if w == 1:
                    vv = copy_pass("vector", TMPA[:, mh * 512:(mh + 1) * 512],
                                   pb[bk][:],
                                   waits=([(sT, tmm)] + list(y_extra_waits))
                                   if mh == 0 else [(sT, tmm)])
                else:
                    dh, dl = dst[w]
                    oh = dh[:, mh * 512:(mh + 1) * 512]
                    ol = dl[:, mh * 512:(mh + 1) * 512]
                    ah = copy_pass("scalar", oh, pb[bk][:], waits=[(sT, tmm)])
                    vv = ew("vector", lambda e, d=ol, s=pb[bk][:], h=oh:
                            e.tensor_sub(d, s, h), waits=[(sA, ah)])
                BK.rel(bk, sV, vv)
        return sV.n, sA.n

    def emit_tm(srcs, dsts, nmh, waits0):
        rowlen = nmh * 128
        for hl in range(2):
            src, dstt = srcs[hl], dsts[hl]
            for tc in range(NTC):
                ngroups = max(1, nmh // 4)
                for g in range(ngroups):
                    gw = min(4, nmh)
                    bk = BKB.acq()
                    tiles = [(pbb[bk][:, j * 128:(j + 1) * 128],
                              src[:, (g * 4 + j) * 512 + tc * 128:
                                  (g * 4 + j) * 512 + (tc + 1) * 128])
                             for j in range(gw)]
                    tt = tr_group(tiles, IDB[:],
                                  waits=waits0 if (hl == 0 and tc == 0 and g == 0) else ())
                    dstv = dstt[:, tc * rowlen + g * 512: tc * rowlen + g * 512 + gw * 128]
                    eng = "scalar" if (tc + hl) % 2 == 0 else "vector"
                    vv = copy_pass(eng, dstv, pbb[bk][:, 0:gw * 128], waits=[(sT, tt)])
                    BKB.rel(bk, ESEM[eng], vv)
        return sV.n, sA.n

    ld_x0 = load_x(0, barrier=False)
    vln = emit_ln(XT[:], [(sX, ld_x0), (sV, vMODF)])
    vh1, _th1 = emit_h1(0, [(sX, ld_x0)], vln)
    dump("h1h_0", H1H[:], [(sV, vh1)])
    dump("h1l_0", H1L[:], [(sV, vh1)])
    emit_proj(0, [0, 1, 2], vh1)
    vproj, aproj = sV.n, sA.n
    dump("zh_0", ZH[:], [(sV, vproj), (sA, aproj)])
    dump("y_0", TMPA[:], [(sV, vproj), (sA, aproj)])
    dump("qh_0", QH[:], [(sV, vproj), (sA, aproj)])

    prev_vh2 = None
    prev_th1 = _th1
    h2s_slot_guard = {}
    for t in range(n_steps):
        for mh in range(NHC):
            bk = BK.acq()
            mms = []
            for kd in range(NKD):
                lt_h = W1TH[:, kd * 256 + mh * 128: kd * 256 + (mh + 1) * 128]
                lt_l = W1TL[:, kd * 256 + mh * 128: kd * 256 + (mh + 1) * 128]
                rh_h = ZH[:, kd * 512:(kd + 1) * 512]
                rh_l = ZL[:, kd * 512:(kd + 1) * 512]
                mms += [(lt_h, rh_h), (lt_h, rh_l), (lt_l, rh_h)]
            tA = mm_group(pb[bk][:], mms, waits=[(sV, vproj), (sA, aproj)])
            b1c = B1C[:, mh:mh + 1]
            aH = ew("scalar", lambda e, d=HFF[:, mh * 512:(mh + 1) * 512],
                    s=pb[bk][:], b=b1c: e.activation(d, s, AF.Gelu, bias=b),
                    waits=[(sT, tA)])
            vA = ew("vector", lambda e, d=AFB[:, mh * 512:(mh + 1) * 512],
                    s=pb[bk][:], b=b1c: e.tensor_scalar(d, s, b, None, ALU.add),
                    waits=[(sA, aH)])
            BK.rel(bk, sV, vA)
        aH_all, vA_all = sA.n, sV.n
        aHH, vHL = split_hi_lo(HFF[:], HH_[:], HL_[:], waits_hi=[(sA, aH_all)])
        dump(f"A_{t}", AFB[:], [(sV, vA_all)])
        dump(f"H_{t}", HFF[:], [(sA, aH_all)])
        emit_tm((HH_, HL_), (HTH, HTL), NHC, [(sA, aHH), (sV, vHL)])

        agp0 = ew("scalar", lambda e: e.activation(T1K[:], AFB[:], AF.Abs),
                  waits=[(sV, vA_all)])
        vgp = ew("vector", lambda e: e.tensor_scalar(
            T1K[:], T1K[:], TINY, None, ALU.is_lt), waits=[(sA, agp0)])
        vgp = ew("vector", lambda e: e.tensor_add(T2K[:], AFB[:], T1K[:]))
        vgp = ew("vector", lambda e: e.reciprocal(T2K[:], T2K[:]))
        vgp = ew("vector", lambda e: e.tensor_mul(T2K[:], T2K[:], HFF[:]),
                 waits=[(sA, aHH)])
        vgp = ew("vector", lambda e: e.tensor_scalar(
            HFF[:], T2K[:], -1.0, 0.5, ALU.mult, ALU.add))
        vgp = ew("vector", lambda e: e.tensor_mul(HFF[:], HFF[:], T1K[:]))
        vgp = ew("vector", lambda e: e.tensor_add(T2K[:], T2K[:], HFF[:]))
        agp = ew("scalar", lambda e: e.activation(T1K[:], AFB[:], AF.Square),
                 waits=[(sV, vgp)])
        agp = ew("scalar", lambda e: e.activation(T1K[:], T1K[:], AF.Exp, scale=-0.5))
        vgp = ew("vector", lambda e: e.scalar_tensor_tensor(
            T1K[:], AFB[:], INV_SQRT_2PI, T1K[:], ALU.mult, ALU.mult),
            waits=[(sA, agp)])
        vgp = ew("vector", lambda e: e.tensor_add(T2K[:], T2K[:], T1K[:]))
        dump(f"gp_{t}", T2K[:], [(sV, vgp)])

        vGr = aGr = None
        for md2 in range(NKD):
            bk = BK.acq()
            mms = []
            for kh in range(NHC):
                lt_h = W2TH[:, kh * 1024 + md2 * 128: kh * 1024 + (md2 + 1) * 128]
                lt_l = W2TL[:, kh * 1024 + md2 * 128: kh * 1024 + (md2 + 1) * 128]
                rh_h = HH_[:, kh * 512:(kh + 1) * 512]
                rh_l = HL_[:, kh * 512:(kh + 1) * 512]
                mms += [(lt_h, rh_h), (lt_h, rh_l), (lt_l, rh_h)]
            sl = slice(md2 * 512, (md2 + 1) * 512)
            mms += [(IDB[:], ZH[:, sl]), (IDB[:], ZL[:, sl])]
            tR = mm_group(pb[bk][:], mms,
                          waits=[(sA, aHH), (sV, vHL)] if md2 == 0 else ())
            b2c = B2C[:, md2:md2 + 1]
            vE = ew("vector", lambda e, s=pb[bk][:], b=b2c, y=TMPA[:, sl]:
                    e.scalar_tensor_tensor(T512[:], s, b, y, ALU.add, ALU.subtract),
                    waits=[(sT, tR)])
            aGr = ew("scalar", lambda e, d=H1H[:, sl],
                     acc=GBIA[:, 2 + md2:3 + md2]:
                     e.activation(d, T512[:], AF.Copy, scale=COEF, accum_out=acc),
                     waits=[(sV, vE)])
            vGr = ew("vector", lambda e, d=H1L[:, sl], h=H1H[:, sl]:
                     e.scalar_tensor_tensor(d, T512[:], COEF, h, ALU.mult,
                                            ALU.subtract),
                     waits=[(sA, aGr)])
            BK.rel(bk, sV, vGr)
        vGr_all, aGr_all = sV.n, sA.n
        dump(f"Gr_{t}", H1H[:], [(sV, vGr_all), (sA, aGr_all)])

        vga = None
        for mh in range(NHC):
            bk = BK.acq()
            mms = []
            for kd2 in range(NKD):
                lt_h = W2H[:, kd2 * 256 + mh * 128: kd2 * 256 + (mh + 1) * 128]
                lt_l = W2L[:, kd2 * 256 + mh * 128: kd2 * 256 + (mh + 1) * 128]
                rh_h = H1H[:, kd2 * 512:(kd2 + 1) * 512]
                rh_l = H1L[:, kd2 * 512:(kd2 + 1) * 512]
                mms += [(lt_h, rh_h), (lt_h, rh_l), (lt_l, rh_h)]
            tGh = mm_group(pb[bk][:], mms,
                           waits=[(sV, vGr_all), (sA, aGr_all)] if mh == 0 else ())
            sl = slice(mh * 512, (mh + 1) * 512)
            vga = ew("vector", lambda e, sl=sl, s=pb[bk][:],
                     acc=GBIA[:, mh:mh + 1]:
                     e.scalar_tensor_tensor(T1K[:, sl], s, 1.0, T2K[:, sl],
                                            ALU.mult, ALU.mult, accum_out=acc),
                     waits=[(sT, tGh), (sV, vgp)])
            BK.rel(bk, sV, vga)
        aga, vga = split_hi_lo(T1K[:], GAH[:], GAL[:], waits_hi=[(sV, vga)])
        dump(f"Ga_{t}", T1K[:], [(sV, vga)])
        emit_tm((GAH, GAL), (GATH, GATL), NHC, [(sA, aga), (sV, vga)])
        vtm_all, atm_all = sV.n, sA.n

        zts_guard = {}
        gst_guard = {}
        gsti = 0
        for md in range(NKD):
            zsl = ZTS[md % 2]
            zwaits = [(sT, zts_guard[md - 2])] if md >= 2 else ()
            evs = {}
            for hl, src in ((0, ZH), (1, ZL)):
                bk = BKB.acq()
                tiles = [(pbb[bk][:, tc * 128:(tc + 1) * 128],
                          src[:, md * 512 + tc * 128: md * 512 + (tc + 1) * 128])
                         for tc in range(NTC)]
                wt = list(zwaits) + ([(sV, vtm_all), (sA, atm_all)]
                                     if md == 0 and hl == 0 else [])
                tt = tr_group(tiles, IDB[:], waits=wt)
                eng = "scalar" if hl == 0 else "vector"
                vv = copy_pass(eng, zsl[:, hl * 512:(hl + 1) * 512],
                               pbb[bk][:, 0:512], waits=[(sT, tt)])
                evs[eng] = vv
                BKB.rel(bk, ESEM[eng], vv)
            bk = BK.acq()
            mms = []
            for tc in range(NTC):
                lt_h = zsl[:, tc * 128:(tc + 1) * 128]
                lt_l = zsl[:, 512 + tc * 128: 512 + (tc + 1) * 128]
                rh_h = GATH[:, tc * 256:(tc + 1) * 256]
                rh_l = GATL[:, tc * 256:(tc + 1) * 256]
                mms += [(lt_h, rh_h), (lt_h, rh_l), (lt_l, rh_h)]
            wmm = [(ESEM[k], v) for k, v in evs.items()]
            tg = mm_group(pb[bk][:, 0:256], mms, waits=wmm)
            zts_guard[md] = tg
            gw = [(sGS[gsti % 2], gst_guard[gsti - 2])] if gsti >= 2 else []
            vg = copy_pass("vector", GSTG[gsti % 2][:, 0:256], pb[bk][:, 0:256],
                           waits=[(sT, tg)] + gw)
            BK.rel(bk, sV, vg)
            gst_guard[gsti] = store(gin_d[:, md * 256:(md + 1) * 256],
                                    GSTG[gsti % 2][:, 0:256], sGS[gsti % 2],
                                    waits=[(sV, vg)])
            gsti += 1

        emit_tm((H1H, H1L), (ZH, ZL), NKD, [(sV, vGr_all), (sA, aGr_all)])
        vgrt_all, agrt_all = sV.n, sA.n
        for md2 in range(NKD):
            bk = BK.acq()
            mms = []
            for tc in range(NTC):
                lt_h = ZH[:, tc * 1024 + md2 * 128: tc * 1024 + (md2 + 1) * 128]
                lt_l = ZL[:, tc * 1024 + md2 * 128: tc * 1024 + (md2 + 1) * 128]
                rh_h = HTH[:, tc * 256:(tc + 1) * 256]
                rh_l = HTL[:, tc * 256:(tc + 1) * 256]
                mms += [(lt_h, rh_h), (lt_h, rh_l), (lt_l, rh_h)]
            tg = mm_group(pb[bk][:, 0:256], mms,
                          waits=[(sV, vgrt_all), (sA, agrt_all)] if md2 == 0 else ())
            gw = [(sGS[gsti % 2], gst_guard[gsti - 2])] if gsti >= 2 else []
            vg = copy_pass("vector", GSTG[gsti % 2][:, 0:256], pb[bk][:, 0:256],
                           waits=[(sT, tg)] + gw)
            BK.rel(bk, sV, vg)
            gst_guard[gsti] = store(gin_d[:, 2048 + md2 * 256: 2048 + (md2 + 1) * 256],
                                    GSTG[gsti % 2][:, 0:256], sGS[gsti % 2],
                                    waits=[(sV, vg)])
            gsti += 1
        store(gin_d[:, 4096:4106], GBIA[:, 0:10], sGS[0],
              waits=[(sV, vga), (sA, aGr_all)])

        if f"grads_{t}" in dbg:
            tdbg = nc.dram_tensor(f"dbg_grads_{t}", [128, 4106], F32,
                                  kind="ExternalOutput").ap()
            dbg_names.append(f"grads_{t}")
            P.op("gpsimd", lambda e, o=tdbg: e.dma_start(out=o, in_=gin_d[:, :]),
                 waits=[(sGS[0], sGS[0].n), (sGS[1], sGS[1].n)], inc=sST, inc_n=16)
        ccv = P.op("gpsimd", lambda e: e.collective_compute(
            "AllReduce", ALU.add, replica_groups=RG,
            ins=[gin_d.opt()], outs=[gout_d.opt()]),
            waits=[(sGS[0], sGS[0].n), (sGS[1], sGS[1].n)], inc=sCC)
        if f"gback_{t}" in dbg:
            tdbg = nc.dram_tensor(f"dbg_gback_{t}", [128, 4106], F32,
                                  kind="ExternalOutput").ap()
            dbg_names.append(f"gback_{t}")
            P.op("gpsimd", lambda e, o=tdbg: e.dma_start(out=o, in_=gout_d[:, :]),
                 waits=[(sCC, ccv)], inc=sST, inc_n=16)

        th1n = None
        if t + 1 < n_steps:
            ldx = load_x(t + 1, barrier=False,
                         extra=([(sV, prev_vh2)] if prev_vh2 else []) +
                         [(sX2, sX2.n)])
            vln2 = emit_ln(XT[:], [(sX, ldx), (sV, sV.n)])
            vh1n, th1n = emit_h1(t + 1, [(sX, ldx)], vln2)
            emit_proj(t + 1, [0], vh1n)

        go_guard = {}
        for ch in range(9):
            gw = [(sCC, ccv)]
            if ch >= 4:
                gw.append((sV, go_guard[ch - 4]))
            if ch < 8:
                ldg = load(GOUT[ch % 4][:], gout_d[:, ch * 512:(ch + 1) * 512],
                           sGO[ch % 4], waits=gw)
                if ch < 4:
                    dst = W1T_m[:, ch * 512:(ch + 1) * 512]
                else:
                    dst = W2_m[:, (ch - 4) * 512:(ch - 3) * 512]
                vv = ew("vector", lambda e, d=dst, g=GOUT[ch % 4][:]:
                        e.scalar_tensor_tensor(d, g, -LR, d, ALU.mult, ALU.add),
                        waits=[(sGO[ch % 4], ldg)])
                go_guard[ch] = vv
                csl = slice(ch * 512 % 2048, ch * 512 % 2048 + 512)
                if ch < 4:
                    split_hi_lo(W1T_m[:, csl], W1TH[:, csl], W1TL[:, csl],
                                waits_hi=[(sV, vv)])
            else:
                ldg = load(GBIA[:, 0:10], gout_d[:, 4096:4106], sGO[0], waits=gw)
                vv = ew("vector", lambda e: e.scalar_tensor_tensor(
                    B1C[:], GBIA[:, 0:2], -LR, B1C[:], ALU.mult, ALU.add),
                    waits=[(sGO[0], ldg)])
                vv = ew("vector", lambda e: e.scalar_tensor_tensor(
                    B2C[:], GBIA[:, 2:10], -LR, B2C[:], ALU.mult, ALU.add))
        vUPD = sV.n
        split_hi_lo(W2_m[:], W2H[:], W2L[:], waits_hi=[(sV, vUPD)])
        for g in range(4):
            bk = BK.acq()
            tiles = []
            for j in range(4):
                c2, hc = divmod(g * 4 + j, 2)
                tiles.append((pb[bk][:, j * 128:(j + 1) * 128],
                              W2_m[:, c2 * 256 + hc * 128:
                                   c2 * 256 + (hc + 1) * 128]))
            tt = tr_group(tiles, IDF[:], waits=[(sV, vUPD)] if g == 0 else ())
            vv2 = None
            for j in range(4):
                c2, hc = divmod(g * 4 + j, 2)
                sl = slice(hc * 1024 + c2 * 128, hc * 1024 + (c2 + 1) * 128)
                srcp = pb[bk][:, j * 128:(j + 1) * 128]
                ah = copy_pass("scalar", W2TH[:, sl], srcp,
                               waits=[(sT, tt)] if j == 0 else ())
                vv2 = ew("vector", lambda e, d=W2TL[:, sl], s=srcp, h=W2TH[:, sl]:
                         e.tensor_sub(d, s, h), waits=[(sA, ah)])
            BK.rel(bk, sV, vv2)
        aUP, vUP = sA.n, sV.n
        dump(f"W1m_{t}", W1T_m[:], [(sV, vUPD)])
        dump(f"W2m_{t}", W2_m[:], [(sV, vUPD)])

        ldxr = load_x(t, barrier=False,
                      extra=[(sT, th1n if th1n is not None else prev_th1),
                             (sX2, sX2.n)])
        if th1n is not None:
            prev_th1 = th1n

        for mh in range(NHC):
            bk = BK.acq()
            mms = []
            for kd in range(NKD):
                lt_h = W1TH[:, kd * 256 + mh * 128: kd * 256 + (mh + 1) * 128]
                lt_l = W1TL[:, kd * 256 + mh * 128: kd * 256 + (mh + 1) * 128]
                rh_h = QH[:, kd * 512:(kd + 1) * 512]
                rh_l = QL[:, kd * 512:(kd + 1) * 512]
                mms += [(lt_h, rh_h), (lt_h, rh_l), (lt_l, rh_h)]
            tA2 = mm_group(pb[bk][:], mms, waits=[(sA, aUP), (sV, vUP)])
            aH2 = ew("scalar", lambda e, d=HFF[:, mh * 512:(mh + 1) * 512],
                     s=pb[bk][:], b=B1C[:, mh:mh + 1]:
                     e.activation(d, s, AF.Gelu, bias=b), waits=[(sT, tA2)])
            BK.rel(bk, sA, aH2)
        aH2_all = sA.n
        aH2h, vH2l = split_hi_lo(HFF[:], HH_[:], HL_[:], waits_hi=[(sA, aH2_all)])
        for md2 in range(NKD):
            bk = BK.acq()
            mms = []
            for kh in range(NHC):
                lt_h = W2TH[:, kh * 1024 + md2 * 128: kh * 1024 + (md2 + 1) * 128]
                lt_l = W2TL[:, kh * 1024 + md2 * 128: kh * 1024 + (md2 + 1) * 128]
                rh_h = HH_[:, kh * 512:(kh + 1) * 512]
                rh_l = HL_[:, kh * 512:(kh + 1) * 512]
                mms += [(lt_h, rh_h), (lt_h, rh_l), (lt_l, rh_h)]
            sl = slice(md2 * 512, (md2 + 1) * 512)
            mms += [(IDB[:], QH[:, sl]), (IDB[:], QL[:, sl])]
            tO = mm_group(pb[bk][:], mms,
                          waits=[(sA, aH2h), (sV, vH2l)] if md2 == 0 else ())
            vO = ew("vector", lambda e, d=TMPA[:, sl], s=pb[bk][:],
                    b=B2C[:, md2:md2 + 1]:
                    e.tensor_scalar(d, s, b, None, ALU.add), waits=[(sT, tO)])
            BK.rel(bk, sV, vO)
        vATT = sV.n
        dump(f"attn_{t}", TMPA[:], [(sV, vATT)])

        ldga = load(BCA[:], rowsd[0, t, :].rearrange("(o d) -> o d", o=1)
                    .to_broadcast([128, D]), sBC,
                    waits=[(sRW, stROWS)] + ([(sV, prev_vh2)] if prev_vh2 else []))
        vx2 = None
        for tc in range(NTC):
            for g in range(2):
                bk = BK.acq()
                tiles = [(pb[bk][:, j * 128:(j + 1) * 128],
                          TMPA[:, (g * 4 + j) * 512 + tc * 128:
                               (g * 4 + j) * 512 + (tc + 1) * 128])
                         for j in range(4)]
                tt = tr_group(tiles, IDF[:],
                              waits=[(sV, vATT)] if (tc == 0 and g == 0) else ())
                dsl = slice(tc * 1024 + g * 512, tc * 1024 + (g + 1) * 512)
                vx2 = ew("vector", lambda e, s=pb[bk][:], b=BCA[:, g * 512:(g + 1) * 512]:
                         e.tensor_mul(T512[:], s, b),
                         waits=[(sT, tt), (sBC, ldga), (sX, ldxr)])
                vx2 = ew("vector", lambda e, d=XT[:, dsl]:
                         e.tensor_add(d, d, T512[:]))
                BK.rel(bk, sV, vx2)
        vX2 = sV.n
        store(x2d[t * S:(t + 1) * S, :].rearrange("(tc p) d -> p tc d", p=128),
              XT[:].rearrange("p (tc d) -> p tc d", tc=NTC), sX2,
              waits=[(sV, vX2)])
        dump(f"x2_{t}", XT[:], [(sV, vX2)])

        vln3 = emit_ln(XT[:], [(sV, vX2)])
        vu2 = None
        for tc in range(NTC):
            vu2 = ew("vector", lambda e, tc=tc: e.tensor_scalar(
                TMPA[:, tc * 1024:(tc + 1) * 1024], XT[:, tc * 1024:(tc + 1) * 1024],
                stat_a(tc), stat_b(tc), ALU.mult, ALU.add),
                waits=[(sV, vln3)] if tc == 0 else ())
        vh2 = None
        for kd in range(NKD):
            bk = BK.acq()
            tiles = [(pb[bk][:, tc * 128:(tc + 1) * 128],
                      TMPA[:, tc * 1024 + kd * 128: tc * 1024 + (kd + 1) * 128])
                     for tc in range(NTC)]
            tt = tr_group(tiles, IDF[:], waits=[(sV, vu2)] if kd == 0 else ())
            cb1 = modcol(V_CB1, kd, t); sb_ = modcol(V_SB, kd, t)
            hw = ([(sH2S[kd % 2], h2s_slot_guard[kd % 2])]
                  if kd % 2 in h2s_slot_guard else [])
            vh2 = ew("vector", lambda e, s=pb[bk][:], a=cb1, b=sb_,
                     d=H2S[kd % 2][:]:
                     e.tensor_scalar(d, s, a, b, ALU.mult, ALU.add),
                     waits=[(sT, tt)] + hw)
            vh2 = ew("vector", lambda e, d=H2S[kd % 2][:], c=t * 8 + kd:
                     e.reduce_max(MAXH[:, c:c + 1], d, axis=AX,
                                  apply_absolute_value=True))
            BK.rel(bk, sV, vh2)
            h2s_slot_guard[kd % 2] = store(
                h2d[kd, :, t * S:(t + 1) * S], H2S[kd % 2][:],
                sH2S[kd % 2], waits=[(sV, vh2)])
            dump(f"h2_{t}_{kd}", H2S[kd % 2][:], [(sV, vh2)])
        vH2 = sV.n
        prev_vh2 = vH2

        if t + 1 < n_steps:
            emit_proj(t + 1, [1], vh1n)
            emit_proj(t + 1, [2], vh1n)
            vproj, aproj = sV.n, sA.n


    es_ttt.close()

    es_ffn = ExitStack()

    def sbf(name, cols, dt=F32):
        return es_ffn.enter_context(nc.sbuf_tensor(name, [128, cols], dt))

    if do_ffn:
        sMXS = Sem(es, nc, "sMXS"); sMXL = Sem(es, nc, "sMXL")
        sFW = [Sem(es, nc, f"sFW{i}") for i in range(4)]
        sFS = [Sem(es, nc, "sFS0"), Sem(es, nc, "sFS1")]
        sIG = Sem(es, nc, "sIG")
        sH2L = [Sem(es, nc, "sH2L0"), Sem(es, nc, "sH2L1")]
        sPS = Sem(es, nc, "sPS"); sPL = Sem(es, nc, "sPL")
        sX2L = [Sem(es, nc, "sX2L0"), Sem(es, nc, "sX2L1")]
        sOUT = [Sem(es, nc, "sOUT0"), Sem(es, nc, "sOUT1")]
        ALL_DMA_SEMS.extend([sMXS, sMXL, sFW[2], sFW[3],
                             sFW[0], sFW[1], sFS[0], sFS[1], sIG,
                             sH2L[0], sH2L[1], sPS, sPL, sX2L[0], sX2L[1],
                             sOUT[0], sOUT[1]])

        FMISC = sbf("FMISC", 256)
        SXC = sbf("SXC", 8)
        SGC = sbf("SGC", 24); SUC = sbf("SUC", 24); SDC = sbf("SDC", 8)
        SXG = sbf("SXG", 24); SXU = sbf("SXU", 24)
        SDB = sbf("SDB", 1024)
        QV = [sbf(f"QV{i}", 512) for i in range(2)]
        QF = [sbf(f"QF{i}", 512) for i in range(2)]
        QJ = [sbf(f"QJ{i}", 512, I8) for i in range(2)]
        QC = [sbf(f"QC{i}", 512, BF16) for i in range(2)]
        SILG = sbf("SILG", 512)

        ttt_join = [(s_, s_.n) for s_ in
                    (sV, sA, sT, sG, sX2, sH2S[0], sH2S[1], sGS[0], sGS[1],
                     sX, sBC, sGO[0], sGO[1], sPW[0], sPW[1], sWM[0], sWM[1],
                     sWP, sINIT, sRW, sPWD) if True]

        def global_max(cols, mxi, mxo, fc, xw=()):
            v1 = ew("vector", lambda e: e.reduce_max(
                MAXH[:, 250:251], MAXH[:, cols[0]:cols[1]], axis=AX),
                waits=[(sV, sV.n)])
            bk = BK.acq()
            t1 = tr_group([(pb[bk][0:1, 0:128], MAXH[:, 250:251])], IDF[:],
                          waits=[(sV, v1)])
            a1 = copy_pass("scalar", FMISC[0:1, fc:fc + 128], pb[bk][0:1, 0:128],
                           waits=[(sT, t1)] + list(xw))
            BK.rel(bk, sA, a1)
            v2 = ew("vector", lambda e: e.reduce_max(
                FMISC[0:1, fc + 128:fc + 129], FMISC[0:1, fc:fc + 128], axis=AX),
                waits=[(sA, a1)])
            g1 = ew("gpsimd", lambda e: e.tensor_scalar(
                FMISC[0:1, fc + 136:fc + 144], ONESROW[0:1, 0:8],
                FMISC[0:1, fc + 128:fc + 129], None, ALU.mult),
                waits=[(sV, v2)])
            st1 = store(mxi, FMISC[0:1, fc + 136:fc + 144], sMXS, waits=[(sG, g1)])
            cc1 = P.op("gpsimd", lambda e: e.collective_compute(
                "AllReduce", ALU.max, replica_groups=RG,
                ins=[mxi.opt()], outs=[mxo.opt()]),
                waits=[(sMXS, st1)], inc=sCC)
            ld1 = load(FMISC[0:1, fc + 144:fc + 152], mxo, sMXL, waits=[(sCC, cc1)])
            return FMISC[0:1, fc + 144:fc + 145], ld1

        def scale_pair(gmax_ap, ld1, clip_col, sxc0, fc):
            a1 = ew("scalar", lambda e: e.activation(
                FMISC[0:1, fc:fc + 1], CLIPS[0:1, clip_col:clip_col + 1],
                AF.Sigmoid), waits=[(sINIT, vINIT)])
            v1 = ew("vector", lambda e: e.tensor_mul(
                FMISC[0:1, fc + 1:fc + 2], gmax_ap, FMISC[0:1, fc:fc + 1]),
                waits=[(sA, a1), (sMXL, ld1)])
            a2 = ew("scalar", lambda e: e.activation(
                FMISC[0:1, fc + 2:fc + 3], FMISC[0:1, fc + 1:fc + 2],
                AF.Copy, scale=1.0 / 127.0), waits=[(sV, v1)])
            v2 = ew("vector", lambda e: e.reciprocal(
                FMISC[0:1, fc + 3:fc + 4], FMISC[0:1, fc + 2:fc + 3]),
                waits=[(sA, a2)])
            v3 = ew("vector", lambda e: e.tensor_scalar(
                FMISC[0:1, fc + 4:fc + 5], FMISC[0:1, fc + 3:fc + 4],
                -1.0, None, ALU.mult))
            bk = BK.acq()
            tb = mm_group(pb[bk][:, 0:3],
                          [(ONESROW[0:1, 0:128], FMISC[0:1, fc + 2:fc + 5])],
                          waits=[(sV, v3)])
            ab = copy_pass("scalar", SXC[:, sxc0:sxc0 + 3], pb[bk][:, 0:3],
                           waits=[(sT, tb)])
            BK.rel(bk, sA, ab)
            return ab

        gm1, ldm1 = global_max((0, 64), mx1i, mx1o, 0)
        ab1 = scale_pair(gm1, ldm1, 0, 0, 160)

        es_fp = ExitStack()

        def sbq(name, cols, dt=F32):
            return es_fp.enter_context(nc.sbuf_tensor(name, [128, cols], dt))

        WQN = [sbq(f"WQN{i}", 2816) for i in range(4)]
        WQV = [sbq(f"WQV{i}", 2816) for i in range(2)]
        WQF = sbq("WQF", 2816); WQD = sbq("WQD", 2816)
        WQJ = sbq("WQJ", 2816, I8); WQCB = sbq("WQCB", 2816, BF16)
        WQB = sbq("WQB", 2816, BF16)
        WQS = [sbq(f"WQS{i}", 2816, BF16) for i in range(2)]
        RMX = sbq("RMX", 64); RQ1 = sbq("RQ1", 64); RQC = sbq("RQC", 64)

        fp_chunks = ([("g", c, wg_in, D, SGC, igtd) for c in range(NIC)] +
                     [("u", c, wu_in, D, SUC, iutd) for c in range(NIC)] +
                     [("d", c, wd_in, II, SDC, idtd) for c in range(NKD)])
        NCH = len(fp_chunks)
        wq_guard = {}
        ws_guard = {}
        prev_vc1 = None
        prev_tt = None
        prev_vd = None
        av_fp = {}
        for cx in range(NCH + 2):
            if 0 <= cx - 1 < NCH:
                j = cx - 1
                kindj, cj, w_inj, cwj, _, _ = fp_chunks[j]
                vrq = ew("vector", lambda e, j=j: e.reciprocal(
                    RQ1[:, j:j + 1], RMX[:, j:j + 1]))
                ew("scalar", lambda e, j=j: e.activation(
                    RQC[:, j:j + 1], RQ1[:, j:j + 1], AF.Copy, scale=127.0),
                    waits=[(sV, vrq)])
                av_fp[j] = ew(
                    "scalar", lambda e, s=WQN[j % 4][:, 0:cwj], cwj=cwj, j=j:
                    e.activation(WQV[j % 2][:, 0:cwj], s, AF.Copy,
                                 scale=RQC[:, j:j + 1]),
                    waits=([(sV, prev_vd)] if prev_vd else []))
            ci = cx - 2
            if ci >= 0:
                kind, c, w_in, cw, scol, dstd = fp_chunks[ci]
                p2 = ci % 2
                vj = ew("vector", lambda e, cw=cw, p2=p2: e.tensor_copy(
                    WQJ[:, 0:cw], WQV[p2][:, 0:cw]),
                    waits=[(sA, av_fp[ci])])
                vd = ew("vector", lambda e, cw=cw, p2=p2: e.tensor_sub(
                    WQD[:, 0:cw], WQV[p2][:, 0:cw], WQJ[:, 0:cw]))
                prev_vd = vd
                vc1 = ew("vector", lambda e, cw=cw: e.scalar_tensor_tensor(
                    WQF[:, 0:cw], WQD[:, 0:cw], 0.5, WQJ[:, 0:cw],
                    ALU.is_ge, ALU.add))
                vc2 = ew("vector", lambda e, cw=cw: e.tensor_scalar(
                    WQCB[:, 0:cw], WQD[:, 0:cw], -0.5, -1.0,
                    ALU.is_le, ALU.mult))
                vq = ew("vector", lambda e, cw=cw: e.tensor_add(
                    WQB[:, 0:cw], WQF[:, 0:cw], WQCB[:, 0:cw]),
                    waits=([(sT, prev_tt)] if prev_tt else []))
                wq_guard[ci] = vq
                ntile = cw // 128
                sw = [(sFS[ci % 2], ws_guard[ci - 2])] if ci >= 2 else []
                for g0 in range(0, ntile, 4):
                    gw_ = min(4, ntile - g0)
                    bkb = BKB.acq()
                    tiles = [(pbb[bkb][:, j * 128:(j + 1) * 128],
                              WQB[:, (g0 + j) * 128:(g0 + j + 1) * 128])
                             for j in range(gw_)]
                    tt = tr_group(tiles, IDB[:],
                                  waits=([(sV, vq)] + sw) if g0 == 0 else ())
                    evv = copy_pass("scalar",
                                    WQS[ci % 2][:, g0 * 128:(g0 + gw_) * 128],
                                    pbb[bkb][:, 0:gw_ * 128], waits=[(sT, tt)])
                    BKB.rel(bkb, sA, evv)
                    prev_tt = tt
                store(dstd[0:ntile, :, c * 128:(c + 1) * 128]
                      .rearrange("k p f -> p k f"),
                      WQS[ci % 2][:, 0:ntile * 128]
                      .rearrange("p (k f) -> p k f", k=ntile), sFS[ci % 2],
                      waits=[(sV, sV.n), (sA, sA.n)])
                ws_guard[ci] = sFS[ci % 2].n
            if cx < NCH:
                kind, c, w_in, cw, scol, dstd = fp_chunks[cx]
                sl4 = cx % 4
                lw = (list(ttt_join) if cx < 4 else
                      [(sV, wq_guard[cx - 4])])
                load(WQN[sl4][0:64, 0:cw], w_in[c * 128:c * 128 + 64, :],
                     sFW[sl4], waits=lw)
                ldn = load(WQN[sl4][64:128, 0:cw],
                           w_in[c * 128 + 64:(c + 1) * 128, :], sFW[sl4])
                vr = ew("vector", lambda e, s=WQN[sl4][:, 0:cw], cx=cx:
                        e.reduce_max(RMX[:, cx:cx + 1], s, axis=AX,
                                     apply_absolute_value=True),
                        waits=[(sFW[sl4], ldn)])
                wq_guard.setdefault(cx, vr)
                ew("scalar", lambda e, d=scol[:, c:c + 1], cx=cx:
                   e.activation(d, RMX[:, cx:cx + 1], AF.Copy,
                                scale=1.0 / 127.0), waits=[(sV, vr)])
        bk = BK.acq()
        tsd = tr_group([(pb[bk][0:8, 0:128], SDC[:, 0:8])], IDF[:],
                       waits=[(sA, sA.n), (sV, sV.n)])
        asd = copy_pass("scalar", FMISC[0:8, 0:128], pb[bk][0:8, 0:128],
                        waits=[(sT, tsd)])
        BK.rel(bk, sA, asd)
        stsd = store(sdrowd.rearrange("o (a f) -> (o a) f", a=8), FMISC[0:8, 0:128],
                     sMXS, waits=[(sA, asd)])

        qst = {"i": 0, "last_vq": [None, None]}

        def quant_slab(src_ap, dst_ap, nrcol, waits_src=()):
            i = qst["i"] % 2; qst["i"] += 1
            gw0 = [(sV, qst["last_vq"][i])] if qst["last_vq"][i] else []
            av = ew("scalar", lambda e: e.activation(QF[i][:], src_ap, AF.Relu,
                                                     bias=CB49[:, 0:1],
                                                     scale=nrcol),
                    waits=list(waits_src) + gw0)
            av2 = ew("scalar", lambda e: e.activation(QV[i][:], QF[i][:],
                                                      AF.Relu,
                                                      bias=CB49[:, 1:2],
                                                      scale=-1.0))
            vj = ew("vector", lambda e: e.tensor_scalar(
                QJ[i][:], QV[i][:], -128.49, None, ALU.add),
                waits=[(sA, av2)])
            vd = ew("vector", lambda e: e.scalar_tensor_tensor(
                QV[i][:], QV[i][:], -128.49, QJ[i][:], ALU.add, ALU.subtract))
            vc1 = ew("vector", lambda e: e.scalar_tensor_tensor(
                QF[i][:], QV[i][:], 0.5, QJ[i][:], ALU.is_ge, ALU.add))
            vc2 = ew("vector", lambda e: e.tensor_scalar(
                QC[i][:], QV[i][:], -0.5, -1.0, ALU.is_le, ALU.mult))
            vq = ew("vector", lambda e: e.tensor_add(dst_ap, QF[i][:], QC[i][:]))
            qst["last_vq"][i] = vq
            return vq

        es_fp.close()
        PTILE = sbf("PTILE", NIC * 512)
        es_f1 = ExitStack()
        IGT = es_f1.enter_context(nc.sbuf_tensor("IGT", [128, NKD * II], BF16))
        IUT = es_f1.enter_context(nc.sbuf_tensor("IUT", [128, NKD * II], BF16))
        H2T = [es_f1.enter_context(nc.sbuf_tensor(f"H2T{i}", [128, 4096], F32))
               for i in range(2)]
        IQF = [es_f1.enter_context(nc.sbuf_tensor(f"IQF{i}", [128, 4096], BF16))
               for i in range(2)]
        for kd in range(NKD):
            load(IGT[:, kd * II:(kd + 1) * II], igtd[kd, :, :], sIG,
                 waits=[(sFS[0], sFS[0].n), (sFS[1], sFS[1].n)] if kd == 0 else ())
            load(IUT[:, kd * II:(kd + 1) * II], iutd[kd, :, :], sIG)
        vIG = sIG.n
        vsxg = ew("vector", lambda e: e.tensor_scalar(
            SXG[:, 0:NIC], SGC[:, 0:NIC], SXC[:, 0:1], None, ALU.mult),
            waits=[(sA, ab1), (sV, sV.n)])
        vsxu = ew("vector", lambda e: e.tensor_scalar(
            SXU[:, 0:NIC], SUC[:, 0:NIC], SXC[:, 0:1], None, ALU.mult))

        h2l_guard = {}
        iqf_guard = {}
        ps_tile_done = {}
        for tau in range(C):
            sl_h = tau % 2
            hw_ = ([(sV, h2l_guard[tau - 2])] if tau >= 2 else list(ttt_join))
            ldh = None
            for kd in range(NKD):
                ldh = load(H2T[sl_h][:, kd * 512:(kd + 1) * 512],
                           h2d[kd, :, tau * S:(tau + 1) * S], sH2L[sl_h],
                           waits=hw_ if kd == 0 else ())
            vq = None
            for kd in range(NKD):
                wsrc = [(sH2L[sl_h], ldh)] if kd == 0 else ()
                wsrc = list(wsrc)
                if tau >= 2 and kd == 0:
                    wsrc.append((sT, iqf_guard[tau - 2]))
                vq = quant_slab(H2T[sl_h][:, kd * 512:(kd + 1) * 512],
                                IQF[tau % 2][:, kd * 512:(kd + 1) * 512],
                                SXC[:, 2:3], waits_src=wsrc)
            h2l_guard[tau] = vq
            pw_ = [(sPS, ps_tile_done[tau - 1])] if tau >= 1 else []
            stp = None
            prev_vp = None
            for m in range(NIC):
                bk = BK.acq()
                mms = [(IGT[:, kd * II + m * 128: kd * II + (m + 1) * 128],
                        IQF[tau % 2][:, kd * 512:(kd + 1) * 512])
                       for kd in range(NKD)]
                tg = mm_group(pb[bk][:], mms,
                              waits=[(sV, vq), (sIG, vIG)] if m == 0 else ())
                asil = ew("scalar", lambda e, s=pb[bk][:], m=m: e.activation(
                    SILG[:], s, AF.Silu, scale=SXG[:, m:m + 1]),
                    waits=([(sT, tg), (sV, vsxg)] if m == 0 else [(sT, tg)]) +
                    ([(sV, prev_vp)] if prev_vp else []))
                BK.rel(bk, sA, asil)
                bk2 = BK.acq()
                mms = [(IUT[:, kd * II + m * 128: kd * II + (m + 1) * 128],
                        IQF[tau % 2][:, kd * 512:(kd + 1) * 512])
                       for kd in range(NKD)]
                tu = mm_group(pb[bk2][:], mms)
                vp = ew("vector", lambda e, s=pb[bk2][:], m=m,
                        d=PTILE[:, m * 512:(m + 1) * 512]:
                        e.scalar_tensor_tensor(d, s, SXU[:, m:m + 1], SILG[:],
                                               ALU.mult, ALU.mult),
                        waits=([(sT, tu), (sA, asil), (sV, vsxu)] + pw_)
                        if m == 0 else [(sT, tu), (sA, asil)])
                BK.rel(bk2, sV, vp)
                prev_vp = vp
                vmx = ew("vector", lambda e, d=PTILE[:, m * 512:(m + 1) * 512],
                         c=64 + tau * NIC + m:
                         e.reduce_max(MAXH[:, c:c + 1], d, axis=AX,
                                      apply_absolute_value=True))
                stp = store(Pd[m, :, tau * S:(tau + 1) * S],
                            PTILE[:, m * 512:(m + 1) * 512], sPS,
                            waits=[(sV, vmx)])
            iqf_guard[tau] = sT.n
            ps_tile_done[tau] = sPS.n

        gm2, ldm2 = global_max((64, 240), mx2i, mx2o, 32, xw=[(sMXS, stsd)])
        ab2 = scale_pair(gm2, ldm2, 1, 3, 192)

        es_f1.close()
        IDT = sbf("IDT", NIC * 1024, BF16)
        HQ2 = [sbf("HQFA", NIC * 512, BF16), sbf("HQFB", NIC * 512, BF16)]
        X2L = [sbf("X2TA", 4096), sbf("X2TB", 4096)]
        CBL = [sbf("CBTA", 1024), sbf("CBTB", 1024)]
        BCL = [sbf("BCF0", 1024), sbf("BCF1", 1024)]
        OUT5 = [sbf(f"OUT5{i}", 512) for i in range(2)]
        T512F = sbf("T512F", 512)
        for ki in range(NIC):
            load(IDT[:, ki * 1024:(ki + 1) * 1024], idtd[ki, :, :], sIG,
                 waits=[(sT, sT.n)] if ki == 0 else ())
        ldsdb = load(SDB[:], sdrowd.rearrange("o d -> o d", o=1)
                     .to_broadcast([128, D]), sIG, waits=[(sMXS, stsd)])
        vIDT = sIG.n
        f2bar = [(sT, sT.n), (sV, sV.n), (sA, sA.n), (sIG, sIG.n),
                 (sH2L[0], sH2L[0].n), (sH2L[1], sH2L[1].n)]

        groupsF2 = [(tc, dg) for tc in range(NTC) for dg in range(2)]
        x2_guard = {}
        out_done = {}
        out5_guard = {}
        gstate = {}
        ldx2s = {}
        ldp_tot = {}
        vqs = {}
        EV_AT = (2, 5, 8, 11, 14, 17, 20)

        def f2_group(tau_, j, waits=()):
            tc, dg = groupsF2[j]
            bk = BK.acq()
            mms = [(HQ2[tau_ % 2][:, ki * 512 + tc * 128:
                                  ki * 512 + (tc + 1) * 128],
                    IDT[:, ki * 1024 + dg * 512:ki * 1024 + (dg + 1) * 512])
                   for ki in range(NIC)]
            tdn = mm_group(pb[bk][:], mms, waits=waits)
            return (bk, tdn)

        def f2_evac(tau_, j, bk, tdn):
            tc, dg = groupsF2[j]
            w1 = [(sT, tdn)] + ([(sX2L[tau_ % 2], ldx2s[tau_])]
                                if j == 0 else [])
            ew("vector", lambda e, s=pb[bk][:], dg=dg, p=tau_ % 2:
               e.tensor_mul(T512F[:], s, CBL[p][:, dg * 512:(dg + 1) * 512]),
               waits=w1)
            BK.rel(bk, sV, sV.n)
            sl = j % 2
            vo2 = ew("vector", lambda e, tc=tc, dg=dg, sl=sl, p=tau_ % 2:
                     e.tensor_add(OUT5[sl][:],
                                  X2L[p][:, tc * 1024 + dg * 512:
                                         tc * 1024 + (dg + 1) * 512],
                                  T512F[:]),
                     waits=([(sOUT[sl], out5_guard[sl])]
                            if sl in out5_guard else []))
            out5_guard[sl] = store(
                out_hbm[tau_][tc * 128:(tc + 1) * 128, dg * 512:(dg + 1) * 512],
                OUT5[sl][:], sOUT[sl], waits=[(sV, vo2)])

        def f2_consume(tau_, j):
            bk, tdn = gstate[tau_][j]
            f2_evac(tau_, j, bk, tdn)
            if j < 2:
                gstate[tau_].append(f2_group(tau_, 6 + j))

        ldp = None
        for m in range(NIC):
            ldp = load(PTILE[:, m * 512:(m + 1) * 512], Pd[m, :, 0:S], sPL,
                       waits=[(sPS, sPS.n), (sV, sV.n)] if m == 0 else ())
        ldp_tot[0] = ldp

        for tau in range(C):
            par = tau % 2
            xw = ([(sV, x2_guard[tau - 2])] if tau >= 2 else list(f2bar))
            ldx2s[tau] = load(
                X2L[par][:].rearrange("p (tc d) -> p tc d", tc=NTC),
                x2d[tau * S:(tau + 1) * S, :]
                .rearrange("(tc p) d -> p tc d", p=128), sX2L[par],
                waits=(xw + [(sX2, sX2.n)]) if tau == 0 else xw)
            ldgb = load(BCL[par][:], rowsd[3, tau, :]
                        .rearrange("(o d) -> o d", o=1)
                        .to_broadcast([128, D]), sBC, waits=list(xw))
            ew("vector", lambda e, par=par: e.scalar_tensor_tensor(
                CBL[par][:], BCL[par][:], SXC[:, 3:4], SDB[:],
                ALU.mult, ALU.mult),
                waits=[(sBC, ldgb), (sIG, vIDT), (sA, ab2)])
            vql = []
            for m in range(NIC):
                wsrc = []
                if m == 0:
                    wsrc = [(sPL, ldp_tot[tau])] + (
                        [(sT, out_done[tau - 2])] if tau >= 2 else list(f2bar))
                vq = quant_slab(PTILE[:, m * 512:(m + 1) * 512],
                                HQ2[par][:, m * 512:(m + 1) * 512],
                                SXC[:, 5:6], waits_src=wsrc)
                vql.append(vq)
                if tau >= 1 and m in EV_AT:
                    f2_consume(tau - 1, EV_AT.index(m))
            if tau >= 1:
                f2_consume(tau - 1, 7)
                out_done[tau - 1] = sT.n
                x2_guard[tau - 1] = sV.n
            vqs[tau] = vql
            if tau + 1 < C:
                ldp = None
                for m in range(NIC):
                    ldp = load(PTILE[:, m * 512:(m + 1) * 512],
                               Pd[m, :, (tau + 1) * S:(tau + 2) * S], sPL,
                               waits=[(sV, vql[m])])
                ldp_tot[tau + 1] = ldp
            gstate[tau] = [f2_group(tau, j,
                                    waits=[(sV, vql[-1])] if j == 0 else ())
                           for j in range(6)]
        for j in range(8):
            f2_consume(C - 1, j)
    es_ffn.close()

    with nc.Block() as blk:
        @blk.sync
        def _(e):
            for f in P.q["sync"]:
                f(e)

        @blk.vector
        def _(e):
            for f in P.q["vector"]:
                f(e)

        @blk.scalar
        def _(e):
            for f in P.q["scalar"]:
                f(e)

        @blk.tensor
        def _(e):
            for f in P.q["tensor"]:
                f(e)

        @blk.gpsimd
        def _(e):
            for f in P.q["gpsimd"]:
                f(e)
            for s_ in ALL_DMA_SEMS:
                if s_.n:
                    e.wait_ge(s_.h, s_.n)

    es.close()
    return nc, dbg_names


def make_inputs_per_core(inputs, core):
    x = np.ascontiguousarray(inputs["L0_attn_in_seq"][:, core * S:(core + 1) * S, :])
    eye = np.eye(128, dtype=np.float32)
    return {
        "x": x,
        "tk": np.asarray(inputs["L0_attn_t_k"]),
        "tv": np.asarray(inputs["L0_attn_t_v"]),
        "tq": np.asarray(inputs["L0_attn_t_q"]),
        "W1": np.asarray(inputs["L0_attn_W1"]), "b1": np.asarray(inputs["L0_attn_b1"]),
        "W2": np.asarray(inputs["L0_attn_W2"]), "b2": np.asarray(inputs["L0_attn_b2"]),
        "clip_x": np.asarray(inputs["L0_ffn_clip_x"]),
        "clip_down": np.asarray(inputs["L0_ffn_clip_down"]),
        "wg": np.asarray(inputs["L0_ffn_w_gate"]),
        "wu": np.asarray(inputs["L0_ffn_w_up"]),
        "wd": np.asarray(inputs["L0_ffn_w_down"]),
        "cvec": np.asarray(inputs["L0_c"]), "wmod": np.asarray(inputs["L0_wmod"]),
        "identf": eye,
        "identb": eye.astype(ml_dtypes.bfloat16),
    }


def kernel(**inputs):
    from concourse.bass_utils import run_bass_kernel_spmd
    nc, _ = build_nc(n_steps=8, do_ffn=True, dbg=())
    in_maps = [make_inputs_per_core(inputs, c) for c in range(8)]
    res = run_bass_kernel_spmd(nc, in_maps, core_ids=list(range(8)))
    out = np.empty((C, 4096, D), dtype=np.float32)
    for c in range(8):
        out[:, c * S:(c + 1) * S, :] = np.asarray(res.results[c]["out"])
    return out


# version 19
# speedup vs baseline: 1.0071x; 1.0036x over previous
import numpy as np
import ml_dtypes
import concourse.bass as bass
import concourse.mybir as mybir
from contextlib import ExitStack

F32 = mybir.dt.float32
BF16 = mybir.dt.bfloat16
I8 = mybir.dt.int8
AF = mybir.ActivationFunctionType
ALU = mybir.AluOpType
AX = mybir.AxisListType.X

S, C, D, HH, II = 512, 8, 1024, 256, 2816
R = C * S
NKD, NTC, NIC, NHC = 8, 4, 22, 2
LR = 1e-3
COEF = 2.0 / (4096.0 * 1024.0)
EPS = 1e-5
INV_SQRT_2PI = 0.3989422804014327
TINY = 2.0 ** -40
ENGINES = ("sync", "vector", "scalar", "tensor", "gpsimd")
V_SA, V_CA1, V_GA, V_SB, V_CB1, V_GB = range(6)


class Sem:
    def __init__(self, es, nc, name):
        self.h = es.enter_context(nc.semaphore(name))
        self.n = 0


class Prog:
    def __init__(self):
        self.q = {e: [] for e in ENGINES}

    def op(self, eng, fn, waits=(), inc=None, inc_n=1):
        if inc is not None:
            inc.n += inc_n
        def run(e, fn=fn, waits=tuple(waits), inc=inc, inc_n=inc_n):
            for s, v in waits:
                e.wait_ge(s.h, v)
            ins = fn(e)
            if inc is not None:
                ins.then_inc(inc.h, inc_n)
        self.q[eng].append(run)
        return inc.n if inc is not None else None

    def wait(self, eng, sem, value=None):
        v = sem.n if value is None else value
        self.q[eng].append(lambda e, s=sem, v=v: e.wait_ge(s.h, v))


def build_nc(n_steps=8, do_ffn=True, dbg=()):
    dbg = set(dbg)
    nc = bass.Bass("TRN2", target_bir_lowering=False)

    def din(name, shape, dt=F32):
        return nc.dram_tensor(name, shape, dt, kind="ExternalInput").ap()

    x_in = din("x", [C, S, D])
    tw_in = [din(n, [D, D]) for n in ("tk", "tv", "tq")]
    w1_in = din("W1", [HH, D]); b1_in = din("b1", [HH])
    w2_in = din("W2", [D, HH]); b2_in = din("b2", [D])
    clipx_in = din("clip_x", [1]); clipd_in = din("clip_down", [1])
    wg_in = din("wg", [II, D]); wu_in = din("wu", [II, D]); wd_in = din("wd", [D, II])
    c_in = din("cvec", [C, D]); wmod_in = din("wmod", [D, 6 * D])
    idf_in = din("identf", [128, 128]); idb_in = din("identb", [128, 128], BF16)
    out_hbm = nc.dram_tensor("out", [C, S, D], F32, kind="ExternalOutput").ap()

    def dint(name, shape, dt=F32):
        return nc.dram_tensor(name, shape, dt).ap()

    x2d = dint("x2d", [R, D])
    h2d = dint("h2d", [NKD, 128, R])
    Pd = dint("Pd", [NIC, 128, R])
    pwd = dint("pwd", [3, NKD, 2, 128, D], BF16)
    igtd = dint("igtd", [NKD, 128, II], BF16)
    iutd = dint("iutd", [NKD, 128, II], BF16)
    idtd = dint("idtd", [NIC, 128, D], BF16)
    rowsd = dint("rowsd", [4, C, D])
    sdrowd = dint("sdrowd", [1, D])
    gin_d = dint("gin", [128, 4106]); gout_d = dint("gout", [128, 4106])
    mx1i = dint("mx1i", [1, 8]); mx1o = dint("mx1o", [1, 8])
    mx2i = dint("mx2i", [1, 8]); mx2o = dint("mx2o", [1, 8])
    RG = [list(range(8))]

    dbg_names = []
    es = ExitStack()

    def sb(name, cols, dt=F32):
        return es.enter_context(nc.sbuf_tensor(name, [128, cols], dt))

    pb = [es.enter_context(nc.psum_tensor(f"pb{i}", [128, 512], F32)) for i in range(6)]
    pbb = [es.enter_context(nc.psum_tensor(f"pbb{i}", [128, 1024], BF16)) for i in range(2)]

    sST = Sem(es, nc, "sST")
    sT = Sem(es, nc, "sT"); sV = Sem(es, nc, "sV"); sA = Sem(es, nc, "sA")
    sG = Sem(es, nc, "sG"); sCC = Sem(es, nc, "sCC")
    sINIT = Sem(es, nc, "sINIT")
    sWM = [Sem(es, nc, "sWM0"), Sem(es, nc, "sWM1")]
    sWP = [Sem(es, nc, "sWP0"), Sem(es, nc, "sWP1")]
    sPW = [Sem(es, nc, "sPW0"), Sem(es, nc, "sPW1")]
    sX = Sem(es, nc, "sX")
    sBC = Sem(es, nc, "sBC")
    sGO = [Sem(es, nc, f"sGO{i}") for i in range(4)]
    sRW = Sem(es, nc, "sRW")
    sPWD = Sem(es, nc, "sPWD")
    sX2 = Sem(es, nc, "sX2")
    sH2S = [Sem(es, nc, "sH2S0"), Sem(es, nc, "sH2S1")]
    sGS = [Sem(es, nc, "sGS0"), Sem(es, nc, "sGS1")]
    ALL_DMA_SEMS = [sST, sINIT, sWM[0], sWM[1], sWP[0], sWP[1],
                    sPW[0], sPW[1], sX, sBC,
                    sGO[0], sGO[1], sGO[2], sGO[3],
                    sRW, sPWD, sX2, sH2S[0], sH2S[1],
                    sGS[0], sGS[1]]
    ESEM = {"vector": sV, "scalar": sA, "gpsimd": sG, "tensor": sT}

    P = Prog()

    IDF = sb("IDF", 128)
    IDB = sb("IDB", 128, BF16)
    ONESROW = sb("ONESROW", 128)
    CB49 = sb("CB49", 2)
    CLIPS = sb("CLIPS", 2)
    MAXH = sb("MAXH", 256)
    SMALL = sb("SMALL", 32)
    es_ttt = ExitStack()

    def sbt(name, cols, dt=F32):
        return es_ttt.enter_context(nc.sbuf_tensor(name, [128, cols], dt))

    MODF = sbt("MODF", 384)
    W1T_m = sbt("W1T_m", 2048); W2_m = sbt("W2_m", 2048)
    B1C = sbt("B1C", 2); B2C = sbt("B2C", 8)
    W1TH = sbt("W1TH", 2048, BF16); W1TL = sbt("W1TL", 2048, BF16)
    W2H = sbt("W2H", 2048, BF16); W2L = sbt("W2L", 2048, BF16)
    W2TH = sbt("W2TH", 2048, BF16); W2TL = sbt("W2TL", 2048, BF16)
    PW = [sbt(f"PW{i}", 2048, BF16) for i in range(2)]
    XT = sbt("XT", 4096)
    STAT = sbt("STAT", 64)
    TMPA = sbt("TMPA", 4096)
    T512 = sbt("T512", 512)
    H1H = sbt("H1H", 4096, BF16); H1L = sbt("H1L", 4096, BF16)
    TKN = [sbt(f"TKN{i}", 1024) for i in range(2)]
    TKH = sbt("TKH", 1024, BF16)
    TKL = sbt("TKL", 1024, BF16)
    PWS = sbt("PWS", 2048, BF16)
    GBIA = sbt("GBIA", 16)
    GSTG = [sbt(f"GSTG{i}", 512) for i in range(2)]
    GOUT = [sbt(f"GOUT{i}", 512) for i in range(4)]
    H2S = [sbt(f"H2S{i}", 512) for i in range(2)]
    ZTS = [sbt(f"ZTS{i}", 1024, BF16) for i in range(2)]

    def modcol(vec, kd, b):
        i = vec * 64 + b * 8 + kd
        return MODF[:, i:i + 1]

    def modset(vec, b):
        i = vec * 64 + b * 8
        return MODF[:, i:i + 8]

    def load(out_ap, in_ap, sem, eng="sync", waits=(), slow=False):
        if slow:
            return P.op(eng, lambda e, o=out_ap, i=in_ap: e.dma_start(
                out=o, in_=i, allow_slow_non_contiguous=True),
                waits=waits, inc=sem, inc_n=16)
        return P.op(eng, lambda e, o=out_ap, i=in_ap: e.dma_start(out=o, in_=i),
                    waits=waits, inc=sem, inc_n=16)

    def store(out_ap, in_ap, sem, eng="gpsimd", waits=()):
        return P.op(eng, lambda e, o=out_ap, i=in_ap: e.dma_start(out=o, in_=i),
                    waits=waits, inc=sem, inc_n=16)

    def ew(eng, fn, waits=()):
        sw = [(ESEM[eng], ESEM[eng].n)]
        return P.op(eng, fn, waits=list(waits) + sw, inc=ESEM[eng])

    def dump(name, ap_, waits):
        if name not in dbg:
            return
        t = nc.dram_tensor("dbg_" + name, list(ap_.shape), ap_.dtype,
                           kind="ExternalOutput").ap()
        dbg_names.append(name)
        P.op("gpsimd", lambda e, o=t, i=ap_: e.dma_start(out=o, in_=i),
             waits=waits, inc=sST, inc_n=16)

    class Banks:
        def __init__(self, n):
            self.n = n
            self.guard = [None] * n
            self.rr = 0

        def acq(self, pick=None):
            i = self.rr if pick is None else pick
            if pick is None:
                self.rr = (self.rr + 1) % self.n
            if self.guard[i] is not None:
                s, v = self.guard[i]
                P.wait("tensor", s, v)
                self.guard[i] = None
            return i

        def rel(self, i, sem, count):
            self.guard[i] = (sem, count)

    BK = Banks(6)
    BKB = Banks(2)

    def mm_group(bank_ap, mms, waits=()):
        n = len(mms)
        for j, (lt, rh) in enumerate(mms):
            P.op("tensor",
                 lambda e, o=bank_ap, lt=lt, rh=rh, st=(j == 0), sp=(j == n - 1):
                 e.matmul(o, lt, rh, start=st, stop=sp),
                 waits=waits if j == 0 else (),
                 inc=sT if j == n - 1 else None)
        return sT.n

    def tr_group(pairs, ident, waits=()):
        n = len(pairs)
        for j, (o, i) in enumerate(pairs):
            P.op("tensor", lambda e, o=o, i=i, idn=ident: e.transpose(o, i, idn),
                 waits=waits if j == 0 else (),
                 inc=sT if j == n - 1 else None)
        return sT.n

    def copy_pass(eng, dst, src, waits=()):
        if eng == "scalar":
            return ew(eng, lambda e, d=dst, s=src: e.activation(d, s, AF.Copy),
                      waits=waits)
        return ew(eng, lambda e, d=dst, s=src: e.tensor_copy(d, s), waits=waits)

    def split_hi_lo(src_ap, hi_ap, lo_ap, waits_hi=(), waits_lo=()):
        a = copy_pass("scalar", hi_ap, src_ap, waits=waits_hi)
        v = ew("vector", lambda e, d=lo_ap, s=src_ap, h=hi_ap: e.tensor_sub(d, s, h),
               waits=[(sA, a)] + list(waits_lo))
        return a, v

    es_pro = ExitStack()

    def sbp(name, cols, dt=F32):
        return es_pro.enter_context(nc.sbuf_tensor(name, [128, cols], dt))

    load(IDF[:], idf_in, sINIT)
    load(IDB[:], idb_in, sINIT)
    ew("vector", lambda e: e.memset(ONESROW[0:1, :], 1.0))
    ew("vector", lambda e: e.memset(CB49[:, 0:1], 127.49))
    ew("vector", lambda e: e.memset(CB49[:, 1:2], 255.98))
    ew("vector", lambda e: e.memset(MAXH[:], 0.0))

    W1N = sbp("W1N", 2048)
    CT = sbp("CT", 1024)
    SILCT = sbp("SILCT", 64)
    load(W1N[:].rearrange("p (c d) -> p c d", c=2),
         w1_in.rearrange("(c p) d -> p c d", p=128), sINIT)
    load(W2_m[:].rearrange("p (c h) -> p c h", c=8),
         w2_in.rearrange("(c p) h -> p c h", p=128), sINIT)
    load(CLIPS[0:1, 0:1], clipx_in.rearrange("(o x) -> o x", o=1), sINIT)
    load(CLIPS[0:1, 1:2], clipd_in.rearrange("(o x) -> o x", o=1), sINIT)
    load(B1C[:], b1_in.rearrange("(c p) -> p c", p=128), sINIT, slow=True)
    load(B2C[:], b2_in.rearrange("(c p) -> p c", p=128), sINIT, slow=True)
    load(CT[0:8, :], c_in, sINIT)
    vINIT = sINIT.n

    for g in range(4):
        bk = BK.acq()
        tiles = []
        for j in range(4):
            hc, kd = divmod(g * 4 + j, 8)
            tiles.append((pb[bk][:, j * 128:(j + 1) * 128],
                          W1N[:, hc * 1024 + kd * 128: hc * 1024 + (kd + 1) * 128]))
        tt = tr_group(tiles, IDF[:], waits=[(sINIT, vINIT)])
        vv = None
        for j in range(4):
            hc, kd = divmod(g * 4 + j, 8)
            dst = W1T_m[:, kd * 256 + hc * 128: kd * 256 + (hc + 1) * 128]
            vv = copy_pass("vector", dst, pb[bk][:, j * 128:(j + 1) * 128],
                           waits=[(sT, tt)] if j == 0 else ())
        BK.rel(bk, sV, vv)
    v_w1t = sV.n

    def resplit_params(waits_w1, waits_w2):
        split_hi_lo(W1T_m[:], W1TH[:], W1TL[:], waits_hi=waits_w1)
        split_hi_lo(W2_m[:], W2H[:], W2L[:], waits_hi=waits_w2)
        for g in range(4):
            bk = BK.acq()
            tiles = []
            for j in range(4):
                c2, hc = divmod(g * 4 + j, 2)
                tiles.append((pb[bk][:, j * 128:(j + 1) * 128],
                              W2_m[:, c2 * 256 + hc * 128: c2 * 256 + (hc + 1) * 128]))
            tt = tr_group(tiles, IDF[:], waits=waits_w2 if g == 0 else ())
            vv = None
            for j in range(4):
                c2, hc = divmod(g * 4 + j, 2)
                sl = slice(hc * 1024 + c2 * 128, hc * 1024 + (c2 + 1) * 128)
                src = pb[bk][:, j * 128:(j + 1) * 128]
                ah = copy_pass("scalar", W2TH[:, sl], src,
                               waits=[(sT, tt)] if j == 0 else ())
                vv = ew("vector", lambda e, d=W2TL[:, sl], s=src, h=W2TH[:, sl]:
                        e.tensor_sub(d, s, h), waits=[(sA, ah)])
            BK.rel(bk, sV, vv)
        return sA.n, sV.n

    resplit_params([(sV, v_w1t)], [(sINIT, vINIT)])

    aC = ew("scalar", lambda e: e.activation(CT[0:8, :], CT[0:8, :], AF.Silu),
            waits=[(sINIT, vINIT)])
    bk = BK.acq()
    tS = tr_group([(pb[bk][:, kd * 8:(kd + 1) * 8],
                    CT[0:8, kd * 128:(kd + 1) * 128]) for kd in range(NKD)],
                  IDF[0:8, 0:8], waits=[(sA, aC), (sINIT, vINIT)])
    vS = copy_pass("vector", SILCT[:], pb[bk][:, 0:64], waits=[(sT, tS)])
    BK.rel(bk, sV, vS)

    WMC = [sbp(f"WMC{i}", 4096) for i in range(2)]
    MODT = sbp("MODT", 6144)
    wm_guard = {}
    vMT = None
    for msl in range(12):
        waits = [(sT, wm_guard[msl - 2])] if msl >= 2 else ()
        ldw = load(WMC[msl % 2][:].rearrange("p (kd n) -> p kd n", kd=NKD),
                   wmod_in[:, msl * 512:(msl + 1) * 512]
                   .rearrange("(kd p) n -> p kd n", p=128), sWM[msl % 2],
                   waits=waits)
        bk = BK.acq()
        mms = [(SILCT[:, kd * 8:(kd + 1) * 8],
                WMC[msl % 2][:, kd * 512:(kd + 1) * 512]) for kd in range(NKD)]
        tmm = mm_group(pb[bk][0:8, :], mms, waits=[(sWM[msl % 2], ldw), (sV, vS)])
        wm_guard[msl] = tmm
        vMT = copy_pass("vector", MODT[0:8, msl * 512:(msl + 1) * 512],
                        pb[bk][0:8, :], waits=[(sT, tmm)])
        BK.rel(bk, sV, vMT)
    vMOD = None
    for j in range(48):
        vec, kd = divmod(j, 8)
        bk = BK.acq()
        tt = tr_group([(pb[bk][:, 0:8], MODT[0:8, j * 128:(j + 1) * 128])],
                      IDF[0:8, 0:8], waits=[(sV, vMT)] if j == 0 else ())
        dst = MODF[:, vec * 64:(vec + 1) * 64].rearrange(
            "p (b k) -> p b k", b=8, k=8)[:, :, kd:kd + 1]
        vMOD = ew("vector", lambda e, d=dst, s=pb[bk][:, 0:8].unsqueeze(2):
                  e.tensor_copy(d, s), waits=[(sT, tt)])
        BK.rel(bk, sV, vMOD)
    dump("modf", MODF[:], [(sV, vMOD)])

    for vec in (V_CA1, V_CB1):
        sl = slice(vec * 64, (vec + 1) * 64)
        vMOD = ew("vector", lambda e, sl=sl: e.tensor_scalar(
            MODF[:, sl], MODF[:, sl], 1.0, None, ALU.add), waits=[(sV, vMOD)])
    GVECS = (V_GA, V_GB)
    def gsl(vec):
        return slice(vec * 64, (vec + 1) * 64)
    for i, vec in enumerate(GVECS):
        vMOD = ew("vector", lambda e, sl=gsl(vec), t=CT[:, i * 256:i * 256 + 64]:
                  e.tensor_scalar(t, MODF[:, sl], 0.0, 2.0, ALU.is_ge, ALU.mult),
                  waits=[(sV, vMOD)] if i == 0 else ())
    for i, vec in enumerate(GVECS):
        vMOD = ew("vector", lambda e, t=CT[:, i * 256:i * 256 + 64]:
                  e.tensor_scalar(t, t, -1.0, None, ALU.add))
    for i, vec in enumerate(GVECS):
        aM = ew("scalar", lambda e, sl=gsl(vec), t=CT[:, i * 256 + 64:i * 256 + 128]:
                e.activation(t, MODF[:, sl], AF.Abs),
                waits=[(sV, vMOD)] if i == 0 else ())
    for i, vec in enumerate(GVECS):
        vMOD = ew("vector", lambda e, t=CT[:, i * 256 + 64:i * 256 + 128]:
                  e.tensor_scalar(t, t, 0.25, None, ALU.max),
                  waits=[(sA, aM)] if i == 0 else ())
    for i, vec in enumerate(GVECS):
        vMOD = ew("vector", lambda e, sl=gsl(vec), t=CT[:, i * 256:i * 256 + 64],
                  t2=CT[:, i * 256 + 64:i * 256 + 128]:
                  e.tensor_mul(MODF[:, sl], t2, t))
    vMODF = vMOD
    dump("modp", MODF[:], [(sV, vMODF)])

    ROWS = sbp("ROWS", 128)
    for ri, vec in enumerate((V_GA, V_CB1, V_SB, V_GB)):
        for b in range(C):
            bk = BK.acq()
            tt = tr_group([(pb[bk][0:8, 0:128], modset(vec, b))], IDF[:],
                          waits=[(sV, vMODF)])
            vv = copy_pass("vector", ROWS[0:8, :], pb[bk][0:8, 0:128],
                           waits=[(sT, tt), (sRW, sRW.n)])
            BK.rel(bk, sV, vv)
            store(rowsd[ri, b, :].rearrange("(a f) -> a f", a=8), ROWS[0:8, :],
                  sRW, waits=[(sV, vv)])
    stROWS = sRW.n

    st_pws_prev = None
    pwst = {}
    prev_split = {}
    for w in range(3):
        for mh in range(NKD):
            i2 = (w * 8 + mh) % 2
            gate = ([(sA, prev_split[i2][0]), (sV, prev_split[i2][1])]
                    if i2 in prev_split else [])
            load(TKN[i2][0:64, :], tw_in[w][mh * 128:mh * 128 + 64, :],
                 sWP[i2], waits=gate)
            ld = load(TKN[i2][64:128, :],
                      tw_in[w][mh * 128 + 64:(mh + 1) * 128, :], sWP[i2])
            wts = [(sPWD, st_pws_prev)] if st_pws_prev else []
            ah, vl = split_hi_lo(TKN[i2][:], TKH[:], TKL[:],
                                 waits_hi=[(sWP[i2], ld)] + wts)
            prev_split[i2] = (ah, vl)
            lastev = {"scalar": None, "vector": None}
            for hl, srcw in ((0, TKH), (1, TKL)):
                bk = BKB.acq()
                tiles = [(pbb[bk][:, j * 128:(j + 1) * 128],
                          srcw[:, j * 128:(j + 1) * 128]) for j in range(8)]
                wt = [(sA, ah), (sV, vl)] if hl == 0 else ()
                tt = tr_group(tiles, IDB[:], waits=wt)
                dst = PWS[:, hl * 1024:(hl + 1) * 1024]
                eng = "scalar" if hl == 0 else "vector"
                vv = copy_pass(eng, dst, pbb[bk][:, 0:1024], waits=[(sT, tt)])
                lastev[eng] = vv
                BKB.rel(bk, ESEM[eng], vv)
            wts2 = [(ESEM[k], v) for k, v in lastev.items() if v is not None]
            store(pwd[w, mh, 0, :, :], PWS[:, 0:1024], sPWD, waits=wts2)
            st_pws_prev = store(pwd[w, mh, 1, :, :], PWS[:, 1024:2048], sPWD)
            pwst[(w, mh)] = st_pws_prev
    stPW = sPWD.n

    es_pro.close()
    ZH = sbt("ZH", 4096, BF16); ZL = sbt("ZL", 4096, BF16)
    BCA = sbt("BCA", 1024)
    QH = sbt("QH", 4096, BF16); QL = sbt("QL", 4096, BF16)
    AFB = sbt("AFB", 1024)
    HFF = sbt("HFF", 1024)
    T1K = sbt("T1K", 1024); T2K = sbt("T2K", 1024)
    HH_ = sbt("HH_", 1024, BF16); HL_ = sbt("HL_", 1024, BF16)
    HTH = sbt("HTH", 1024, BF16); HTL = sbt("HTL", 1024, BF16)
    GAH = sbt("GAH", 1024, BF16); GAL = sbt("GAL", 1024, BF16)
    GATH = sbt("GATH", 1024, BF16); GATL = sbt("GATL", 1024, BF16)

    def load_x(t, barrier=False, extra=()):
        w = (([(s, s.n) for s in (sV, sA, sT, sG, sINIT, sWM[0], sWM[1],
                                  sWP[0], sWP[1], sPWD, sRW)])
             if barrier else []) + list(extra)
        return load(XT[:].rearrange("p (tc d) -> p tc d", tc=NTC),
                    x_in[t].rearrange("(tc p) d -> p tc d", p=128), sX, waits=w)

    def stat_a(tc):
        return STAT[:, tc * 16 + 14: tc * 16 + 15]

    def stat_b(tc):
        return STAT[:, tc * 16 + 15: tc * 16 + 16]

    def emit_ln(src, waits, sqtmp=None):
        sq = sqtmp if sqtmp is not None else T1K
        v = None
        for tc in range(NTC):
            b = tc * 16
            xs = src[:, tc * 1024:(tc + 1) * 1024]
            v = ew("vector", lambda e, b=b, xs=xs: e.reduce_sum(
                STAT[:, b:b + 1], xs, axis=AX),
                waits=waits if tc == 0 else ())
            a = ew("scalar", lambda e, b=b, xs=xs: e.activation(
                sq[:], xs, AF.Square, accum_out=STAT[:, b + 1:b + 2]),
                waits=waits if tc == 0 else ())
        vS1, aS1 = sV.n, sA.n
        def stage(fn, wt=()):
            vv = None
            for tc in range(NTC):
                vv = ew("vector", lambda e, b=tc * 16: fn(e, b),
                        waits=wt if tc == 0 else ())
            return vv
        stage(lambda e, b: e.tensor_scalar(
            STAT[:, b + 2:b + 3], STAT[:, b:b + 1], 1.0 / D, None, ALU.mult),
            wt=[(sV, vS1), (sA, aS1)])
        stage(lambda e, b: e.tensor_scalar(
            STAT[:, b + 3:b + 4], STAT[:, b + 1:b + 2], 1.0 / D, None, ALU.mult))
        stage(lambda e, b: e.scalar_tensor_tensor(
            STAT[:, b + 4:b + 5], STAT[:, b + 2:b + 3], 1.0,
            STAT[:, b + 2:b + 3], ALU.mult, ALU.mult))
        stage(lambda e, b: e.tensor_sub(
            STAT[:, b + 3:b + 4], STAT[:, b + 3:b + 4], STAT[:, b + 4:b + 5]))
        v = stage(lambda e, b: e.tensor_scalar(
            STAT[:, b + 3:b + 4], STAT[:, b + 3:b + 4], EPS, None, ALU.add))
        aq = None
        for tc in range(NTC):
            aq = ew("scalar", lambda e, b=tc * 16: e.activation(
                STAT[:, b + 5:b + 6], STAT[:, b + 3:b + 4], AF.Sqrt),
                waits=[(sV, v)] if tc == 0 else ())
        stage(lambda e, b: e.reciprocal(
            STAT[:, b + 6:b + 7], STAT[:, b + 5:b + 6]), wt=[(sA, aq)])
        stage(lambda e, b: e.scalar_tensor_tensor(
            STAT[:, b + 7:b + 8], STAT[:, b + 6:b + 7], 1.0,
            STAT[:, b + 6:b + 7], ALU.mult, ALU.mult))
        stage(lambda e, b: e.tensor_mul(
            STAT[:, b + 8:b + 9], STAT[:, b + 7:b + 8], STAT[:, b + 3:b + 4]))
        stage(lambda e, b: e.tensor_scalar(
            STAT[:, b + 9:b + 10], STAT[:, b + 8:b + 9], -0.5, 1.5,
            ALU.mult, ALU.add))
        stage(lambda e, b: e.tensor_mul(
            STAT[:, b + 14:b + 15], STAT[:, b + 6:b + 7], STAT[:, b + 9:b + 10]))
        v = stage(lambda e, b: e.scalar_tensor_tensor(
            STAT[:, b + 15:b + 16], STAT[:, b + 2:b + 3], -1.0,
            STAT[:, b + 14:b + 15], ALU.mult, ALU.mult))
        return v
        return v

    def emit_h1(t, waits_x, vln):
        vu = None
        for tc in range(NTC):
            vu = ew("vector", lambda e, tc=tc: e.tensor_scalar(
                TMPA[:, tc * 1024:(tc + 1) * 1024], XT[:, tc * 1024:(tc + 1) * 1024],
                stat_a(tc), stat_b(tc), ALU.mult, ALU.add),
                waits=([(sV, vln)] + list(waits_x)) if tc == 0 else ())
        dump(f"u_dbg_{t}", TMPA[:], [(sV, vu)])
        dump(f"stat_dbg_{t}", STAT[:], [(sV, vu)])
        vh = None
        tlast = None
        for kd in range(NKD):
            bk = BK.acq()
            tiles = [(pb[bk][:, tc * 128:(tc + 1) * 128],
                      TMPA[:, tc * 1024 + kd * 128: tc * 1024 + (kd + 1) * 128])
                     for tc in range(NTC)]
            tlast = tr_group(tiles, IDF[:], waits=[(sV, vu)] if kd == 0 else ())
            ca1 = modcol(V_CA1, kd, t); sa_ = modcol(V_SA, kd, t)
            dsth = H1H[:, kd * 512:(kd + 1) * 512]
            dstl = H1L[:, kd * 512:(kd + 1) * 512]
            vh = ew("vector", lambda e, d=dsth, s=pb[bk][:], a=ca1, b=sa_:
                    e.tensor_scalar(d, s, a, b, ALU.mult, ALU.add),
                    waits=[(sT, tlast)])
            vh = ew("vector", lambda e, s=pb[bk][:], a=ca1, b=sa_:
                    e.tensor_scalar(T512[:], s, a, b, ALU.mult, ALU.add))
            vh = ew("vector", lambda e, d=dstl, h=dsth:
                    e.tensor_sub(d, T512[:], h))
            BK.rel(bk, sV, vh)
        return vh, sT.n

    pw_state = {"k": 0, "guards": {}}

    def pw_issue(w, mh):
        k = pw_state["k"]; pw_state["k"] += 1
        waits = [(sPWD, pwst[(w, mh)])]
        if k >= 2:
            waits.append((sT, pw_state["guards"][k - 2]))
        sl = PW[k % 2]
        load(sl[:, 0:1024], pwd[w, mh, 0, :, :], sPW[k % 2], waits=waits)
        ldv = load(sl[:, 1024:2048], pwd[w, mh, 1, :, :], sPW[k % 2])
        return k, sl, ldv

    def emit_proj(t, which, vh1, y_extra_waits=()):
        dst = {0: (ZH, ZL), 2: (QH, QL)}
        for w in which:
            for mh in range(NKD):
                k, sl, ldv = pw_issue(w, mh)
                bk = BK.acq()
                mms = []
                for kd in range(NKD):
                    lt_h = sl[:, kd * 128:(kd + 1) * 128]
                    lt_l = sl[:, 1024 + kd * 128: 1024 + (kd + 1) * 128]
                    rh_h = H1H[:, kd * 512:(kd + 1) * 512]
                    rh_l = H1L[:, kd * 512:(kd + 1) * 512]
                    mms += [(lt_h, rh_h), (lt_h, rh_l), (lt_l, rh_h)]
                tmm = mm_group(pb[bk][:], mms, waits=[(sPW[k % 2], ldv), (sV, vh1)])
                pw_state["guards"][k] = tmm
                if w == 1:
                    vv = copy_pass("vector", TMPA[:, mh * 512:(mh + 1) * 512],
                                   pb[bk][:],
                                   waits=([(sT, tmm)] + list(y_extra_waits))
                                   if mh == 0 else [(sT, tmm)])
                else:
                    dh, dl = dst[w]
                    oh = dh[:, mh * 512:(mh + 1) * 512]
                    ol = dl[:, mh * 512:(mh + 1) * 512]
                    ah = copy_pass("scalar", oh, pb[bk][:], waits=[(sT, tmm)])
                    vv = ew("vector", lambda e, d=ol, s=pb[bk][:], h=oh:
                            e.tensor_sub(d, s, h), waits=[(sA, ah)])
                BK.rel(bk, sV, vv)
        return sV.n, sA.n

    def emit_tm(srcs, dsts, nmh, waits0):
        rowlen = nmh * 128
        for hl in range(2):
            src, dstt = srcs[hl], dsts[hl]
            for tc in range(NTC):
                ngroups = max(1, nmh // 4)
                for g in range(ngroups):
                    gw = min(4, nmh)
                    bk = BKB.acq()
                    tiles = [(pbb[bk][:, j * 128:(j + 1) * 128],
                              src[:, (g * 4 + j) * 512 + tc * 128:
                                  (g * 4 + j) * 512 + (tc + 1) * 128])
                             for j in range(gw)]
                    tt = tr_group(tiles, IDB[:],
                                  waits=waits0 if (hl == 0 and tc == 0 and g == 0) else ())
                    dstv = dstt[:, tc * rowlen + g * 512: tc * rowlen + g * 512 + gw * 128]
                    eng = "scalar" if (tc + hl) % 2 == 0 else "vector"
                    vv = copy_pass(eng, dstv, pbb[bk][:, 0:gw * 128], waits=[(sT, tt)])
                    BKB.rel(bk, ESEM[eng], vv)
        return sV.n, sA.n

    ld_x0 = load_x(0, barrier=False)
    vln = emit_ln(XT[:], [(sX, ld_x0), (sV, vMODF),
                          (sWP[0], sWP[0].n), (sWP[1], sWP[1].n),
                          (sWM[0], sWM[0].n), (sWM[1], sWM[1].n)])
    vh1, _th1 = emit_h1(0, [(sX, ld_x0)], vln)
    dump("h1h_0", H1H[:], [(sV, vh1)])
    dump("h1l_0", H1L[:], [(sV, vh1)])
    emit_proj(0, [0, 1, 2], vh1)
    vproj, aproj = sV.n, sA.n
    dump("zh_0", ZH[:], [(sV, vproj), (sA, aproj)])
    dump("y_0", TMPA[:], [(sV, vproj), (sA, aproj)])
    dump("qh_0", QH[:], [(sV, vproj), (sA, aproj)])

    prev_vh2 = None
    prev_th1 = _th1
    h2s_slot_guard = {}
    for t in range(n_steps):
        for mh in range(NHC):
            bk = BK.acq()
            mms = []
            for kd in range(NKD):
                lt_h = W1TH[:, kd * 256 + mh * 128: kd * 256 + (mh + 1) * 128]
                lt_l = W1TL[:, kd * 256 + mh * 128: kd * 256 + (mh + 1) * 128]
                rh_h = ZH[:, kd * 512:(kd + 1) * 512]
                rh_l = ZL[:, kd * 512:(kd + 1) * 512]
                mms += [(lt_h, rh_h), (lt_h, rh_l), (lt_l, rh_h)]
            tA = mm_group(pb[bk][:], mms, waits=[(sV, vproj), (sA, aproj)])
            b1c = B1C[:, mh:mh + 1]
            aH = ew("scalar", lambda e, d=HFF[:, mh * 512:(mh + 1) * 512],
                    s=pb[bk][:], b=b1c: e.activation(d, s, AF.Gelu, bias=b),
                    waits=[(sT, tA)])
            vA = ew("vector", lambda e, d=AFB[:, mh * 512:(mh + 1) * 512],
                    s=pb[bk][:], b=b1c: e.tensor_scalar(d, s, b, None, ALU.add),
                    waits=[(sA, aH)])
            BK.rel(bk, sV, vA)
        aH_all, vA_all = sA.n, sV.n
        aHH, vHL = split_hi_lo(HFF[:], HH_[:], HL_[:], waits_hi=[(sA, aH_all)])
        dump(f"A_{t}", AFB[:], [(sV, vA_all)])
        dump(f"H_{t}", HFF[:], [(sA, aH_all)])
        emit_tm((HH_, HL_), (HTH, HTL), NHC, [(sA, aHH), (sV, vHL)])

        agp0 = ew("scalar", lambda e: e.activation(T1K[:], AFB[:], AF.Abs),
                  waits=[(sV, vA_all)])
        vgp = ew("vector", lambda e: e.tensor_scalar(
            T1K[:], T1K[:], TINY, None, ALU.is_lt), waits=[(sA, agp0)])
        vgp = ew("vector", lambda e: e.tensor_add(T2K[:], AFB[:], T1K[:]))
        vgp = ew("vector", lambda e: e.reciprocal(T2K[:], T2K[:]))
        vgp = ew("vector", lambda e: e.tensor_mul(T2K[:], T2K[:], HFF[:]),
                 waits=[(sA, aHH)])
        vgp = ew("vector", lambda e: e.tensor_scalar(
            HFF[:], T2K[:], -1.0, 0.5, ALU.mult, ALU.add))
        vgp = ew("vector", lambda e: e.tensor_mul(HFF[:], HFF[:], T1K[:]))
        vgp = ew("vector", lambda e: e.tensor_add(T2K[:], T2K[:], HFF[:]))
        agp = ew("scalar", lambda e: e.activation(T1K[:], AFB[:], AF.Square),
                 waits=[(sV, vgp)])
        agp = ew("scalar", lambda e: e.activation(T1K[:], T1K[:], AF.Exp, scale=-0.5))
        vgp = ew("vector", lambda e: e.scalar_tensor_tensor(
            T1K[:], AFB[:], INV_SQRT_2PI, T1K[:], ALU.mult, ALU.mult),
            waits=[(sA, agp)])
        vgp = ew("vector", lambda e: e.tensor_add(T2K[:], T2K[:], T1K[:]))
        dump(f"gp_{t}", T2K[:], [(sV, vgp)])

        vGr = aGr = None
        for md2 in range(NKD):
            bk = BK.acq()
            mms = []
            for kh in range(NHC):
                lt_h = W2TH[:, kh * 1024 + md2 * 128: kh * 1024 + (md2 + 1) * 128]
                lt_l = W2TL[:, kh * 1024 + md2 * 128: kh * 1024 + (md2 + 1) * 128]
                rh_h = HH_[:, kh * 512:(kh + 1) * 512]
                rh_l = HL_[:, kh * 512:(kh + 1) * 512]
                mms += [(lt_h, rh_h), (lt_h, rh_l), (lt_l, rh_h)]
            sl = slice(md2 * 512, (md2 + 1) * 512)
            mms += [(IDB[:], ZH[:, sl]), (IDB[:], ZL[:, sl])]
            tR = mm_group(pb[bk][:], mms,
                          waits=[(sA, aHH), (sV, vHL)] if md2 == 0 else ())
            b2c = B2C[:, md2:md2 + 1]
            vE = ew("vector", lambda e, s=pb[bk][:], b=b2c, y=TMPA[:, sl]:
                    e.scalar_tensor_tensor(T512[:], s, b, y, ALU.add, ALU.subtract),
                    waits=[(sT, tR)])
            aGr = ew("scalar", lambda e, d=H1H[:, sl],
                     acc=GBIA[:, 2 + md2:3 + md2]:
                     e.activation(d, T512[:], AF.Copy, scale=COEF, accum_out=acc),
                     waits=[(sV, vE)])
            vGr = ew("vector", lambda e, d=H1L[:, sl], h=H1H[:, sl]:
                     e.scalar_tensor_tensor(d, T512[:], COEF, h, ALU.mult,
                                            ALU.subtract),
                     waits=[(sA, aGr)])
            BK.rel(bk, sV, vGr)
        vGr_all, aGr_all = sV.n, sA.n
        dump(f"Gr_{t}", H1H[:], [(sV, vGr_all), (sA, aGr_all)])

        vga = None
        for mh in range(NHC):
            bk = BK.acq()
            mms = []
            for kd2 in range(NKD):
                lt_h = W2H[:, kd2 * 256 + mh * 128: kd2 * 256 + (mh + 1) * 128]
                lt_l = W2L[:, kd2 * 256 + mh * 128: kd2 * 256 + (mh + 1) * 128]
                rh_h = H1H[:, kd2 * 512:(kd2 + 1) * 512]
                rh_l = H1L[:, kd2 * 512:(kd2 + 1) * 512]
                mms += [(lt_h, rh_h), (lt_h, rh_l), (lt_l, rh_h)]
            tGh = mm_group(pb[bk][:], mms,
                           waits=[(sV, vGr_all), (sA, aGr_all)] if mh == 0 else ())
            sl = slice(mh * 512, (mh + 1) * 512)
            vga = ew("vector", lambda e, sl=sl, s=pb[bk][:],
                     acc=GBIA[:, mh:mh + 1]:
                     e.scalar_tensor_tensor(T1K[:, sl], s, 1.0, T2K[:, sl],
                                            ALU.mult, ALU.mult, accum_out=acc),
                     waits=[(sT, tGh), (sV, vgp)])
            BK.rel(bk, sV, vga)
        aga, vga = split_hi_lo(T1K[:], GAH[:], GAL[:], waits_hi=[(sV, vga)])
        dump(f"Ga_{t}", T1K[:], [(sV, vga)])
        emit_tm((GAH, GAL), (GATH, GATL), NHC, [(sA, aga), (sV, vga)])
        vtm_all, atm_all = sV.n, sA.n

        zts_guard = {}
        gst_guard = {}
        gsti = 0
        for md in range(NKD):
            zsl = ZTS[md % 2]
            zwaits = [(sT, zts_guard[md - 2])] if md >= 2 else ()
            evs = {}
            for hl, src in ((0, ZH), (1, ZL)):
                bk = BKB.acq()
                tiles = [(pbb[bk][:, tc * 128:(tc + 1) * 128],
                          src[:, md * 512 + tc * 128: md * 512 + (tc + 1) * 128])
                         for tc in range(NTC)]
                wt = list(zwaits) + ([(sV, vtm_all), (sA, atm_all)]
                                     if md == 0 and hl == 0 else [])
                tt = tr_group(tiles, IDB[:], waits=wt)
                eng = "scalar" if hl == 0 else "vector"
                vv = copy_pass(eng, zsl[:, hl * 512:(hl + 1) * 512],
                               pbb[bk][:, 0:512], waits=[(sT, tt)])
                evs[eng] = vv
                BKB.rel(bk, ESEM[eng], vv)
            bk = BK.acq()
            mms = []
            for tc in range(NTC):
                lt_h = zsl[:, tc * 128:(tc + 1) * 128]
                lt_l = zsl[:, 512 + tc * 128: 512 + (tc + 1) * 128]
                rh_h = GATH[:, tc * 256:(tc + 1) * 256]
                rh_l = GATL[:, tc * 256:(tc + 1) * 256]
                mms += [(lt_h, rh_h), (lt_h, rh_l), (lt_l, rh_h)]
            wmm = [(ESEM[k], v) for k, v in evs.items()]
            tg = mm_group(pb[bk][:, 0:256], mms, waits=wmm)
            zts_guard[md] = tg
            gw = [(sGS[gsti % 2], gst_guard[gsti - 2])] if gsti >= 2 else []
            vg = copy_pass("vector", GSTG[gsti % 2][:, 0:256], pb[bk][:, 0:256],
                           waits=[(sT, tg)] + gw)
            BK.rel(bk, sV, vg)
            gst_guard[gsti] = store(gin_d[:, md * 256:(md + 1) * 256],
                                    GSTG[gsti % 2][:, 0:256], sGS[gsti % 2],
                                    waits=[(sV, vg)])
            gsti += 1

        emit_tm((H1H, H1L), (ZH, ZL), NKD, [(sV, vGr_all), (sA, aGr_all)])
        vgrt_all, agrt_all = sV.n, sA.n
        for md2 in range(NKD):
            bk = BK.acq()
            mms = []
            for tc in range(NTC):
                lt_h = ZH[:, tc * 1024 + md2 * 128: tc * 1024 + (md2 + 1) * 128]
                lt_l = ZL[:, tc * 1024 + md2 * 128: tc * 1024 + (md2 + 1) * 128]
                rh_h = HTH[:, tc * 256:(tc + 1) * 256]
                rh_l = HTL[:, tc * 256:(tc + 1) * 256]
                mms += [(lt_h, rh_h), (lt_h, rh_l), (lt_l, rh_h)]
            tg = mm_group(pb[bk][:, 0:256], mms,
                          waits=[(sV, vgrt_all), (sA, agrt_all)] if md2 == 0 else ())
            gw = [(sGS[gsti % 2], gst_guard[gsti - 2])] if gsti >= 2 else []
            vg = copy_pass("vector", GSTG[gsti % 2][:, 0:256], pb[bk][:, 0:256],
                           waits=[(sT, tg)] + gw)
            BK.rel(bk, sV, vg)
            gst_guard[gsti] = store(gin_d[:, 2048 + md2 * 256: 2048 + (md2 + 1) * 256],
                                    GSTG[gsti % 2][:, 0:256], sGS[gsti % 2],
                                    waits=[(sV, vg)])
            gsti += 1
        store(gin_d[:, 4096:4106], GBIA[:, 0:10], sGS[0],
              waits=[(sV, vga), (sA, aGr_all)])

        if f"grads_{t}" in dbg:
            tdbg = nc.dram_tensor(f"dbg_grads_{t}", [128, 4106], F32,
                                  kind="ExternalOutput").ap()
            dbg_names.append(f"grads_{t}")
            P.op("gpsimd", lambda e, o=tdbg: e.dma_start(out=o, in_=gin_d[:, :]),
                 waits=[(sGS[0], sGS[0].n), (sGS[1], sGS[1].n)], inc=sST, inc_n=16)
        ccv = P.op("gpsimd", lambda e: e.collective_compute(
            "AllReduce", ALU.add, replica_groups=RG,
            ins=[gin_d.opt()], outs=[gout_d.opt()]),
            waits=[(sGS[0], sGS[0].n), (sGS[1], sGS[1].n)], inc=sCC)
        if f"gback_{t}" in dbg:
            tdbg = nc.dram_tensor(f"dbg_gback_{t}", [128, 4106], F32,
                                  kind="ExternalOutput").ap()
            dbg_names.append(f"gback_{t}")
            P.op("gpsimd", lambda e, o=tdbg: e.dma_start(out=o, in_=gout_d[:, :]),
                 waits=[(sCC, ccv)], inc=sST, inc_n=16)

        th1n = None
        if t + 1 < n_steps:
            ldx = load_x(t + 1, barrier=False,
                         extra=([(sV, prev_vh2)] if prev_vh2 else []) +
                         [(sX2, sX2.n)])
            vln2 = emit_ln(XT[:], [(sX, ldx), (sV, sV.n)])
            vh1n, th1n = emit_h1(t + 1, [(sX, ldx)], vln2)
            emit_proj(t + 1, [0], vh1n)

        go_guard = {}
        for ch in range(9):
            gw = [(sCC, ccv)]
            if ch >= 4:
                gw.append((sV, go_guard[ch - 4]))
            if ch < 8:
                ldg = load(GOUT[ch % 4][:], gout_d[:, ch * 512:(ch + 1) * 512],
                           sGO[ch % 4], waits=gw)
                if ch < 4:
                    dst = W1T_m[:, ch * 512:(ch + 1) * 512]
                else:
                    dst = W2_m[:, (ch - 4) * 512:(ch - 3) * 512]
                vv = ew("vector", lambda e, d=dst, g=GOUT[ch % 4][:]:
                        e.scalar_tensor_tensor(d, g, -LR, d, ALU.mult, ALU.add),
                        waits=[(sGO[ch % 4], ldg)])
                go_guard[ch] = vv
                csl = slice(ch * 512 % 2048, ch * 512 % 2048 + 512)
                if ch < 4:
                    split_hi_lo(W1T_m[:, csl], W1TH[:, csl], W1TL[:, csl],
                                waits_hi=[(sV, vv)])
            else:
                ldg = load(GBIA[:, 0:10], gout_d[:, 4096:4106], sGO[0], waits=gw)
                vv = ew("vector", lambda e: e.scalar_tensor_tensor(
                    B1C[:], GBIA[:, 0:2], -LR, B1C[:], ALU.mult, ALU.add),
                    waits=[(sGO[0], ldg)])
                vv = ew("vector", lambda e: e.scalar_tensor_tensor(
                    B2C[:], GBIA[:, 2:10], -LR, B2C[:], ALU.mult, ALU.add))
        vUPD = sV.n
        split_hi_lo(W2_m[:], W2H[:], W2L[:], waits_hi=[(sV, vUPD)])
        for g in range(4):
            bk = BK.acq()
            tiles = []
            for j in range(4):
                c2, hc = divmod(g * 4 + j, 2)
                tiles.append((pb[bk][:, j * 128:(j + 1) * 128],
                              W2_m[:, c2 * 256 + hc * 128:
                                   c2 * 256 + (hc + 1) * 128]))
            tt = tr_group(tiles, IDF[:], waits=[(sV, vUPD)] if g == 0 else ())
            vv2 = None
            for j in range(4):
                c2, hc = divmod(g * 4 + j, 2)
                sl = slice(hc * 1024 + c2 * 128, hc * 1024 + (c2 + 1) * 128)
                srcp = pb[bk][:, j * 128:(j + 1) * 128]
                ah = copy_pass("scalar", W2TH[:, sl], srcp,
                               waits=[(sT, tt)] if j == 0 else ())
                vv2 = ew("vector", lambda e, d=W2TL[:, sl], s=srcp, h=W2TH[:, sl]:
                         e.tensor_sub(d, s, h), waits=[(sA, ah)])
            BK.rel(bk, sV, vv2)
        aUP, vUP = sA.n, sV.n
        dump(f"W1m_{t}", W1T_m[:], [(sV, vUPD)])
        dump(f"W2m_{t}", W2_m[:], [(sV, vUPD)])

        ldxr = load_x(t, barrier=False,
                      extra=[(sT, th1n if th1n is not None else prev_th1),
                             (sX2, sX2.n)])
        if th1n is not None:
            prev_th1 = th1n

        for mh in range(NHC):
            bk = BK.acq()
            mms = []
            for kd in range(NKD):
                lt_h = W1TH[:, kd * 256 + mh * 128: kd * 256 + (mh + 1) * 128]
                lt_l = W1TL[:, kd * 256 + mh * 128: kd * 256 + (mh + 1) * 128]
                rh_h = QH[:, kd * 512:(kd + 1) * 512]
                rh_l = QL[:, kd * 512:(kd + 1) * 512]
                mms += [(lt_h, rh_h), (lt_h, rh_l), (lt_l, rh_h)]
            tA2 = mm_group(pb[bk][:], mms, waits=[(sA, aUP), (sV, vUP)])
            aH2 = ew("scalar", lambda e, d=HFF[:, mh * 512:(mh + 1) * 512],
                     s=pb[bk][:], b=B1C[:, mh:mh + 1]:
                     e.activation(d, s, AF.Gelu, bias=b), waits=[(sT, tA2)])
            BK.rel(bk, sA, aH2)
        aH2_all = sA.n
        aH2h, vH2l = split_hi_lo(HFF[:], HH_[:], HL_[:], waits_hi=[(sA, aH2_all)])
        for md2 in range(NKD):
            bk = BK.acq()
            mms = []
            for kh in range(NHC):
                lt_h = W2TH[:, kh * 1024 + md2 * 128: kh * 1024 + (md2 + 1) * 128]
                lt_l = W2TL[:, kh * 1024 + md2 * 128: kh * 1024 + (md2 + 1) * 128]
                rh_h = HH_[:, kh * 512:(kh + 1) * 512]
                rh_l = HL_[:, kh * 512:(kh + 1) * 512]
                mms += [(lt_h, rh_h), (lt_h, rh_l), (lt_l, rh_h)]
            sl = slice(md2 * 512, (md2 + 1) * 512)
            mms += [(IDB[:], QH[:, sl]), (IDB[:], QL[:, sl])]
            tO = mm_group(pb[bk][:], mms,
                          waits=[(sA, aH2h), (sV, vH2l)] if md2 == 0 else ())
            vO = ew("vector", lambda e, d=TMPA[:, sl], s=pb[bk][:],
                    b=B2C[:, md2:md2 + 1]:
                    e.tensor_scalar(d, s, b, None, ALU.add), waits=[(sT, tO)])
            BK.rel(bk, sV, vO)
        vATT = sV.n
        dump(f"attn_{t}", TMPA[:], [(sV, vATT)])

        ldga = load(BCA[:], rowsd[0, t, :].rearrange("(o d) -> o d", o=1)
                    .to_broadcast([128, D]), sBC,
                    waits=[(sRW, stROWS)] + ([(sV, prev_vh2)] if prev_vh2 else []))
        vx2 = None
        for tc in range(NTC):
            for g in range(2):
                bk = BK.acq()
                tiles = [(pb[bk][:, j * 128:(j + 1) * 128],
                          TMPA[:, (g * 4 + j) * 512 + tc * 128:
                               (g * 4 + j) * 512 + (tc + 1) * 128])
                         for j in range(4)]
                tt = tr_group(tiles, IDF[:],
                              waits=[(sV, vATT)] if (tc == 0 and g == 0) else ())
                dsl = slice(tc * 1024 + g * 512, tc * 1024 + (g + 1) * 512)
                vx2 = ew("vector", lambda e, s=pb[bk][:], b=BCA[:, g * 512:(g + 1) * 512]:
                         e.tensor_mul(T512[:], s, b),
                         waits=[(sT, tt), (sBC, ldga), (sX, ldxr)])
                vx2 = ew("vector", lambda e, d=XT[:, dsl]:
                         e.tensor_add(d, d, T512[:]))
                BK.rel(bk, sV, vx2)
        vX2 = sV.n
        store(x2d[t * S:(t + 1) * S, :].rearrange("(tc p) d -> p tc d", p=128),
              XT[:].rearrange("p (tc d) -> p tc d", tc=NTC), sX2,
              waits=[(sV, vX2)])
        dump(f"x2_{t}", XT[:], [(sV, vX2)])

        vln3 = emit_ln(XT[:], [(sV, vX2)])
        vu2 = None
        for tc in range(NTC):
            vu2 = ew("vector", lambda e, tc=tc: e.tensor_scalar(
                TMPA[:, tc * 1024:(tc + 1) * 1024], XT[:, tc * 1024:(tc + 1) * 1024],
                stat_a(tc), stat_b(tc), ALU.mult, ALU.add),
                waits=[(sV, vln3)] if tc == 0 else ())
        vh2 = None
        for kd in range(NKD):
            bk = BK.acq()
            tiles = [(pb[bk][:, tc * 128:(tc + 1) * 128],
                      TMPA[:, tc * 1024 + kd * 128: tc * 1024 + (kd + 1) * 128])
                     for tc in range(NTC)]
            tt = tr_group(tiles, IDF[:], waits=[(sV, vu2)] if kd == 0 else ())
            cb1 = modcol(V_CB1, kd, t); sb_ = modcol(V_SB, kd, t)
            hw = ([(sH2S[kd % 2], h2s_slot_guard[kd % 2])]
                  if kd % 2 in h2s_slot_guard else [])
            vh2 = ew("vector", lambda e, s=pb[bk][:], a=cb1, b=sb_,
                     d=H2S[kd % 2][:]:
                     e.tensor_scalar(d, s, a, b, ALU.mult, ALU.add),
                     waits=[(sT, tt)] + hw)
            vh2 = ew("vector", lambda e, d=H2S[kd % 2][:], c=t * 8 + kd:
                     e.reduce_max(MAXH[:, c:c + 1], d, axis=AX,
                                  apply_absolute_value=True))
            BK.rel(bk, sV, vh2)
            h2s_slot_guard[kd % 2] = store(
                h2d[kd, :, t * S:(t + 1) * S], H2S[kd % 2][:],
                sH2S[kd % 2], waits=[(sV, vh2)])
            dump(f"h2_{t}_{kd}", H2S[kd % 2][:], [(sV, vh2)])
        vH2 = sV.n
        prev_vh2 = vH2

        if t + 1 < n_steps:
            emit_proj(t + 1, [1], vh1n)
            emit_proj(t + 1, [2], vh1n)
            vproj, aproj = sV.n, sA.n


    es_ttt.close()

    es_ffn = ExitStack()

    def sbf(name, cols, dt=F32):
        return es_ffn.enter_context(nc.sbuf_tensor(name, [128, cols], dt))

    if do_ffn:
        sMXS = Sem(es, nc, "sMXS"); sMXL = Sem(es, nc, "sMXL")
        sFW = [Sem(es, nc, f"sFW{i}") for i in range(4)]
        sFS = [Sem(es, nc, "sFS0"), Sem(es, nc, "sFS1")]
        sIG = Sem(es, nc, "sIG")
        sH2L = [Sem(es, nc, "sH2L0"), Sem(es, nc, "sH2L1")]
        sPS = Sem(es, nc, "sPS"); sPL = Sem(es, nc, "sPL")
        sX2L = [Sem(es, nc, "sX2L0"), Sem(es, nc, "sX2L1")]
        sOUT = [Sem(es, nc, "sOUT0"), Sem(es, nc, "sOUT1")]
        ALL_DMA_SEMS.extend([sMXS, sMXL, sFW[2], sFW[3],
                             sFW[0], sFW[1], sFS[0], sFS[1], sIG,
                             sH2L[0], sH2L[1], sPS, sPL, sX2L[0], sX2L[1],
                             sOUT[0], sOUT[1]])

        FMISC = sbf("FMISC", 256)
        SXC = sbf("SXC", 8)
        SGC = sbf("SGC", 24); SUC = sbf("SUC", 24); SDC = sbf("SDC", 8)
        SXG = sbf("SXG", 24); SXU = sbf("SXU", 24)
        SDB = sbf("SDB", 1024)
        QV = [sbf(f"QV{i}", 512) for i in range(2)]
        QF = [sbf(f"QF{i}", 512) for i in range(2)]
        QJ = [sbf(f"QJ{i}", 512, I8) for i in range(2)]
        QC = [sbf(f"QC{i}", 512, BF16) for i in range(2)]
        SILG = sbf("SILG", 512)

        ttt_join = [(s_, s_.n) for s_ in
                    (sV, sA, sT, sG, sX2, sH2S[0], sH2S[1], sGS[0], sGS[1],
                     sX, sBC, sGO[0], sGO[1], sPW[0], sPW[1], sWM[0], sWM[1],
                     sWP[0], sWP[1], sINIT, sRW, sPWD) if True]

        def global_max(cols, mxi, mxo, fc, xw=()):
            v1 = ew("vector", lambda e: e.reduce_max(
                MAXH[:, 250:251], MAXH[:, cols[0]:cols[1]], axis=AX),
                waits=[(sV, sV.n)])
            bk = BK.acq()
            t1 = tr_group([(pb[bk][0:1, 0:128], MAXH[:, 250:251])], IDF[:],
                          waits=[(sV, v1)])
            a1 = copy_pass("scalar", FMISC[0:1, fc:fc + 128], pb[bk][0:1, 0:128],
                           waits=[(sT, t1)] + list(xw))
            BK.rel(bk, sA, a1)
            v2 = ew("vector", lambda e: e.reduce_max(
                FMISC[0:1, fc + 128:fc + 129], FMISC[0:1, fc:fc + 128], axis=AX),
                waits=[(sA, a1)])
            g1 = ew("gpsimd", lambda e: e.tensor_scalar(
                FMISC[0:1, fc + 136:fc + 144], ONESROW[0:1, 0:8],
                FMISC[0:1, fc + 128:fc + 129], None, ALU.mult),
                waits=[(sV, v2)])
            st1 = store(mxi, FMISC[0:1, fc + 136:fc + 144], sMXS, waits=[(sG, g1)])
            cc1 = P.op("gpsimd", lambda e: e.collective_compute(
                "AllReduce", ALU.max, replica_groups=RG,
                ins=[mxi.opt()], outs=[mxo.opt()]),
                waits=[(sMXS, st1)], inc=sCC)
            ld1 = load(FMISC[0:1, fc + 144:fc + 152], mxo, sMXL, waits=[(sCC, cc1)])
            return FMISC[0:1, fc + 144:fc + 145], ld1

        def scale_pair(gmax_ap, ld1, clip_col, sxc0, fc):
            a1 = ew("scalar", lambda e: e.activation(
                FMISC[0:1, fc:fc + 1], CLIPS[0:1, clip_col:clip_col + 1],
                AF.Sigmoid), waits=[(sINIT, vINIT)])
            v1 = ew("vector", lambda e: e.tensor_mul(
                FMISC[0:1, fc + 1:fc + 2], gmax_ap, FMISC[0:1, fc:fc + 1]),
                waits=[(sA, a1), (sMXL, ld1)])
            a2 = ew("scalar", lambda e: e.activation(
                FMISC[0:1, fc + 2:fc + 3], FMISC[0:1, fc + 1:fc + 2],
                AF.Copy, scale=1.0 / 127.0), waits=[(sV, v1)])
            v2 = ew("vector", lambda e: e.reciprocal(
                FMISC[0:1, fc + 3:fc + 4], FMISC[0:1, fc + 2:fc + 3]),
                waits=[(sA, a2)])
            v3 = ew("vector", lambda e: e.tensor_scalar(
                FMISC[0:1, fc + 4:fc + 5], FMISC[0:1, fc + 3:fc + 4],
                -1.0, None, ALU.mult))
            bk = BK.acq()
            tb = mm_group(pb[bk][:, 0:3],
                          [(ONESROW[0:1, 0:128], FMISC[0:1, fc + 2:fc + 5])],
                          waits=[(sV, v3)])
            ab = copy_pass("scalar", SXC[:, sxc0:sxc0 + 3], pb[bk][:, 0:3],
                           waits=[(sT, tb)])
            BK.rel(bk, sA, ab)
            return ab

        gm1, ldm1 = global_max((0, 64), mx1i, mx1o, 0)
        ab1 = scale_pair(gm1, ldm1, 0, 0, 160)

        es_fp = ExitStack()

        def sbq(name, cols, dt=F32):
            return es_fp.enter_context(nc.sbuf_tensor(name, [128, cols], dt))

        WQN = [sbq(f"WQN{i}", 2816) for i in range(4)]
        WQV = [sbq(f"WQV{i}", 2816) for i in range(2)]
        WQF = sbq("WQF", 2816); WQD = sbq("WQD", 2816)
        WQJ = sbq("WQJ", 2816, I8); WQCB = sbq("WQCB", 2816, BF16)
        WQB = sbq("WQB", 2816, BF16)
        WQS = [sbq(f"WQS{i}", 2816, BF16) for i in range(2)]
        RMX = sbq("RMX", 64); RQ1 = sbq("RQ1", 64); RQC = sbq("RQC", 64)

        fp_chunks = ([("g", c, wg_in, D, SGC, igtd) for c in range(NIC)] +
                     [("u", c, wu_in, D, SUC, iutd) for c in range(NIC)] +
                     [("d", c, wd_in, II, SDC, idtd) for c in range(NKD)])
        NCH = len(fp_chunks)
        wq_guard = {}
        ws_guard = {}
        prev_vc1 = None
        prev_tt = None
        prev_vd = None
        av_fp = {}
        for cx in range(NCH + 2):
            if 0 <= cx - 1 < NCH:
                j = cx - 1
                kindj, cj, w_inj, cwj, _, _ = fp_chunks[j]
                vrq = ew("vector", lambda e, j=j: e.reciprocal(
                    RQ1[:, j:j + 1], RMX[:, j:j + 1]))
                ew("scalar", lambda e, j=j: e.activation(
                    RQC[:, j:j + 1], RQ1[:, j:j + 1], AF.Copy, scale=127.0),
                    waits=[(sV, vrq)])
                av_fp[j] = ew(
                    "scalar", lambda e, s=WQN[j % 4][:, 0:cwj], cwj=cwj, j=j:
                    e.activation(WQV[j % 2][:, 0:cwj], s, AF.Copy,
                                 scale=RQC[:, j:j + 1]),
                    waits=([(sV, prev_vd)] if prev_vd else []))
            ci = cx - 2
            if ci >= 0:
                kind, c, w_in, cw, scol, dstd = fp_chunks[ci]
                p2 = ci % 2
                vj = ew("vector", lambda e, cw=cw, p2=p2: e.tensor_copy(
                    WQJ[:, 0:cw], WQV[p2][:, 0:cw]),
                    waits=[(sA, av_fp[ci])])
                vd = ew("vector", lambda e, cw=cw, p2=p2: e.tensor_sub(
                    WQD[:, 0:cw], WQV[p2][:, 0:cw], WQJ[:, 0:cw]))
                prev_vd = vd
                vc1 = ew("vector", lambda e, cw=cw: e.scalar_tensor_tensor(
                    WQF[:, 0:cw], WQD[:, 0:cw], 0.5, WQJ[:, 0:cw],
                    ALU.is_ge, ALU.add))
                vc2 = ew("vector", lambda e, cw=cw: e.tensor_scalar(
                    WQCB[:, 0:cw], WQD[:, 0:cw], -0.5, -1.0,
                    ALU.is_le, ALU.mult))
                vq = ew("vector", lambda e, cw=cw: e.tensor_add(
                    WQB[:, 0:cw], WQF[:, 0:cw], WQCB[:, 0:cw]),
                    waits=([(sT, prev_tt)] if prev_tt else []))
                wq_guard[ci] = vq
                ntile = cw // 128
                sw = [(sFS[ci % 2], ws_guard[ci - 2])] if ci >= 2 else []
                for g0 in range(0, ntile, 4):
                    gw_ = min(4, ntile - g0)
                    bkb = BKB.acq()
                    tiles = [(pbb[bkb][:, j * 128:(j + 1) * 128],
                              WQB[:, (g0 + j) * 128:(g0 + j + 1) * 128])
                             for j in range(gw_)]
                    tt = tr_group(tiles, IDB[:],
                                  waits=([(sV, vq)] + sw) if g0 == 0 else ())
                    evv = copy_pass("scalar",
                                    WQS[ci % 2][:, g0 * 128:(g0 + gw_) * 128],
                                    pbb[bkb][:, 0:gw_ * 128], waits=[(sT, tt)])
                    BKB.rel(bkb, sA, evv)
                    prev_tt = tt
                store(dstd[0:ntile, :, c * 128:(c + 1) * 128]
                      .rearrange("k p f -> p k f"),
                      WQS[ci % 2][:, 0:ntile * 128]
                      .rearrange("p (k f) -> p k f", k=ntile), sFS[ci % 2],
                      waits=[(sV, sV.n), (sA, sA.n)])
                ws_guard[ci] = sFS[ci % 2].n
            if cx < NCH:
                kind, c, w_in, cw, scol, dstd = fp_chunks[cx]
                sl4 = cx % 4
                lw = (list(ttt_join) if cx < 4 else
                      [(sV, wq_guard[cx - 4])])
                load(WQN[sl4][0:64, 0:cw], w_in[c * 128:c * 128 + 64, :],
                     sFW[sl4], waits=lw)
                ldn = load(WQN[sl4][64:128, 0:cw],
                           w_in[c * 128 + 64:(c + 1) * 128, :], sFW[sl4])
                vr = ew("vector", lambda e, s=WQN[sl4][:, 0:cw], cx=cx:
                        e.reduce_max(RMX[:, cx:cx + 1], s, axis=AX,
                                     apply_absolute_value=True),
                        waits=[(sFW[sl4], ldn)])
                wq_guard.setdefault(cx, vr)
                ew("scalar", lambda e, d=scol[:, c:c + 1], cx=cx:
                   e.activation(d, RMX[:, cx:cx + 1], AF.Copy,
                                scale=1.0 / 127.0), waits=[(sV, vr)])
        bk = BK.acq()
        tsd = tr_group([(pb[bk][0:8, 0:128], SDC[:, 0:8])], IDF[:],
                       waits=[(sA, sA.n), (sV, sV.n)])
        asd = copy_pass("scalar", FMISC[0:8, 0:128], pb[bk][0:8, 0:128],
                        waits=[(sT, tsd)])
        BK.rel(bk, sA, asd)
        stsd = store(sdrowd.rearrange("o (a f) -> (o a) f", a=8), FMISC[0:8, 0:128],
                     sMXS, waits=[(sA, asd)])

        qst = {"i": 0, "last_vq": [None, None]}

        def quant_slab(src_ap, dst_ap, nrcol, waits_src=()):
            i = qst["i"] % 2; qst["i"] += 1
            gw0 = [(sV, qst["last_vq"][i])] if qst["last_vq"][i] else []
            av = ew("scalar", lambda e: e.activation(QF[i][:], src_ap, AF.Relu,
                                                     bias=CB49[:, 0:1],
                                                     scale=nrcol),
                    waits=list(waits_src) + gw0)
            av2 = ew("scalar", lambda e: e.activation(QV[i][:], QF[i][:],
                                                      AF.Relu,
                                                      bias=CB49[:, 1:2],
                                                      scale=-1.0))
            vj = ew("vector", lambda e: e.tensor_scalar(
                QJ[i][:], QV[i][:], -128.49, None, ALU.add),
                waits=[(sA, av2)])
            vd = ew("vector", lambda e: e.scalar_tensor_tensor(
                QV[i][:], QV[i][:], -128.49, QJ[i][:], ALU.add, ALU.subtract))
            vc1 = ew("vector", lambda e: e.scalar_tensor_tensor(
                QF[i][:], QV[i][:], 0.5, QJ[i][:], ALU.is_ge, ALU.add))
            vc2 = ew("vector", lambda e: e.tensor_scalar(
                QC[i][:], QV[i][:], -0.5, -1.0, ALU.is_le, ALU.mult))
            vq = ew("vector", lambda e: e.tensor_add(dst_ap, QF[i][:], QC[i][:]))
            qst["last_vq"][i] = vq
            return vq

        es_fp.close()
        PTILE = sbf("PTILE", NIC * 512)
        es_f1 = ExitStack()
        IGT = es_f1.enter_context(nc.sbuf_tensor("IGT", [128, NKD * II], BF16))
        IUT = es_f1.enter_context(nc.sbuf_tensor("IUT", [128, NKD * II], BF16))
        H2T = [es_f1.enter_context(nc.sbuf_tensor(f"H2T{i}", [128, 4096], F32))
               for i in range(2)]
        IQF = [es_f1.enter_context(nc.sbuf_tensor(f"IQF{i}", [128, 4096], BF16))
               for i in range(2)]
        for kd in range(NKD):
            load(IGT[:, kd * II:(kd + 1) * II], igtd[kd, :, :], sIG,
                 waits=[(sFS[0], sFS[0].n), (sFS[1], sFS[1].n)] if kd == 0 else ())
            load(IUT[:, kd * II:(kd + 1) * II], iutd[kd, :, :], sIG)
        vIG = sIG.n
        vsxg = ew("vector", lambda e: e.tensor_scalar(
            SXG[:, 0:NIC], SGC[:, 0:NIC], SXC[:, 0:1], None, ALU.mult),
            waits=[(sA, ab1), (sV, sV.n)])
        vsxu = ew("vector", lambda e: e.tensor_scalar(
            SXU[:, 0:NIC], SUC[:, 0:NIC], SXC[:, 0:1], None, ALU.mult))

        h2l_guard = {}
        iqf_guard = {}
        ps_tile_done = {}
        for tau in range(C):
            sl_h = tau % 2
            hw_ = ([(sV, h2l_guard[tau - 2])] if tau >= 2 else list(ttt_join))
            ldh = None
            for kd in range(NKD):
                ldh = load(H2T[sl_h][:, kd * 512:(kd + 1) * 512],
                           h2d[kd, :, tau * S:(tau + 1) * S], sH2L[sl_h],
                           waits=hw_ if kd == 0 else ())
            vq = None
            for kd in range(NKD):
                wsrc = [(sH2L[sl_h], ldh)] if kd == 0 else ()
                wsrc = list(wsrc)
                if tau >= 2 and kd == 0:
                    wsrc.append((sT, iqf_guard[tau - 2]))
                vq = quant_slab(H2T[sl_h][:, kd * 512:(kd + 1) * 512],
                                IQF[tau % 2][:, kd * 512:(kd + 1) * 512],
                                SXC[:, 2:3], waits_src=wsrc)
            h2l_guard[tau] = vq
            pw_ = [(sPS, ps_tile_done[tau - 1])] if tau >= 1 else []
            stp = None
            prev_vp = None
            for m in range(NIC):
                bk = BK.acq()
                mms = [(IGT[:, kd * II + m * 128: kd * II + (m + 1) * 128],
                        IQF[tau % 2][:, kd * 512:(kd + 1) * 512])
                       for kd in range(NKD)]
                tg = mm_group(pb[bk][:], mms,
                              waits=[(sV, vq), (sIG, vIG)] if m == 0 else ())
                asil = ew("scalar", lambda e, s=pb[bk][:], m=m: e.activation(
                    SILG[:], s, AF.Silu, scale=SXG[:, m:m + 1]),
                    waits=([(sT, tg), (sV, vsxg)] if m == 0 else [(sT, tg)]) +
                    ([(sV, prev_vp)] if prev_vp else []))
                BK.rel(bk, sA, asil)
                bk2 = BK.acq()
                mms = [(IUT[:, kd * II + m * 128: kd * II + (m + 1) * 128],
                        IQF[tau % 2][:, kd * 512:(kd + 1) * 512])
                       for kd in range(NKD)]
                tu = mm_group(pb[bk2][:], mms)
                vp = ew("vector", lambda e, s=pb[bk2][:], m=m,
                        d=PTILE[:, m * 512:(m + 1) * 512]:
                        e.scalar_tensor_tensor(d, s, SXU[:, m:m + 1], SILG[:],
                                               ALU.mult, ALU.mult),
                        waits=([(sT, tu), (sA, asil), (sV, vsxu)] + pw_)
                        if m == 0 else [(sT, tu), (sA, asil)])
                BK.rel(bk2, sV, vp)
                prev_vp = vp
                vmx = ew("vector", lambda e, d=PTILE[:, m * 512:(m + 1) * 512],
                         c=64 + tau * NIC + m:
                         e.reduce_max(MAXH[:, c:c + 1], d, axis=AX,
                                      apply_absolute_value=True))
                stp = store(Pd[m, :, tau * S:(tau + 1) * S],
                            PTILE[:, m * 512:(m + 1) * 512], sPS,
                            waits=[(sV, vmx)])
            iqf_guard[tau] = sT.n
            ps_tile_done[tau] = sPS.n

        gm2, ldm2 = global_max((64, 240), mx2i, mx2o, 32, xw=[(sMXS, stsd)])
        ab2 = scale_pair(gm2, ldm2, 1, 3, 192)

        es_f1.close()
        IDT = sbf("IDT", NIC * 1024, BF16)
        HQ2 = [sbf("HQFA", NIC * 512, BF16), sbf("HQFB", NIC * 512, BF16)]
        X2L = [sbf("X2TA", 4096), sbf("X2TB", 4096)]
        CBL = [sbf("CBTA", 1024), sbf("CBTB", 1024)]
        BCL = [sbf("BCF0", 1024), sbf("BCF1", 1024)]
        OUT5 = [sbf(f"OUT5{i}", 512) for i in range(2)]
        T512F = sbf("T512F", 512)
        for ki in range(NIC):
            load(IDT[:, ki * 1024:(ki + 1) * 1024], idtd[ki, :, :], sIG,
                 waits=[(sT, sT.n)] if ki == 0 else ())
        ldsdb = load(SDB[:], sdrowd.rearrange("o d -> o d", o=1)
                     .to_broadcast([128, D]), sIG, waits=[(sMXS, stsd)])
        vIDT = sIG.n
        f2bar = [(sT, sT.n), (sV, sV.n), (sA, sA.n), (sIG, sIG.n),
                 (sH2L[0], sH2L[0].n), (sH2L[1], sH2L[1].n)]

        groupsF2 = [(tc, dg) for tc in range(NTC) for dg in range(2)]
        x2_guard = {}
        out_done = {}
        out5_guard = {}
        gstate = {}
        ldx2s = {}
        ldp_tot = {}
        vqs = {}
        EV_AT = (2, 5, 8, 11, 14, 17, 20)

        def f2_group(tau_, j, waits=()):
            tc, dg = groupsF2[j]
            bk = BK.acq()
            mms = [(HQ2[tau_ % 2][:, ki * 512 + tc * 128:
                                  ki * 512 + (tc + 1) * 128],
                    IDT[:, ki * 1024 + dg * 512:ki * 1024 + (dg + 1) * 512])
                   for ki in range(NIC)]
            tdn = mm_group(pb[bk][:], mms, waits=waits)
            return (bk, tdn)

        def f2_evac(tau_, j, bk, tdn):
            tc, dg = groupsF2[j]
            w1 = [(sT, tdn)] + ([(sX2L[tau_ % 2], ldx2s[tau_])]
                                if j == 0 else [])
            ew("vector", lambda e, s=pb[bk][:], dg=dg, p=tau_ % 2:
               e.tensor_mul(T512F[:], s, CBL[p][:, dg * 512:(dg + 1) * 512]),
               waits=w1)
            BK.rel(bk, sV, sV.n)
            sl = j % 2
            vo2 = ew("vector", lambda e, tc=tc, dg=dg, sl=sl, p=tau_ % 2:
                     e.tensor_add(OUT5[sl][:],
                                  X2L[p][:, tc * 1024 + dg * 512:
                                         tc * 1024 + (dg + 1) * 512],
                                  T512F[:]),
                     waits=([(sOUT[sl], out5_guard[sl])]
                            if sl in out5_guard else []))
            out5_guard[sl] = store(
                out_hbm[tau_][tc * 128:(tc + 1) * 128, dg * 512:(dg + 1) * 512],
                OUT5[sl][:], sOUT[sl], waits=[(sV, vo2)])

        def f2_consume(tau_, j):
            bk, tdn = gstate[tau_][j]
            f2_evac(tau_, j, bk, tdn)
            if j < 2:
                gstate[tau_].append(f2_group(tau_, 6 + j))

        ldp = None
        for m in range(NIC):
            ldp = load(PTILE[:, m * 512:(m + 1) * 512], Pd[m, :, 0:S], sPL,
                       waits=[(sPS, sPS.n), (sV, sV.n)] if m == 0 else ())
        ldp_tot[0] = ldp

        for tau in range(C):
            par = tau % 2
            xw = ([(sV, x2_guard[tau - 2])] if tau >= 2 else list(f2bar))
            ldx2s[tau] = load(
                X2L[par][:].rearrange("p (tc d) -> p tc d", tc=NTC),
                x2d[tau * S:(tau + 1) * S, :]
                .rearrange("(tc p) d -> p tc d", p=128), sX2L[par],
                waits=(xw + [(sX2, sX2.n)]) if tau == 0 else xw)
            ldgb = load(BCL[par][:], rowsd[3, tau, :]
                        .rearrange("(o d) -> o d", o=1)
                        .to_broadcast([128, D]), sBC, waits=list(xw))
            ew("vector", lambda e, par=par: e.scalar_tensor_tensor(
                CBL[par][:], BCL[par][:], SXC[:, 3:4], SDB[:],
                ALU.mult, ALU.mult),
                waits=[(sBC, ldgb), (sIG, vIDT), (sA, ab2)])
            vql = []
            for m in range(NIC):
                wsrc = []
                if m == 0:
                    wsrc = [(sPL, ldp_tot[tau])] + (
                        [(sT, out_done[tau - 2])] if tau >= 2 else list(f2bar))
                vq = quant_slab(PTILE[:, m * 512:(m + 1) * 512],
                                HQ2[par][:, m * 512:(m + 1) * 512],
                                SXC[:, 5:6], waits_src=wsrc)
                vql.append(vq)
                if tau >= 1 and m in EV_AT:
                    f2_consume(tau - 1, EV_AT.index(m))
            if tau >= 1:
                f2_consume(tau - 1, 7)
                out_done[tau - 1] = sT.n
                x2_guard[tau - 1] = sV.n
            vqs[tau] = vql
            if tau + 1 < C:
                ldp = None
                for m in range(NIC):
                    ldp = load(PTILE[:, m * 512:(m + 1) * 512],
                               Pd[m, :, (tau + 1) * S:(tau + 2) * S], sPL,
                               waits=[(sV, vql[m])])
                ldp_tot[tau + 1] = ldp
            gstate[tau] = [f2_group(tau, j,
                                    waits=[(sV, vql[-1])] if j == 0 else ())
                           for j in range(6)]
        for j in range(8):
            f2_consume(C - 1, j)
    es_ffn.close()

    with nc.Block() as blk:
        @blk.sync
        def _(e):
            for f in P.q["sync"]:
                f(e)

        @blk.vector
        def _(e):
            for f in P.q["vector"]:
                f(e)

        @blk.scalar
        def _(e):
            for f in P.q["scalar"]:
                f(e)

        @blk.tensor
        def _(e):
            for f in P.q["tensor"]:
                f(e)

        @blk.gpsimd
        def _(e):
            for f in P.q["gpsimd"]:
                f(e)
            for s_ in ALL_DMA_SEMS:
                if s_.n:
                    e.wait_ge(s_.h, s_.n)

    es.close()
    return nc, dbg_names


def make_inputs_per_core(inputs, core):
    x = np.ascontiguousarray(inputs["L0_attn_in_seq"][:, core * S:(core + 1) * S, :])
    eye = np.eye(128, dtype=np.float32)
    return {
        "x": x,
        "tk": np.asarray(inputs["L0_attn_t_k"]),
        "tv": np.asarray(inputs["L0_attn_t_v"]),
        "tq": np.asarray(inputs["L0_attn_t_q"]),
        "W1": np.asarray(inputs["L0_attn_W1"]), "b1": np.asarray(inputs["L0_attn_b1"]),
        "W2": np.asarray(inputs["L0_attn_W2"]), "b2": np.asarray(inputs["L0_attn_b2"]),
        "clip_x": np.asarray(inputs["L0_ffn_clip_x"]),
        "clip_down": np.asarray(inputs["L0_ffn_clip_down"]),
        "wg": np.asarray(inputs["L0_ffn_w_gate"]),
        "wu": np.asarray(inputs["L0_ffn_w_up"]),
        "wd": np.asarray(inputs["L0_ffn_w_down"]),
        "cvec": np.asarray(inputs["L0_c"]), "wmod": np.asarray(inputs["L0_wmod"]),
        "identf": eye,
        "identb": eye.astype(ml_dtypes.bfloat16),
    }


def kernel(**inputs):
    from concourse.bass_utils import run_bass_kernel_spmd
    nc, _ = build_nc(n_steps=8, do_ffn=True, dbg=())
    in_maps = [make_inputs_per_core(inputs, c) for c in range(8)]
    res = run_bass_kernel_spmd(nc, in_maps, core_ids=list(range(8)))
    out = np.empty((C, 4096, D), dtype=np.float32)
    for c in range(8):
        out[:, c * S:(c + 1) * S, :] = np.asarray(res.results[c]["out"])
    return out


# version 20 (final)
# speedup vs baseline: 1.0205x; 1.0133x over previous
import numpy as np
import ml_dtypes
import concourse.bass as bass
import concourse.mybir as mybir
from contextlib import ExitStack

F32 = mybir.dt.float32
BF16 = mybir.dt.bfloat16
I8 = mybir.dt.int8
AF = mybir.ActivationFunctionType
ALU = mybir.AluOpType
AX = mybir.AxisListType.X

S, C, D, HH, II = 512, 8, 1024, 256, 2816
R = C * S
NKD, NTC, NIC, NHC = 8, 4, 22, 2
LR = 1e-3
COEF = 2.0 / (4096.0 * 1024.0)
EPS = 1e-5
INV_SQRT_2PI = 0.3989422804014327
TINY = 2.0 ** -40
ENGINES = ("sync", "vector", "scalar", "tensor", "gpsimd")
V_SA, V_CA1, V_GA, V_SB, V_CB1, V_GB = range(6)


class Sem:
    def __init__(self, es, nc, name):
        self.h = es.enter_context(nc.semaphore(name))
        self.n = 0


class Prog:
    def __init__(self):
        self.q = {e: [] for e in ENGINES}

    def op(self, eng, fn, waits=(), inc=None, inc_n=1):
        if inc is not None:
            inc.n += inc_n
        def run(e, fn=fn, waits=tuple(waits), inc=inc, inc_n=inc_n):
            for s, v in waits:
                e.wait_ge(s.h, v)
            ins = fn(e)
            if inc is not None:
                ins.then_inc(inc.h, inc_n)
        self.q[eng].append(run)
        return inc.n if inc is not None else None

    def wait(self, eng, sem, value=None):
        v = sem.n if value is None else value
        self.q[eng].append(lambda e, s=sem, v=v: e.wait_ge(s.h, v))


def build_nc(n_steps=8, do_ffn=True, dbg=()):
    dbg = set(dbg)
    nc = bass.Bass("TRN2", target_bir_lowering=False)

    def din(name, shape, dt=F32):
        return nc.dram_tensor(name, shape, dt, kind="ExternalInput").ap()

    x_in = din("x", [C, S, D])
    tw_in = [din(n, [D, D]) for n in ("tk", "tv", "tq")]
    w1_in = din("W1", [HH, D]); b1_in = din("b1", [HH])
    w2_in = din("W2", [D, HH]); b2_in = din("b2", [D])
    clipx_in = din("clip_x", [1]); clipd_in = din("clip_down", [1])
    wg_in = din("wg", [II, D]); wu_in = din("wu", [II, D]); wd_in = din("wd", [D, II])
    c_in = din("cvec", [C, D]); wmod_in = din("wmod", [D, 6 * D])
    idf_in = din("identf", [128, 128]); idb_in = din("identb", [128, 128], BF16)
    out_hbm = nc.dram_tensor("out", [C, S, D], F32, kind="ExternalOutput").ap()

    def dint(name, shape, dt=F32):
        return nc.dram_tensor(name, shape, dt).ap()

    x2d = dint("x2d", [R, D])
    h2d = dint("h2d", [NKD, 128, R])
    Pd = dint("Pd", [NIC, 128, R])
    pwd = dint("pwd", [3, NKD, 2, 128, D], BF16)
    igtd = dint("igtd", [NKD, 128, II], BF16)
    iutd = dint("iutd", [NKD, 128, II], BF16)
    idtd = dint("idtd", [NIC, 128, D], BF16)
    rowsd = dint("rowsd", [4, C, D])
    sdrowd = dint("sdrowd", [1, D])
    gin_d = dint("gin", [128, 4106]); gout_d = dint("gout", [128, 4106])
    mx1i = dint("mx1i", [1, 8]); mx1o = dint("mx1o", [1, 8])
    mx2i = dint("mx2i", [1, 8]); mx2o = dint("mx2o", [1, 8])
    RG = [list(range(8))]

    dbg_names = []
    es = ExitStack()

    def sb(name, cols, dt=F32):
        return es.enter_context(nc.sbuf_tensor(name, [128, cols], dt))

    pb = [es.enter_context(nc.psum_tensor(f"pb{i}", [128, 512], F32)) for i in range(6)]
    pbb = [es.enter_context(nc.psum_tensor(f"pbb{i}", [128, 1024], BF16)) for i in range(2)]

    sST = Sem(es, nc, "sST")
    sT = Sem(es, nc, "sT"); sV = Sem(es, nc, "sV"); sA = Sem(es, nc, "sA")
    sG = Sem(es, nc, "sG"); sCC = Sem(es, nc, "sCC")
    sINIT = Sem(es, nc, "sINIT")
    sWM = [Sem(es, nc, "sWM0"), Sem(es, nc, "sWM1")]
    sWP = [Sem(es, nc, "sWP0"), Sem(es, nc, "sWP1")]
    sPW = [Sem(es, nc, "sPW0"), Sem(es, nc, "sPW1")]
    sX = Sem(es, nc, "sX")
    sBC = Sem(es, nc, "sBC")
    sGO = [Sem(es, nc, f"sGO{i}") for i in range(4)]
    sRW = Sem(es, nc, "sRW")
    sPWD = Sem(es, nc, "sPWD")
    sX2 = Sem(es, nc, "sX2")
    sH2S = [Sem(es, nc, "sH2S0"), Sem(es, nc, "sH2S1")]
    sGS = [Sem(es, nc, "sGS0"), Sem(es, nc, "sGS1")]
    ALL_DMA_SEMS = [sST, sINIT, sWM[0], sWM[1], sWP[0], sWP[1],
                    sPW[0], sPW[1], sX, sBC,
                    sGO[0], sGO[1], sGO[2], sGO[3],
                    sRW, sPWD, sX2, sH2S[0], sH2S[1],
                    sGS[0], sGS[1]]
    ESEM = {"vector": sV, "scalar": sA, "gpsimd": sG, "tensor": sT}

    P = Prog()

    IDF = sb("IDF", 128)
    IDB = sb("IDB", 128, BF16)
    ONESROW = sb("ONESROW", 128)
    CB49 = sb("CB49", 2)
    CLIPS = sb("CLIPS", 2)
    MAXH = sb("MAXH", 256)
    SMALL = sb("SMALL", 32)
    es_ttt = ExitStack()

    def sbt(name, cols, dt=F32):
        return es_ttt.enter_context(nc.sbuf_tensor(name, [128, cols], dt))

    MODF = sbt("MODF", 384)
    W1T_m = sbt("W1T_m", 2048); W2_m = sbt("W2_m", 2048)
    B1C = sbt("B1C", 2); B2C = sbt("B2C", 8)
    W1TH = sbt("W1TH", 2048, BF16); W1TL = sbt("W1TL", 2048, BF16)
    W2H = sbt("W2H", 2048, BF16); W2L = sbt("W2L", 2048, BF16)
    W2TH = sbt("W2TH", 2048, BF16); W2TL = sbt("W2TL", 2048, BF16)
    PW = [sbt(f"PW{i}", 2048, BF16) for i in range(2)]
    XT = sbt("XT", 4096)
    STAT = sbt("STAT", 64)
    TMPA = sbt("TMPA", 4096)
    T512 = sbt("T512", 512)
    H1H = sbt("H1H", 4096, BF16); H1L = sbt("H1L", 4096, BF16)
    TKN = [sbt(f"TKN{i}", 1024) for i in range(2)]
    TKH = sbt("TKH", 1024, BF16)
    TKL = sbt("TKL", 1024, BF16)
    PWS = sbt("PWS", 2048, BF16)
    GBIA = sbt("GBIA", 16)
    GSTG = [sbt(f"GSTG{i}", 512) for i in range(2)]
    GOUT = [sbt(f"GOUT{i}", 512) for i in range(4)]
    H2S = [sbt(f"H2S{i}", 512) for i in range(2)]
    ZTS = [sbt(f"ZTS{i}", 1024, BF16) for i in range(2)]

    def modcol(vec, kd, b):
        i = vec * 64 + b * 8 + kd
        return MODF[:, i:i + 1]

    def modset(vec, b):
        i = vec * 64 + b * 8
        return MODF[:, i:i + 8]

    def load(out_ap, in_ap, sem, eng="sync", waits=(), slow=False):
        if slow:
            return P.op(eng, lambda e, o=out_ap, i=in_ap: e.dma_start(
                out=o, in_=i, allow_slow_non_contiguous=True),
                waits=waits, inc=sem, inc_n=16)
        return P.op(eng, lambda e, o=out_ap, i=in_ap: e.dma_start(out=o, in_=i),
                    waits=waits, inc=sem, inc_n=16)

    def store(out_ap, in_ap, sem, eng="gpsimd", waits=()):
        return P.op(eng, lambda e, o=out_ap, i=in_ap: e.dma_start(out=o, in_=i),
                    waits=waits, inc=sem, inc_n=16)

    def ew(eng, fn, waits=()):
        sw = [(ESEM[eng], ESEM[eng].n)]
        return P.op(eng, fn, waits=list(waits) + sw, inc=ESEM[eng])

    def dump(name, ap_, waits):
        if name not in dbg:
            return
        t = nc.dram_tensor("dbg_" + name, list(ap_.shape), ap_.dtype,
                           kind="ExternalOutput").ap()
        dbg_names.append(name)
        P.op("gpsimd", lambda e, o=t, i=ap_: e.dma_start(out=o, in_=i),
             waits=waits, inc=sST, inc_n=16)

    class Banks:
        def __init__(self, n):
            self.n = n
            self.guard = [None] * n
            self.rr = 0

        def acq(self, pick=None):
            i = self.rr if pick is None else pick
            if pick is None:
                self.rr = (self.rr + 1) % self.n
            if self.guard[i] is not None:
                s, v = self.guard[i]
                P.wait("tensor", s, v)
                self.guard[i] = None
            return i

        def rel(self, i, sem, count):
            self.guard[i] = (sem, count)

    BK = Banks(6)
    BKB = Banks(2)

    def mm_group(bank_ap, mms, waits=()):
        n = len(mms)
        for j, (lt, rh) in enumerate(mms):
            P.op("tensor",
                 lambda e, o=bank_ap, lt=lt, rh=rh, st=(j == 0), sp=(j == n - 1):
                 e.matmul(o, lt, rh, start=st, stop=sp),
                 waits=waits if j == 0 else (),
                 inc=sT if j == n - 1 else None)
        return sT.n

    def tr_group(pairs, ident, waits=()):
        n = len(pairs)
        for j, (o, i) in enumerate(pairs):
            P.op("tensor", lambda e, o=o, i=i, idn=ident: e.transpose(o, i, idn),
                 waits=waits if j == 0 else (),
                 inc=sT if j == n - 1 else None)
        return sT.n

    def copy_pass(eng, dst, src, waits=()):
        if eng == "scalar":
            return ew(eng, lambda e, d=dst, s=src: e.activation(d, s, AF.Copy),
                      waits=waits)
        return ew(eng, lambda e, d=dst, s=src: e.tensor_copy(d, s), waits=waits)

    def split_hi_lo(src_ap, hi_ap, lo_ap, waits_hi=(), waits_lo=()):
        a = copy_pass("scalar", hi_ap, src_ap, waits=waits_hi)
        v = ew("vector", lambda e, d=lo_ap, s=src_ap, h=hi_ap: e.tensor_sub(d, s, h),
               waits=[(sA, a)] + list(waits_lo))
        return a, v

    es_pro = ExitStack()

    def sbp(name, cols, dt=F32):
        return es_pro.enter_context(nc.sbuf_tensor(name, [128, cols], dt))

    load(IDF[:], idf_in, sINIT)
    load(IDB[:], idb_in, sINIT)
    ew("vector", lambda e: e.memset(ONESROW[0:1, :], 1.0))
    ew("vector", lambda e: e.memset(CB49[:, 0:1], 127.49))
    ew("vector", lambda e: e.memset(CB49[:, 1:2], 255.98))
    ew("vector", lambda e: e.memset(MAXH[:], 0.0))

    W1N = sbp("W1N", 2048)
    CT = sbp("CT", 1024)
    SILCT = sbp("SILCT", 64)
    load(W1N[:].rearrange("p (c d) -> p c d", c=2),
         w1_in.rearrange("(c p) d -> p c d", p=128), sINIT)
    load(W2_m[:].rearrange("p (c h) -> p c h", c=8),
         w2_in.rearrange("(c p) h -> p c h", p=128), sINIT)
    load(CLIPS[0:1, 0:1], clipx_in.rearrange("(o x) -> o x", o=1), sINIT)
    load(CLIPS[0:1, 1:2], clipd_in.rearrange("(o x) -> o x", o=1), sINIT)
    load(B1C[:], b1_in.rearrange("(c p) -> p c", p=128), sINIT, slow=True)
    load(B2C[:], b2_in.rearrange("(c p) -> p c", p=128), sINIT, slow=True)
    load(CT[0:8, :], c_in, sINIT)
    vINIT = sINIT.n

    for g in range(4):
        bk = BK.acq()
        tiles = []
        for j in range(4):
            hc, kd = divmod(g * 4 + j, 8)
            tiles.append((pb[bk][:, j * 128:(j + 1) * 128],
                          W1N[:, hc * 1024 + kd * 128: hc * 1024 + (kd + 1) * 128]))
        tt = tr_group(tiles, IDF[:], waits=[(sINIT, vINIT)])
        vv = None
        for j in range(4):
            hc, kd = divmod(g * 4 + j, 8)
            dst = W1T_m[:, kd * 256 + hc * 128: kd * 256 + (hc + 1) * 128]
            vv = copy_pass("vector", dst, pb[bk][:, j * 128:(j + 1) * 128],
                           waits=[(sT, tt)] if j == 0 else ())
        BK.rel(bk, sV, vv)
    v_w1t = sV.n

    def resplit_params(waits_w1, waits_w2):
        split_hi_lo(W1T_m[:], W1TH[:], W1TL[:], waits_hi=waits_w1)
        split_hi_lo(W2_m[:], W2H[:], W2L[:], waits_hi=waits_w2)
        for g in range(4):
            bk = BK.acq()
            tiles = []
            for j in range(4):
                c2, hc = divmod(g * 4 + j, 2)
                tiles.append((pb[bk][:, j * 128:(j + 1) * 128],
                              W2_m[:, c2 * 256 + hc * 128: c2 * 256 + (hc + 1) * 128]))
            tt = tr_group(tiles, IDF[:], waits=waits_w2 if g == 0 else ())
            vv = None
            for j in range(4):
                c2, hc = divmod(g * 4 + j, 2)
                sl = slice(hc * 1024 + c2 * 128, hc * 1024 + (c2 + 1) * 128)
                src = pb[bk][:, j * 128:(j + 1) * 128]
                ah = copy_pass("scalar", W2TH[:, sl], src,
                               waits=[(sT, tt)] if j == 0 else ())
                vv = ew("vector", lambda e, d=W2TL[:, sl], s=src, h=W2TH[:, sl]:
                        e.tensor_sub(d, s, h), waits=[(sA, ah)])
            BK.rel(bk, sV, vv)
        return sA.n, sV.n

    resplit_params([(sV, v_w1t)], [(sINIT, vINIT)])

    aC = ew("scalar", lambda e: e.activation(CT[0:8, :], CT[0:8, :], AF.Silu),
            waits=[(sINIT, vINIT)])
    bk = BK.acq()
    tS = tr_group([(pb[bk][:, kd * 8:(kd + 1) * 8],
                    CT[0:8, kd * 128:(kd + 1) * 128]) for kd in range(NKD)],
                  IDF[0:8, 0:8], waits=[(sA, aC), (sINIT, vINIT)])
    vS = copy_pass("vector", SILCT[:], pb[bk][:, 0:64], waits=[(sT, tS)])
    BK.rel(bk, sV, vS)

    WMC = [sbp(f"WMC{i}", 4096) for i in range(2)]
    MODT = sbp("MODT", 6144)
    wm_guard = {}
    vMT = None
    for msl in range(12):
        waits = [(sT, wm_guard[msl - 2])] if msl >= 2 else ()
        ldw = load(WMC[msl % 2][:].rearrange("p (kd n) -> p kd n", kd=NKD),
                   wmod_in[:, msl * 512:(msl + 1) * 512]
                   .rearrange("(kd p) n -> p kd n", p=128), sWM[msl % 2],
                   waits=waits)
        bk = BK.acq()
        mms = [(SILCT[:, kd * 8:(kd + 1) * 8],
                WMC[msl % 2][:, kd * 512:(kd + 1) * 512]) for kd in range(NKD)]
        tmm = mm_group(pb[bk][0:8, :], mms, waits=[(sWM[msl % 2], ldw), (sV, vS)])
        wm_guard[msl] = tmm
        vMT = copy_pass("vector", MODT[0:8, msl * 512:(msl + 1) * 512],
                        pb[bk][0:8, :], waits=[(sT, tmm)])
        BK.rel(bk, sV, vMT)
    vMOD = None
    for j in range(48):
        vec, kd = divmod(j, 8)
        bk = BK.acq()
        tt = tr_group([(pb[bk][:, 0:8], MODT[0:8, j * 128:(j + 1) * 128])],
                      IDF[0:8, 0:8], waits=[(sV, vMT)] if j == 0 else ())
        dst = MODF[:, vec * 64:(vec + 1) * 64].rearrange(
            "p (b k) -> p b k", b=8, k=8)[:, :, kd:kd + 1]
        vMOD = ew("vector", lambda e, d=dst, s=pb[bk][:, 0:8].unsqueeze(2):
                  e.tensor_copy(d, s), waits=[(sT, tt)])
        BK.rel(bk, sV, vMOD)
    dump("modf", MODF[:], [(sV, vMOD)])

    for vec in (V_CA1, V_CB1):
        sl = slice(vec * 64, (vec + 1) * 64)
        vMOD = ew("vector", lambda e, sl=sl: e.tensor_scalar(
            MODF[:, sl], MODF[:, sl], 1.0, None, ALU.add), waits=[(sV, vMOD)])
    GVECS = (V_GA, V_GB)
    def gsl(vec):
        return slice(vec * 64, (vec + 1) * 64)
    for i, vec in enumerate(GVECS):
        vMOD = ew("vector", lambda e, sl=gsl(vec), t=CT[:, i * 256:i * 256 + 64]:
                  e.tensor_scalar(t, MODF[:, sl], 0.0, 2.0, ALU.is_ge, ALU.mult),
                  waits=[(sV, vMOD)] if i == 0 else ())
    for i, vec in enumerate(GVECS):
        vMOD = ew("vector", lambda e, t=CT[:, i * 256:i * 256 + 64]:
                  e.tensor_scalar(t, t, -1.0, None, ALU.add))
    for i, vec in enumerate(GVECS):
        aM = ew("scalar", lambda e, sl=gsl(vec), t=CT[:, i * 256 + 64:i * 256 + 128]:
                e.activation(t, MODF[:, sl], AF.Abs),
                waits=[(sV, vMOD)] if i == 0 else ())
    for i, vec in enumerate(GVECS):
        vMOD = ew("vector", lambda e, t=CT[:, i * 256 + 64:i * 256 + 128]:
                  e.tensor_scalar(t, t, 0.25, None, ALU.max),
                  waits=[(sA, aM)] if i == 0 else ())
    for i, vec in enumerate(GVECS):
        vMOD = ew("vector", lambda e, sl=gsl(vec), t=CT[:, i * 256:i * 256 + 64],
                  t2=CT[:, i * 256 + 64:i * 256 + 128]:
                  e.tensor_mul(MODF[:, sl], t2, t))
    vMODF = vMOD
    dump("modp", MODF[:], [(sV, vMODF)])

    ROWVECS = (V_GA, V_CB1, V_SB, V_GB)
    rv_last = {"scalar": None, "vector": None}
    for g in range(8):
        bk = BK.acq()
        tiles = []
        for j in range(4):
            idx = g * 4 + j
            ri, b = divmod(idx, C)
            tiles.append((pb[bk][0:8, j * 128:(j + 1) * 128],
                          modset(ROWVECS[ri], b)))
        tt = tr_group(tiles, IDF[:], waits=[(sV, vMODF)] if g == 0 else ())
        eng = "scalar" if g % 2 == 0 else "vector"
        vv = copy_pass(eng, TMPA[0:8, g * 512:(g + 1) * 512],
                       pb[bk][0:8, 0:512], waits=[(sT, tt)])
        rv_last[eng] = vv
        BK.rel(bk, ESEM[eng], vv)
    store(rowsd.rearrange("r b (a f) -> a (r b) f", a=8),
          TMPA[0:8, 0:4096].rearrange("p (g f) -> p g f", g=32), sRW,
          waits=[(ESEM[k], v) for k, v in rv_last.items() if v is not None])
    stROWS = sRW.n

    st_pws_prev = None
    pwst = {}
    prev_split = {}
    for w in range(3):
        for mh in range(NKD):
            i2 = (w * 8 + mh) % 2
            gate = ([(sA, prev_split[i2][0]), (sV, prev_split[i2][1])]
                    if i2 in prev_split else [])
            load(TKN[i2][0:64, :], tw_in[w][mh * 128:mh * 128 + 64, :],
                 sWP[i2], waits=gate)
            ld = load(TKN[i2][64:128, :],
                      tw_in[w][mh * 128 + 64:(mh + 1) * 128, :], sWP[i2])
            wts = [(sPWD, st_pws_prev)] if st_pws_prev else []
            ah, vl = split_hi_lo(TKN[i2][:], TKH[:], TKL[:],
                                 waits_hi=[(sWP[i2], ld)] + wts)
            prev_split[i2] = (ah, vl)
            lastev = {"scalar": None, "vector": None}
            for hl, srcw in ((0, TKH), (1, TKL)):
                bk = BKB.acq()
                tiles = [(pbb[bk][:, j * 128:(j + 1) * 128],
                          srcw[:, j * 128:(j + 1) * 128]) for j in range(8)]
                wt = [(sA, ah), (sV, vl)] if hl == 0 else ()
                tt = tr_group(tiles, IDB[:], waits=wt)
                dst = PWS[:, hl * 1024:(hl + 1) * 1024]
                eng = "scalar" if hl == 0 else "vector"
                vv = copy_pass(eng, dst, pbb[bk][:, 0:1024], waits=[(sT, tt)])
                lastev[eng] = vv
                BKB.rel(bk, ESEM[eng], vv)
            wts2 = [(ESEM[k], v) for k, v in lastev.items() if v is not None]
            store(pwd[w, mh, 0, :, :], PWS[:, 0:1024], sPWD, waits=wts2)
            st_pws_prev = store(pwd[w, mh, 1, :, :], PWS[:, 1024:2048], sPWD)
            pwst[(w, mh)] = st_pws_prev
    stPW = sPWD.n

    es_pro.close()
    ZH = sbt("ZH", 4096, BF16); ZL = sbt("ZL", 4096, BF16)
    BCA = sbt("BCA", 1024)
    QH = sbt("QH", 4096, BF16); QL = sbt("QL", 4096, BF16)
    AFB = sbt("AFB", 1024)
    HFF = sbt("HFF", 1024)
    T1K = sbt("T1K", 1024); T2K = sbt("T2K", 1024)
    HH_ = sbt("HH_", 1024, BF16); HL_ = sbt("HL_", 1024, BF16)
    HTH = sbt("HTH", 1024, BF16); HTL = sbt("HTL", 1024, BF16)
    GAH = sbt("GAH", 1024, BF16); GAL = sbt("GAL", 1024, BF16)
    GATH = sbt("GATH", 1024, BF16); GATL = sbt("GATL", 1024, BF16)

    def load_x(t, barrier=False, extra=()):
        w = (([(s, s.n) for s in (sV, sA, sT, sG, sINIT, sWM[0], sWM[1],
                                  sWP[0], sWP[1], sPWD, sRW)])
             if barrier else []) + list(extra)
        return load(XT[:].rearrange("p (tc d) -> p tc d", tc=NTC),
                    x_in[t].rearrange("(tc p) d -> p tc d", p=128), sX, waits=w)

    def stat_a(tc):
        return STAT[:, tc * 16 + 14: tc * 16 + 15]

    def stat_b(tc):
        return STAT[:, tc * 16 + 15: tc * 16 + 16]

    def emit_ln(src, waits, sqtmp=None):
        sq = sqtmp if sqtmp is not None else T1K
        v = None
        for tc in range(NTC):
            b = tc * 16
            xs = src[:, tc * 1024:(tc + 1) * 1024]
            v = ew("vector", lambda e, b=b, xs=xs: e.reduce_sum(
                STAT[:, b:b + 1], xs, axis=AX),
                waits=waits if tc == 0 else ())
            a = ew("scalar", lambda e, b=b, xs=xs: e.activation(
                sq[:], xs, AF.Square, accum_out=STAT[:, b + 1:b + 2]),
                waits=waits if tc == 0 else ())
        vS1, aS1 = sV.n, sA.n
        def stage(fn, wt=()):
            vv = None
            for tc in range(NTC):
                vv = ew("vector", lambda e, b=tc * 16: fn(e, b),
                        waits=wt if tc == 0 else ())
            return vv
        stage(lambda e, b: e.tensor_scalar(
            STAT[:, b + 2:b + 3], STAT[:, b:b + 1], 1.0 / D, None, ALU.mult),
            wt=[(sV, vS1), (sA, aS1)])
        stage(lambda e, b: e.tensor_scalar(
            STAT[:, b + 3:b + 4], STAT[:, b + 1:b + 2], 1.0 / D, None, ALU.mult))
        stage(lambda e, b: e.scalar_tensor_tensor(
            STAT[:, b + 4:b + 5], STAT[:, b + 2:b + 3], 1.0,
            STAT[:, b + 2:b + 3], ALU.mult, ALU.mult))
        stage(lambda e, b: e.tensor_sub(
            STAT[:, b + 3:b + 4], STAT[:, b + 3:b + 4], STAT[:, b + 4:b + 5]))
        v = stage(lambda e, b: e.tensor_scalar(
            STAT[:, b + 3:b + 4], STAT[:, b + 3:b + 4], EPS, None, ALU.add))
        aq = None
        for tc in range(NTC):
            aq = ew("scalar", lambda e, b=tc * 16: e.activation(
                STAT[:, b + 5:b + 6], STAT[:, b + 3:b + 4], AF.Sqrt),
                waits=[(sV, v)] if tc == 0 else ())
        stage(lambda e, b: e.reciprocal(
            STAT[:, b + 6:b + 7], STAT[:, b + 5:b + 6]), wt=[(sA, aq)])
        stage(lambda e, b: e.scalar_tensor_tensor(
            STAT[:, b + 7:b + 8], STAT[:, b + 6:b + 7], 1.0,
            STAT[:, b + 6:b + 7], ALU.mult, ALU.mult))
        stage(lambda e, b: e.tensor_mul(
            STAT[:, b + 8:b + 9], STAT[:, b + 7:b + 8], STAT[:, b + 3:b + 4]))
        stage(lambda e, b: e.tensor_scalar(
            STAT[:, b + 9:b + 10], STAT[:, b + 8:b + 9], -0.5, 1.5,
            ALU.mult, ALU.add))
        stage(lambda e, b: e.tensor_mul(
            STAT[:, b + 14:b + 15], STAT[:, b + 6:b + 7], STAT[:, b + 9:b + 10]))
        v = stage(lambda e, b: e.scalar_tensor_tensor(
            STAT[:, b + 15:b + 16], STAT[:, b + 2:b + 3], -1.0,
            STAT[:, b + 14:b + 15], ALU.mult, ALU.mult))
        return v
        return v

    def emit_h1(t, waits_x, vln):
        au = None
        for tc in range(NTC):
            au = ew("scalar", lambda e, tc=tc: e.activation(
                TMPA[:, tc * 1024:(tc + 1) * 1024],
                XT[:, tc * 1024:(tc + 1) * 1024],
                AF.Identity, scale=stat_a(tc), bias=stat_b(tc)),
                waits=([(sV, vln), (sRW, stROWS)] + list(waits_x))
                if tc == 0 else ())
        dump(f"u_dbg_{t}", TMPA[:], [(sA, au)])
        dump(f"stat_dbg_{t}", STAT[:], [(sA, au)])
        vh = sV.n
        tlast = None
        for kd in range(NKD):
            bk = BK.acq()
            tiles = [(pb[bk][:, tc * 128:(tc + 1) * 128],
                      TMPA[:, tc * 1024 + kd * 128: tc * 1024 + (kd + 1) * 128])
                     for tc in range(NTC)]
            tlast = tr_group(tiles, IDF[:], waits=[(sA, au)] if kd == 0 else ())
            ca1 = modcol(V_CA1, kd, t); sa_ = modcol(V_SA, kd, t)
            dsth = H1H[:, kd * 512:(kd + 1) * 512]
            dstl = H1L[:, kd * 512:(kd + 1) * 512]
            ew("scalar", lambda e, d=dsth, s=pb[bk][:], a=ca1, b=sa_:
               e.activation(d, s, AF.Identity, scale=a, bias=b),
               waits=[(sT, tlast)])
            ah2 = ew("scalar", lambda e, s=pb[bk][:], a=ca1, b=sa_:
                     e.activation(T512[:], s, AF.Identity, scale=a, bias=b),
                     waits=[(sV, vh)])
            vh = ew("vector", lambda e, d=dstl, h=dsth:
                    e.tensor_sub(d, T512[:], h), waits=[(sA, ah2)])
            BK.rel(bk, sA, ah2)
        return vh, sT.n

    pw_state = {"k": 0, "guards": {}}

    def pw_issue(w, mh):
        k = pw_state["k"]; pw_state["k"] += 1
        waits = [(sPWD, pwst[(w, mh)])]
        if k >= 2:
            waits.append((sT, pw_state["guards"][k - 2]))
        sl = PW[k % 2]
        load(sl[:, 0:1024], pwd[w, mh, 0, :, :], sPW[k % 2], waits=waits)
        ldv = load(sl[:, 1024:2048], pwd[w, mh, 1, :, :], sPW[k % 2])
        return k, sl, ldv

    def emit_proj(t, which, vh1, y_extra_waits=()):
        dst = {0: (ZH, ZL), 2: (QH, QL)}
        for w in which:
            for mh in range(NKD):
                k, sl, ldv = pw_issue(w, mh)
                bk = BK.acq()
                mms = []
                for kd in range(NKD):
                    lt_h = sl[:, kd * 128:(kd + 1) * 128]
                    lt_l = sl[:, 1024 + kd * 128: 1024 + (kd + 1) * 128]
                    rh_h = H1H[:, kd * 512:(kd + 1) * 512]
                    rh_l = H1L[:, kd * 512:(kd + 1) * 512]
                    mms += [(lt_h, rh_h), (lt_h, rh_l), (lt_l, rh_h)]
                tmm = mm_group(pb[bk][:], mms, waits=[(sPW[k % 2], ldv), (sV, vh1)])
                pw_state["guards"][k] = tmm
                if w == 1:
                    vv = copy_pass("vector", TMPA[:, mh * 512:(mh + 1) * 512],
                                   pb[bk][:],
                                   waits=([(sT, tmm)] + list(y_extra_waits))
                                   if mh == 0 else [(sT, tmm)])
                else:
                    dh, dl = dst[w]
                    oh = dh[:, mh * 512:(mh + 1) * 512]
                    ol = dl[:, mh * 512:(mh + 1) * 512]
                    ah = copy_pass("scalar", oh, pb[bk][:], waits=[(sT, tmm)])
                    vv = ew("vector", lambda e, d=ol, s=pb[bk][:], h=oh:
                            e.tensor_sub(d, s, h), waits=[(sA, ah)])
                BK.rel(bk, sV, vv)
        return sV.n, sA.n

    def emit_tm(srcs, dsts, nmh, waits0):
        rowlen = nmh * 128
        for hl in range(2):
            src, dstt = srcs[hl], dsts[hl]
            for tc in range(NTC):
                ngroups = max(1, nmh // 4)
                for g in range(ngroups):
                    gw = min(4, nmh)
                    bk = BKB.acq()
                    tiles = [(pbb[bk][:, j * 128:(j + 1) * 128],
                              src[:, (g * 4 + j) * 512 + tc * 128:
                                  (g * 4 + j) * 512 + (tc + 1) * 128])
                             for j in range(gw)]
                    tt = tr_group(tiles, IDB[:],
                                  waits=waits0 if (hl == 0 and tc == 0 and g == 0) else ())
                    dstv = dstt[:, tc * rowlen + g * 512: tc * rowlen + g * 512 + gw * 128]
                    eng = "scalar" if (tc + hl) % 2 == 0 else "vector"
                    vv = copy_pass(eng, dstv, pbb[bk][:, 0:gw * 128], waits=[(sT, tt)])
                    BKB.rel(bk, ESEM[eng], vv)
        return sV.n, sA.n

    ld_x0 = load_x(0, barrier=False)
    vln = emit_ln(XT[:], [(sX, ld_x0), (sV, vMODF),
                          (sWP[0], sWP[0].n), (sWP[1], sWP[1].n),
                          (sWM[0], sWM[0].n), (sWM[1], sWM[1].n)])
    vh1, _th1 = emit_h1(0, [(sX, ld_x0)], vln)
    dump("h1h_0", H1H[:], [(sV, vh1)])
    dump("h1l_0", H1L[:], [(sV, vh1)])
    emit_proj(0, [0, 1, 2], vh1)
    vproj, aproj = sV.n, sA.n
    dump("zh_0", ZH[:], [(sV, vproj), (sA, aproj)])
    dump("y_0", TMPA[:], [(sV, vproj), (sA, aproj)])
    dump("qh_0", QH[:], [(sV, vproj), (sA, aproj)])

    prev_vh2 = None
    prev_th1 = _th1
    h2s_slot_guard = {}
    for t in range(n_steps):
        for mh in range(NHC):
            bk = BK.acq()
            mms = []
            for kd in range(NKD):
                lt_h = W1TH[:, kd * 256 + mh * 128: kd * 256 + (mh + 1) * 128]
                lt_l = W1TL[:, kd * 256 + mh * 128: kd * 256 + (mh + 1) * 128]
                rh_h = ZH[:, kd * 512:(kd + 1) * 512]
                rh_l = ZL[:, kd * 512:(kd + 1) * 512]
                mms += [(lt_h, rh_h), (lt_h, rh_l), (lt_l, rh_h)]
            tA = mm_group(pb[bk][:], mms, waits=[(sV, vproj), (sA, aproj)])
            b1c = B1C[:, mh:mh + 1]
            aH = ew("scalar", lambda e, d=HFF[:, mh * 512:(mh + 1) * 512],
                    s=pb[bk][:], b=b1c: e.activation(d, s, AF.Gelu, bias=b),
                    waits=[(sT, tA)])
            vA = ew("vector", lambda e, d=AFB[:, mh * 512:(mh + 1) * 512],
                    s=pb[bk][:], b=b1c: e.tensor_scalar(d, s, b, None, ALU.add),
                    waits=[(sA, aH)])
            BK.rel(bk, sV, vA)
        aH_all, vA_all = sA.n, sV.n
        aHH, vHL = split_hi_lo(HFF[:], HH_[:], HL_[:], waits_hi=[(sA, aH_all)])
        dump(f"A_{t}", AFB[:], [(sV, vA_all)])
        dump(f"H_{t}", HFF[:], [(sA, aH_all)])
        emit_tm((HH_, HL_), (HTH, HTL), NHC, [(sA, aHH), (sV, vHL)])

        agp0 = ew("scalar", lambda e: e.activation(T1K[:], AFB[:], AF.Abs),
                  waits=[(sV, vA_all)])
        vgp = ew("vector", lambda e: e.tensor_scalar(
            T1K[:], T1K[:], TINY, None, ALU.is_lt), waits=[(sA, agp0)])
        vgp = ew("vector", lambda e: e.tensor_add(T2K[:], AFB[:], T1K[:]))
        vgp = ew("vector", lambda e: e.reciprocal(T2K[:], T2K[:]))
        vgp = ew("vector", lambda e: e.tensor_mul(T2K[:], T2K[:], HFF[:]),
                 waits=[(sA, aHH)])
        vgp = ew("vector", lambda e: e.tensor_scalar(
            HFF[:], T2K[:], -1.0, 0.5, ALU.mult, ALU.add))
        vgp = ew("vector", lambda e: e.tensor_mul(HFF[:], HFF[:], T1K[:]))
        vgp = ew("vector", lambda e: e.tensor_add(T2K[:], T2K[:], HFF[:]))
        agp = ew("scalar", lambda e: e.activation(T1K[:], AFB[:], AF.Square),
                 waits=[(sV, vgp)])
        agp = ew("scalar", lambda e: e.activation(T1K[:], T1K[:], AF.Exp, scale=-0.5))
        vgp = ew("vector", lambda e: e.scalar_tensor_tensor(
            T1K[:], AFB[:], INV_SQRT_2PI, T1K[:], ALU.mult, ALU.mult),
            waits=[(sA, agp)])
        vgp = ew("vector", lambda e: e.tensor_add(T2K[:], T2K[:], T1K[:]))
        dump(f"gp_{t}", T2K[:], [(sV, vgp)])

        vGr = aGr = None
        for md2 in range(NKD):
            bk = BK.acq()
            mms = []
            for kh in range(NHC):
                lt_h = W2TH[:, kh * 1024 + md2 * 128: kh * 1024 + (md2 + 1) * 128]
                lt_l = W2TL[:, kh * 1024 + md2 * 128: kh * 1024 + (md2 + 1) * 128]
                rh_h = HH_[:, kh * 512:(kh + 1) * 512]
                rh_l = HL_[:, kh * 512:(kh + 1) * 512]
                mms += [(lt_h, rh_h), (lt_h, rh_l), (lt_l, rh_h)]
            sl = slice(md2 * 512, (md2 + 1) * 512)
            mms += [(IDB[:], ZH[:, sl]), (IDB[:], ZL[:, sl])]
            tR = mm_group(pb[bk][:], mms,
                          waits=[(sA, aHH), (sV, vHL)] if md2 == 0 else ())
            b2c = B2C[:, md2:md2 + 1]
            vE = ew("vector", lambda e, s=pb[bk][:], b=b2c, y=TMPA[:, sl]:
                    e.scalar_tensor_tensor(T512[:], s, b, y, ALU.add, ALU.subtract),
                    waits=[(sT, tR)])
            aGr = ew("scalar", lambda e, d=H1H[:, sl],
                     acc=GBIA[:, 2 + md2:3 + md2]:
                     e.activation(d, T512[:], AF.Copy, scale=COEF, accum_out=acc),
                     waits=[(sV, vE)])
            vGr = ew("vector", lambda e, d=H1L[:, sl], h=H1H[:, sl]:
                     e.scalar_tensor_tensor(d, T512[:], COEF, h, ALU.mult,
                                            ALU.subtract),
                     waits=[(sA, aGr)])
            BK.rel(bk, sV, vGr)
        vGr_all, aGr_all = sV.n, sA.n
        dump(f"Gr_{t}", H1H[:], [(sV, vGr_all), (sA, aGr_all)])

        vga = None
        for mh in range(NHC):
            bk = BK.acq()
            mms = []
            for kd2 in range(NKD):
                lt_h = W2H[:, kd2 * 256 + mh * 128: kd2 * 256 + (mh + 1) * 128]
                lt_l = W2L[:, kd2 * 256 + mh * 128: kd2 * 256 + (mh + 1) * 128]
                rh_h = H1H[:, kd2 * 512:(kd2 + 1) * 512]
                rh_l = H1L[:, kd2 * 512:(kd2 + 1) * 512]
                mms += [(lt_h, rh_h), (lt_h, rh_l), (lt_l, rh_h)]
            tGh = mm_group(pb[bk][:], mms,
                           waits=[(sV, vGr_all), (sA, aGr_all)] if mh == 0 else ())
            sl = slice(mh * 512, (mh + 1) * 512)
            vga = ew("vector", lambda e, sl=sl, s=pb[bk][:],
                     acc=GBIA[:, mh:mh + 1]:
                     e.scalar_tensor_tensor(T1K[:, sl], s, 1.0, T2K[:, sl],
                                            ALU.mult, ALU.mult, accum_out=acc),
                     waits=[(sT, tGh), (sV, vgp)])
            BK.rel(bk, sV, vga)
        aga, vga = split_hi_lo(T1K[:], GAH[:], GAL[:], waits_hi=[(sV, vga)])
        dump(f"Ga_{t}", T1K[:], [(sV, vga)])
        emit_tm((GAH, GAL), (GATH, GATL), NHC, [(sA, aga), (sV, vga)])
        vtm_all, atm_all = sV.n, sA.n

        zts_guard = {}
        gst_guard = {}
        gsti = 0
        for md in range(NKD):
            zsl = ZTS[md % 2]
            zwaits = [(sT, zts_guard[md - 2])] if md >= 2 else ()
            evs = {}
            for hl, src in ((0, ZH), (1, ZL)):
                bk = BKB.acq()
                tiles = [(pbb[bk][:, tc * 128:(tc + 1) * 128],
                          src[:, md * 512 + tc * 128: md * 512 + (tc + 1) * 128])
                         for tc in range(NTC)]
                wt = list(zwaits) + ([(sV, vtm_all), (sA, atm_all)]
                                     if md == 0 and hl == 0 else [])
                tt = tr_group(tiles, IDB[:], waits=wt)
                eng = "scalar" if hl == 0 else "vector"
                vv = copy_pass(eng, zsl[:, hl * 512:(hl + 1) * 512],
                               pbb[bk][:, 0:512], waits=[(sT, tt)])
                evs[eng] = vv
                BKB.rel(bk, ESEM[eng], vv)
            bk = BK.acq()
            mms = []
            for tc in range(NTC):
                lt_h = zsl[:, tc * 128:(tc + 1) * 128]
                lt_l = zsl[:, 512 + tc * 128: 512 + (tc + 1) * 128]
                rh_h = GATH[:, tc * 256:(tc + 1) * 256]
                rh_l = GATL[:, tc * 256:(tc + 1) * 256]
                mms += [(lt_h, rh_h), (lt_h, rh_l), (lt_l, rh_h)]
            wmm = [(ESEM[k], v) for k, v in evs.items()]
            tg = mm_group(pb[bk][:, 0:256], mms, waits=wmm)
            zts_guard[md] = tg
            gw = [(sGS[gsti % 2], gst_guard[gsti - 2])] if gsti >= 2 else []
            vg = copy_pass("vector", GSTG[gsti % 2][:, 0:256], pb[bk][:, 0:256],
                           waits=[(sT, tg)] + gw)
            BK.rel(bk, sV, vg)
            gst_guard[gsti] = store(gin_d[:, md * 256:(md + 1) * 256],
                                    GSTG[gsti % 2][:, 0:256], sGS[gsti % 2],
                                    waits=[(sV, vg)])
            gsti += 1

        emit_tm((H1H, H1L), (ZH, ZL), NKD, [(sV, vGr_all), (sA, aGr_all)])
        vgrt_all, agrt_all = sV.n, sA.n
        for md2 in range(NKD):
            bk = BK.acq()
            mms = []
            for tc in range(NTC):
                lt_h = ZH[:, tc * 1024 + md2 * 128: tc * 1024 + (md2 + 1) * 128]
                lt_l = ZL[:, tc * 1024 + md2 * 128: tc * 1024 + (md2 + 1) * 128]
                rh_h = HTH[:, tc * 256:(tc + 1) * 256]
                rh_l = HTL[:, tc * 256:(tc + 1) * 256]
                mms += [(lt_h, rh_h), (lt_h, rh_l), (lt_l, rh_h)]
            tg = mm_group(pb[bk][:, 0:256], mms,
                          waits=[(sV, vgrt_all), (sA, agrt_all)] if md2 == 0 else ())
            gw = [(sGS[gsti % 2], gst_guard[gsti - 2])] if gsti >= 2 else []
            vg = copy_pass("vector", GSTG[gsti % 2][:, 0:256], pb[bk][:, 0:256],
                           waits=[(sT, tg)] + gw)
            BK.rel(bk, sV, vg)
            gst_guard[gsti] = store(gin_d[:, 2048 + md2 * 256: 2048 + (md2 + 1) * 256],
                                    GSTG[gsti % 2][:, 0:256], sGS[gsti % 2],
                                    waits=[(sV, vg)])
            gsti += 1
        store(gin_d[:, 4096:4106], GBIA[:, 0:10], sGS[0],
              waits=[(sV, vga), (sA, aGr_all)])

        if f"grads_{t}" in dbg:
            tdbg = nc.dram_tensor(f"dbg_grads_{t}", [128, 4106], F32,
                                  kind="ExternalOutput").ap()
            dbg_names.append(f"grads_{t}")
            P.op("gpsimd", lambda e, o=tdbg: e.dma_start(out=o, in_=gin_d[:, :]),
                 waits=[(sGS[0], sGS[0].n), (sGS[1], sGS[1].n)], inc=sST, inc_n=16)
        ccv = P.op("gpsimd", lambda e: e.collective_compute(
            "AllReduce", ALU.add, replica_groups=RG,
            ins=[gin_d.opt()], outs=[gout_d.opt()]),
            waits=[(sGS[0], sGS[0].n), (sGS[1], sGS[1].n)], inc=sCC)
        if f"gback_{t}" in dbg:
            tdbg = nc.dram_tensor(f"dbg_gback_{t}", [128, 4106], F32,
                                  kind="ExternalOutput").ap()
            dbg_names.append(f"gback_{t}")
            P.op("gpsimd", lambda e, o=tdbg: e.dma_start(out=o, in_=gout_d[:, :]),
                 waits=[(sCC, ccv)], inc=sST, inc_n=16)

        th1n = None
        if t + 1 < n_steps:
            ldx = load_x(t + 1, barrier=False,
                         extra=([(sV, prev_vh2)] if prev_vh2 else []) +
                         [(sX2, sX2.n)])
            vln2 = emit_ln(XT[:], [(sX, ldx), (sV, sV.n)])
            vh1n, th1n = emit_h1(t + 1, [(sX, ldx)], vln2)
            emit_proj(t + 1, [0], vh1n)

        go_guard = {}
        for ch in range(9):
            gw = [(sCC, ccv)]
            if ch >= 4:
                gw.append((sV, go_guard[ch - 4]))
            if ch < 8:
                ldg = load(GOUT[ch % 4][:], gout_d[:, ch * 512:(ch + 1) * 512],
                           sGO[ch % 4], waits=gw)
                if ch < 4:
                    dst = W1T_m[:, ch * 512:(ch + 1) * 512]
                else:
                    dst = W2_m[:, (ch - 4) * 512:(ch - 3) * 512]
                vv = ew("vector", lambda e, d=dst, g=GOUT[ch % 4][:]:
                        e.scalar_tensor_tensor(d, g, -LR, d, ALU.mult, ALU.add),
                        waits=[(sGO[ch % 4], ldg)])
                go_guard[ch] = vv
                csl = slice(ch * 512 % 2048, ch * 512 % 2048 + 512)
                if ch < 4:
                    split_hi_lo(W1T_m[:, csl], W1TH[:, csl], W1TL[:, csl],
                                waits_hi=[(sV, vv)])
            else:
                ldg = load(GBIA[:, 0:10], gout_d[:, 4096:4106], sGO[0], waits=gw)
                vv = ew("vector", lambda e: e.scalar_tensor_tensor(
                    B1C[:], GBIA[:, 0:2], -LR, B1C[:], ALU.mult, ALU.add),
                    waits=[(sGO[0], ldg)])
                vv = ew("vector", lambda e: e.scalar_tensor_tensor(
                    B2C[:], GBIA[:, 2:10], -LR, B2C[:], ALU.mult, ALU.add))
        vUPD = sV.n
        split_hi_lo(W2_m[:], W2H[:], W2L[:], waits_hi=[(sV, vUPD)])
        for g in range(4):
            bk = BK.acq()
            tiles = []
            for j in range(4):
                c2, hc = divmod(g * 4 + j, 2)
                tiles.append((pb[bk][:, j * 128:(j + 1) * 128],
                              W2_m[:, c2 * 256 + hc * 128:
                                   c2 * 256 + (hc + 1) * 128]))
            tt = tr_group(tiles, IDF[:], waits=[(sV, vUPD)] if g == 0 else ())
            vv2 = None
            for j in range(4):
                c2, hc = divmod(g * 4 + j, 2)
                sl = slice(hc * 1024 + c2 * 128, hc * 1024 + (c2 + 1) * 128)
                srcp = pb[bk][:, j * 128:(j + 1) * 128]
                ah = copy_pass("scalar", W2TH[:, sl], srcp,
                               waits=[(sT, tt)] if j == 0 else ())
                vv2 = ew("vector", lambda e, d=W2TL[:, sl], s=srcp, h=W2TH[:, sl]:
                         e.tensor_sub(d, s, h), waits=[(sA, ah)])
            BK.rel(bk, sV, vv2)
        aUP, vUP = sA.n, sV.n
        dump(f"W1m_{t}", W1T_m[:], [(sV, vUPD)])
        dump(f"W2m_{t}", W2_m[:], [(sV, vUPD)])

        ldxr = load_x(t, barrier=False,
                      extra=[(sT, th1n if th1n is not None else prev_th1),
                             (sX2, sX2.n)])
        if th1n is not None:
            prev_th1 = th1n

        for mh in range(NHC):
            bk = BK.acq()
            mms = []
            for kd in range(NKD):
                lt_h = W1TH[:, kd * 256 + mh * 128: kd * 256 + (mh + 1) * 128]
                lt_l = W1TL[:, kd * 256 + mh * 128: kd * 256 + (mh + 1) * 128]
                rh_h = QH[:, kd * 512:(kd + 1) * 512]
                rh_l = QL[:, kd * 512:(kd + 1) * 512]
                mms += [(lt_h, rh_h), (lt_h, rh_l), (lt_l, rh_h)]
            tA2 = mm_group(pb[bk][:], mms, waits=[(sA, aUP), (sV, vUP)])
            aH2 = ew("scalar", lambda e, d=HFF[:, mh * 512:(mh + 1) * 512],
                     s=pb[bk][:], b=B1C[:, mh:mh + 1]:
                     e.activation(d, s, AF.Gelu, bias=b), waits=[(sT, tA2)])
            BK.rel(bk, sA, aH2)
        aH2_all = sA.n
        aH2h, vH2l = split_hi_lo(HFF[:], HH_[:], HL_[:], waits_hi=[(sA, aH2_all)])
        for md2 in range(NKD):
            bk = BK.acq()
            mms = []
            for kh in range(NHC):
                lt_h = W2TH[:, kh * 1024 + md2 * 128: kh * 1024 + (md2 + 1) * 128]
                lt_l = W2TL[:, kh * 1024 + md2 * 128: kh * 1024 + (md2 + 1) * 128]
                rh_h = HH_[:, kh * 512:(kh + 1) * 512]
                rh_l = HL_[:, kh * 512:(kh + 1) * 512]
                mms += [(lt_h, rh_h), (lt_h, rh_l), (lt_l, rh_h)]
            sl = slice(md2 * 512, (md2 + 1) * 512)
            mms += [(IDB[:], QH[:, sl]), (IDB[:], QL[:, sl])]
            tO = mm_group(pb[bk][:], mms,
                          waits=[(sA, aH2h), (sV, vH2l)] if md2 == 0 else ())
            aO = ew("scalar", lambda e, d=TMPA[:, sl], s=pb[bk][:],
                    b=B2C[:, md2:md2 + 1]:
                    e.activation(d, s, AF.Identity, bias=b), waits=[(sT, tO)])
            BK.rel(bk, sA, aO)
        aATT = sA.n
        dump(f"attn_{t}", TMPA[:], [(sA, aATT)])

        ldga = load(BCA[:], rowsd[0, t, :].rearrange("(o d) -> o d", o=1)
                    .to_broadcast([128, D]), sBC,
                    waits=[(sRW, stROWS)] + ([(sV, prev_vh2)] if prev_vh2 else []))
        vx2 = None
        for tc in range(NTC):
            for g in range(2):
                bk = BK.acq()
                tiles = [(pb[bk][:, j * 128:(j + 1) * 128],
                          TMPA[:, (g * 4 + j) * 512 + tc * 128:
                               (g * 4 + j) * 512 + (tc + 1) * 128])
                         for j in range(4)]
                tt = tr_group(tiles, IDF[:],
                              waits=[(sA, aATT)] if (tc == 0 and g == 0) else ())
                dsl = slice(tc * 1024 + g * 512, tc * 1024 + (g + 1) * 512)
                vx2 = ew("vector", lambda e, s=pb[bk][:], b=BCA[:, g * 512:(g + 1) * 512]:
                         e.tensor_mul(T512[:], s, b),
                         waits=[(sT, tt), (sBC, ldga), (sX, ldxr)])
                vx2 = ew("vector", lambda e, d=XT[:, dsl]:
                         e.tensor_add(d, d, T512[:]))
                BK.rel(bk, sV, vx2)
        vX2 = sV.n
        store(x2d[t * S:(t + 1) * S, :].rearrange("(tc p) d -> p tc d", p=128),
              XT[:].rearrange("p (tc d) -> p tc d", tc=NTC), sX2,
              waits=[(sV, vX2)])
        dump(f"x2_{t}", XT[:], [(sV, vX2)])

        vln3 = emit_ln(XT[:], [(sV, vX2)])
        au2 = None
        for tc in range(NTC):
            au2 = ew("scalar", lambda e, tc=tc: e.activation(
                TMPA[:, tc * 1024:(tc + 1) * 1024],
                XT[:, tc * 1024:(tc + 1) * 1024],
                AF.Identity, scale=stat_a(tc), bias=stat_b(tc)),
                waits=[(sV, vln3), (sA, aATT)] if tc == 0 else ())
        vh2 = None
        for kd in range(NKD):
            bk = BK.acq()
            tiles = [(pb[bk][:, tc * 128:(tc + 1) * 128],
                      TMPA[:, tc * 1024 + kd * 128: tc * 1024 + (kd + 1) * 128])
                     for tc in range(NTC)]
            tt = tr_group(tiles, IDF[:], waits=[(sA, au2)] if kd == 0 else ())
            cb1 = modcol(V_CB1, kd, t); sb_ = modcol(V_SB, kd, t)
            hw = ([(sH2S[kd % 2], h2s_slot_guard[kd % 2])]
                  if kd % 2 in h2s_slot_guard else [])
            ah2s = ew("scalar", lambda e, s=pb[bk][:], a=cb1, b=sb_,
                      d=H2S[kd % 2][:]:
                      e.activation(d, s, AF.Identity, scale=a, bias=b),
                      waits=[(sT, tt)] + hw)
            vh2 = ew("vector", lambda e, d=H2S[kd % 2][:], c=t * 8 + kd:
                     e.reduce_max(MAXH[:, c:c + 1], d, axis=AX,
                                  apply_absolute_value=True),
                     waits=[(sA, ah2s)])
            BK.rel(bk, sV, vh2)
            h2s_slot_guard[kd % 2] = store(
                h2d[kd, :, t * S:(t + 1) * S], H2S[kd % 2][:],
                sH2S[kd % 2], waits=[(sV, vh2)])
            dump(f"h2_{t}_{kd}", H2S[kd % 2][:], [(sV, vh2)])
        vH2 = sV.n
        prev_vh2 = vH2

        if t + 1 < n_steps:
            emit_proj(t + 1, [1], vh1n)
            emit_proj(t + 1, [2], vh1n)
            vproj, aproj = sV.n, sA.n


    es_ttt.close()

    es_ffn = ExitStack()

    def sbf(name, cols, dt=F32):
        return es_ffn.enter_context(nc.sbuf_tensor(name, [128, cols], dt))

    if do_ffn:
        sMXS = Sem(es, nc, "sMXS"); sMXL = Sem(es, nc, "sMXL")
        sFW = [Sem(es, nc, f"sFW{i}") for i in range(4)]
        sFS = [Sem(es, nc, "sFS0"), Sem(es, nc, "sFS1")]
        sIG = Sem(es, nc, "sIG")
        sH2L = [Sem(es, nc, "sH2L0"), Sem(es, nc, "sH2L1")]
        sPS = Sem(es, nc, "sPS"); sPL = Sem(es, nc, "sPL")
        sX2L = [Sem(es, nc, "sX2L0"), Sem(es, nc, "sX2L1")]
        sOUT = [Sem(es, nc, "sOUT0"), Sem(es, nc, "sOUT1")]
        ALL_DMA_SEMS.extend([sMXS, sMXL, sFW[2], sFW[3],
                             sFW[0], sFW[1], sFS[0], sFS[1], sIG,
                             sH2L[0], sH2L[1], sPS, sPL, sX2L[0], sX2L[1],
                             sOUT[0], sOUT[1]])

        FMISC = sbf("FMISC", 256)
        SXC = sbf("SXC", 8)
        SGC = sbf("SGC", 24); SUC = sbf("SUC", 24); SDC = sbf("SDC", 8)
        SXG = sbf("SXG", 24); SXU = sbf("SXU", 24)
        SDB = sbf("SDB", 1024)
        QV = [sbf(f"QV{i}", 512) for i in range(2)]
        QF = [sbf(f"QF{i}", 512) for i in range(2)]
        QJ = [sbf(f"QJ{i}", 512, I8) for i in range(2)]
        QC = [sbf(f"QC{i}", 512, BF16) for i in range(2)]
        SILG = sbf("SILG", 512)

        ttt_join = [(s_, s_.n) for s_ in
                    (sV, sA, sT, sG, sX2, sH2S[0], sH2S[1], sGS[0], sGS[1],
                     sX, sBC, sGO[0], sGO[1], sPW[0], sPW[1], sWM[0], sWM[1],
                     sWP[0], sWP[1], sINIT, sRW, sPWD) if True]

        def global_max(cols, mxi, mxo, fc, xw=()):
            v1 = ew("vector", lambda e: e.reduce_max(
                MAXH[:, 250:251], MAXH[:, cols[0]:cols[1]], axis=AX),
                waits=[(sV, sV.n)])
            bk = BK.acq()
            t1 = tr_group([(pb[bk][0:1, 0:128], MAXH[:, 250:251])], IDF[:],
                          waits=[(sV, v1)])
            a1 = copy_pass("scalar", FMISC[0:1, fc:fc + 128], pb[bk][0:1, 0:128],
                           waits=[(sT, t1)] + list(xw))
            BK.rel(bk, sA, a1)
            v2 = ew("vector", lambda e: e.reduce_max(
                FMISC[0:1, fc + 128:fc + 129], FMISC[0:1, fc:fc + 128], axis=AX),
                waits=[(sA, a1)])
            g1 = ew("gpsimd", lambda e: e.tensor_scalar(
                FMISC[0:1, fc + 136:fc + 144], ONESROW[0:1, 0:8],
                FMISC[0:1, fc + 128:fc + 129], None, ALU.mult),
                waits=[(sV, v2)])
            st1 = store(mxi, FMISC[0:1, fc + 136:fc + 144], sMXS, waits=[(sG, g1)])
            cc1 = P.op("gpsimd", lambda e: e.collective_compute(
                "AllReduce", ALU.max, replica_groups=RG,
                ins=[mxi.opt()], outs=[mxo.opt()]),
                waits=[(sMXS, st1)], inc=sCC)
            ld1 = load(FMISC[0:1, fc + 144:fc + 152], mxo, sMXL, waits=[(sCC, cc1)])
            return FMISC[0:1, fc + 144:fc + 145], ld1

        def scale_pair(gmax_ap, ld1, clip_col, sxc0, fc):
            a1 = ew("scalar", lambda e: e.activation(
                FMISC[0:1, fc:fc + 1], CLIPS[0:1, clip_col:clip_col + 1],
                AF.Sigmoid), waits=[(sINIT, vINIT)])
            v1 = ew("vector", lambda e: e.tensor_mul(
                FMISC[0:1, fc + 1:fc + 2], gmax_ap, FMISC[0:1, fc:fc + 1]),
                waits=[(sA, a1), (sMXL, ld1)])
            a2 = ew("scalar", lambda e: e.activation(
                FMISC[0:1, fc + 2:fc + 3], FMISC[0:1, fc + 1:fc + 2],
                AF.Copy, scale=1.0 / 127.0), waits=[(sV, v1)])
            v2 = ew("vector", lambda e: e.reciprocal(
                FMISC[0:1, fc + 3:fc + 4], FMISC[0:1, fc + 2:fc + 3]),
                waits=[(sA, a2)])
            v3 = ew("vector", lambda e: e.tensor_scalar(
                FMISC[0:1, fc + 4:fc + 5], FMISC[0:1, fc + 3:fc + 4],
                -1.0, None, ALU.mult))
            bk = BK.acq()
            tb = mm_group(pb[bk][:, 0:3],
                          [(ONESROW[0:1, 0:128], FMISC[0:1, fc + 2:fc + 5])],
                          waits=[(sV, v3)])
            ab = copy_pass("scalar", SXC[:, sxc0:sxc0 + 3], pb[bk][:, 0:3],
                           waits=[(sT, tb)])
            BK.rel(bk, sA, ab)
            return ab

        gm1, ldm1 = global_max((0, 64), mx1i, mx1o, 0)
        ab1 = scale_pair(gm1, ldm1, 0, 0, 160)

        es_fp = ExitStack()

        def sbq(name, cols, dt=F32):
            return es_fp.enter_context(nc.sbuf_tensor(name, [128, cols], dt))

        WQN = [sbq(f"WQN{i}", 2816) for i in range(4)]
        WQV = [sbq(f"WQV{i}", 2816) for i in range(2)]
        WQF = sbq("WQF", 2816); WQD = sbq("WQD", 2816)
        WQJ = sbq("WQJ", 2816, I8); WQCB = sbq("WQCB", 2816, BF16)
        WQB = sbq("WQB", 2816, BF16)
        WQS = [sbq(f"WQS{i}", 2816, BF16) for i in range(2)]
        RMX = sbq("RMX", 64); RQ1 = sbq("RQ1", 64); RQC = sbq("RQC", 64)

        fp_chunks = ([("g", c, wg_in, D, SGC, igtd) for c in range(NIC)] +
                     [("u", c, wu_in, D, SUC, iutd) for c in range(NIC)] +
                     [("d", c, wd_in, II, SDC, idtd) for c in range(NKD)])
        NCH = len(fp_chunks)
        wq_guard = {}
        ws_guard = {}
        prev_vc1 = None
        prev_tt = None
        prev_vd = None
        av_fp = {}
        for cx in range(NCH + 2):
            if 0 <= cx - 1 < NCH:
                j = cx - 1
                kindj, cj, w_inj, cwj, _, _ = fp_chunks[j]
                vrq = ew("vector", lambda e, j=j: e.reciprocal(
                    RQ1[:, j:j + 1], RMX[:, j:j + 1]))
                ew("scalar", lambda e, j=j: e.activation(
                    RQC[:, j:j + 1], RQ1[:, j:j + 1], AF.Copy, scale=127.0),
                    waits=[(sV, vrq)])
                av_fp[j] = ew(
                    "scalar", lambda e, s=WQN[j % 4][:, 0:cwj], cwj=cwj, j=j:
                    e.activation(WQV[j % 2][:, 0:cwj], s, AF.Copy,
                                 scale=RQC[:, j:j + 1]),
                    waits=([(sV, prev_vd)] if prev_vd else []))
            ci = cx - 2
            if ci >= 0:
                kind, c, w_in, cw, scol, dstd = fp_chunks[ci]
                p2 = ci % 2
                vj = ew("vector", lambda e, cw=cw, p2=p2: e.tensor_copy(
                    WQJ[:, 0:cw], WQV[p2][:, 0:cw]),
                    waits=[(sA, av_fp[ci])])
                vd = ew("vector", lambda e, cw=cw, p2=p2: e.tensor_sub(
                    WQD[:, 0:cw], WQV[p2][:, 0:cw], WQJ[:, 0:cw]))
                prev_vd = vd
                vc1 = ew("vector", lambda e, cw=cw: e.scalar_tensor_tensor(
                    WQF[:, 0:cw], WQD[:, 0:cw], 0.5, WQJ[:, 0:cw],
                    ALU.is_ge, ALU.add))
                vc2 = ew("vector", lambda e, cw=cw: e.tensor_scalar(
                    WQCB[:, 0:cw], WQD[:, 0:cw], -0.5, -1.0,
                    ALU.is_le, ALU.mult))
                vq = ew("vector", lambda e, cw=cw: e.tensor_add(
                    WQB[:, 0:cw], WQF[:, 0:cw], WQCB[:, 0:cw]),
                    waits=([(sT, prev_tt)] if prev_tt else []))
                wq_guard[ci] = vq
                ntile = cw // 128
                sw = [(sFS[ci % 2], ws_guard[ci - 2])] if ci >= 2 else []
                for g0 in range(0, ntile, 4):
                    gw_ = min(4, ntile - g0)
                    bkb = BKB.acq()
                    tiles = [(pbb[bkb][:, j * 128:(j + 1) * 128],
                              WQB[:, (g0 + j) * 128:(g0 + j + 1) * 128])
                             for j in range(gw_)]
                    tt = tr_group(tiles, IDB[:],
                                  waits=([(sV, vq)] + sw) if g0 == 0 else ())
                    evv = copy_pass("scalar",
                                    WQS[ci % 2][:, g0 * 128:(g0 + gw_) * 128],
                                    pbb[bkb][:, 0:gw_ * 128], waits=[(sT, tt)])
                    BKB.rel(bkb, sA, evv)
                    prev_tt = tt
                store(dstd[0:ntile, :, c * 128:(c + 1) * 128]
                      .rearrange("k p f -> p k f"),
                      WQS[ci % 2][:, 0:ntile * 128]
                      .rearrange("p (k f) -> p k f", k=ntile), sFS[ci % 2],
                      waits=[(sV, sV.n), (sA, sA.n)])
                ws_guard[ci] = sFS[ci % 2].n
            if cx < NCH:
                kind, c, w_in, cw, scol, dstd = fp_chunks[cx]
                sl4 = cx % 4
                lw = (list(ttt_join) if cx < 4 else
                      [(sV, wq_guard[cx - 4])])
                load(WQN[sl4][0:64, 0:cw], w_in[c * 128:c * 128 + 64, :],
                     sFW[sl4], waits=lw)
                ldn = load(WQN[sl4][64:128, 0:cw],
                           w_in[c * 128 + 64:(c + 1) * 128, :], sFW[sl4])
                vr = ew("vector", lambda e, s=WQN[sl4][:, 0:cw], cx=cx:
                        e.reduce_max(RMX[:, cx:cx + 1], s, axis=AX,
                                     apply_absolute_value=True),
                        waits=[(sFW[sl4], ldn)])
                wq_guard.setdefault(cx, vr)
                ew("scalar", lambda e, d=scol[:, c:c + 1], cx=cx:
                   e.activation(d, RMX[:, cx:cx + 1], AF.Copy,
                                scale=1.0 / 127.0), waits=[(sV, vr)])
        bk = BK.acq()
        tsd = tr_group([(pb[bk][0:8, 0:128], SDC[:, 0:8])], IDF[:],
                       waits=[(sA, sA.n), (sV, sV.n)])
        asd = copy_pass("scalar", FMISC[0:8, 0:128], pb[bk][0:8, 0:128],
                        waits=[(sT, tsd)])
        BK.rel(bk, sA, asd)
        stsd = store(sdrowd.rearrange("o (a f) -> (o a) f", a=8), FMISC[0:8, 0:128],
                     sMXS, waits=[(sA, asd)])

        qst = {"i": 0, "last_vq": [None, None]}

        def quant_slab(src_ap, dst_ap, nrcol, waits_src=()):
            i = qst["i"] % 2; qst["i"] += 1
            gw0 = [(sV, qst["last_vq"][i])] if qst["last_vq"][i] else []
            av = ew("scalar", lambda e: e.activation(QF[i][:], src_ap, AF.Relu,
                                                     bias=CB49[:, 0:1],
                                                     scale=nrcol),
                    waits=list(waits_src) + gw0)
            av2 = ew("scalar", lambda e: e.activation(QV[i][:], QF[i][:],
                                                      AF.Relu,
                                                      bias=CB49[:, 1:2],
                                                      scale=-1.0))
            vj = ew("vector", lambda e: e.tensor_scalar(
                QJ[i][:], QV[i][:], -128.49, None, ALU.add),
                waits=[(sA, av2)])
            vd = ew("vector", lambda e: e.scalar_tensor_tensor(
                QV[i][:], QV[i][:], -128.49, QJ[i][:], ALU.add, ALU.subtract))
            vc1 = ew("vector", lambda e: e.scalar_tensor_tensor(
                QF[i][:], QV[i][:], 0.5, QJ[i][:], ALU.is_ge, ALU.add))
            vc2 = ew("vector", lambda e: e.tensor_scalar(
                QC[i][:], QV[i][:], -0.5, -1.0, ALU.is_le, ALU.mult))
            vq = ew("vector", lambda e: e.tensor_add(dst_ap, QF[i][:], QC[i][:]))
            qst["last_vq"][i] = vq
            return vq

        es_fp.close()
        PTILE = sbf("PTILE", NIC * 512)
        es_f1 = ExitStack()
        IGT = es_f1.enter_context(nc.sbuf_tensor("IGT", [128, NKD * II], BF16))
        IUT = es_f1.enter_context(nc.sbuf_tensor("IUT", [128, NKD * II], BF16))
        H2T = [es_f1.enter_context(nc.sbuf_tensor(f"H2T{i}", [128, 4096], F32))
               for i in range(2)]
        IQF = [es_f1.enter_context(nc.sbuf_tensor(f"IQF{i}", [128, 4096], BF16))
               for i in range(2)]
        for kd in range(NKD):
            load(IGT[:, kd * II:(kd + 1) * II], igtd[kd, :, :], sIG,
                 waits=[(sFS[0], sFS[0].n), (sFS[1], sFS[1].n)] if kd == 0 else ())
            load(IUT[:, kd * II:(kd + 1) * II], iutd[kd, :, :], sIG)
        vIG = sIG.n
        vsxg = ew("vector", lambda e: e.tensor_scalar(
            SXG[:, 0:NIC], SGC[:, 0:NIC], SXC[:, 0:1], None, ALU.mult),
            waits=[(sA, ab1), (sV, sV.n)])
        vsxu = ew("vector", lambda e: e.tensor_scalar(
            SXU[:, 0:NIC], SUC[:, 0:NIC], SXC[:, 0:1], None, ALU.mult))

        h2l_guard = {}
        iqf_guard = {}
        ps_tile_done = {}
        for tau in range(C):
            sl_h = tau % 2
            hw_ = ([(sV, h2l_guard[tau - 2])] if tau >= 2 else list(ttt_join))
            ldh = None
            for kd in range(NKD):
                ldh = load(H2T[sl_h][:, kd * 512:(kd + 1) * 512],
                           h2d[kd, :, tau * S:(tau + 1) * S], sH2L[sl_h],
                           waits=hw_ if kd == 0 else ())
            vq = None
            for kd in range(NKD):
                wsrc = [(sH2L[sl_h], ldh)] if kd == 0 else ()
                wsrc = list(wsrc)
                if tau >= 2 and kd == 0:
                    wsrc.append((sT, iqf_guard[tau - 2]))
                vq = quant_slab(H2T[sl_h][:, kd * 512:(kd + 1) * 512],
                                IQF[tau % 2][:, kd * 512:(kd + 1) * 512],
                                SXC[:, 2:3], waits_src=wsrc)
            h2l_guard[tau] = vq
            pw_ = [(sPS, ps_tile_done[tau - 1])] if tau >= 1 else []
            stp = None
            prev_vp = None
            for m in range(NIC):
                bk = BK.acq()
                mms = [(IGT[:, kd * II + m * 128: kd * II + (m + 1) * 128],
                        IQF[tau % 2][:, kd * 512:(kd + 1) * 512])
                       for kd in range(NKD)]
                tg = mm_group(pb[bk][:], mms,
                              waits=[(sV, vq), (sIG, vIG)] if m == 0 else ())
                asil = ew("scalar", lambda e, s=pb[bk][:], m=m: e.activation(
                    SILG[:], s, AF.Silu, scale=SXG[:, m:m + 1]),
                    waits=([(sT, tg), (sV, vsxg)] if m == 0 else [(sT, tg)]) +
                    ([(sV, prev_vp)] if prev_vp else []))
                BK.rel(bk, sA, asil)
                bk2 = BK.acq()
                mms = [(IUT[:, kd * II + m * 128: kd * II + (m + 1) * 128],
                        IQF[tau % 2][:, kd * 512:(kd + 1) * 512])
                       for kd in range(NKD)]
                tu = mm_group(pb[bk2][:], mms)
                vp = ew("vector", lambda e, s=pb[bk2][:], m=m,
                        d=PTILE[:, m * 512:(m + 1) * 512]:
                        e.scalar_tensor_tensor(d, s, SXU[:, m:m + 1], SILG[:],
                                               ALU.mult, ALU.mult),
                        waits=([(sT, tu), (sA, asil), (sV, vsxu)] + pw_)
                        if m == 0 else [(sT, tu), (sA, asil)])
                BK.rel(bk2, sV, vp)
                prev_vp = vp
                vmx = ew("vector", lambda e, d=PTILE[:, m * 512:(m + 1) * 512],
                         c=64 + tau * NIC + m:
                         e.reduce_max(MAXH[:, c:c + 1], d, axis=AX,
                                      apply_absolute_value=True))
                stp = store(Pd[m, :, tau * S:(tau + 1) * S],
                            PTILE[:, m * 512:(m + 1) * 512], sPS,
                            waits=[(sV, vmx)])
            iqf_guard[tau] = sT.n
            ps_tile_done[tau] = sPS.n

        gm2, ldm2 = global_max((64, 240), mx2i, mx2o, 32, xw=[(sMXS, stsd)])
        ab2 = scale_pair(gm2, ldm2, 1, 3, 192)

        es_f1.close()
        IDT = sbf("IDT", NIC * 1024, BF16)
        HQ2 = [sbf("HQFA", NIC * 512, BF16), sbf("HQFB", NIC * 512, BF16)]
        X2L = [sbf("X2TA", 4096), sbf("X2TB", 4096)]
        CBL = [sbf("CBTA", 1024), sbf("CBTB", 1024)]
        BCL = [sbf("BCF0", 1024), sbf("BCF1", 1024)]
        OUT5 = [sbf(f"OUT5{i}", 512) for i in range(2)]
        T512F = sbf("T512F", 512)
        for ki in range(NIC):
            load(IDT[:, ki * 1024:(ki + 1) * 1024], idtd[ki, :, :], sIG,
                 waits=[(sT, sT.n)] if ki == 0 else ())
        ldsdb = load(SDB[:], sdrowd.rearrange("o d -> o d", o=1)
                     .to_broadcast([128, D]), sIG, waits=[(sMXS, stsd)])
        vIDT = sIG.n
        f2bar = [(sT, sT.n), (sV, sV.n), (sA, sA.n), (sIG, sIG.n),
                 (sH2L[0], sH2L[0].n), (sH2L[1], sH2L[1].n)]

        groupsF2 = [(tc, dg) for tc in range(NTC) for dg in range(2)]
        x2_guard = {}
        out_done = {}
        out5_guard = {}
        gstate = {}
        ldx2s = {}
        ldp_tot = {}
        vqs = {}
        EV_AT = (2, 5, 8, 11, 14, 17, 20)

        def f2_group(tau_, j, waits=()):
            tc, dg = groupsF2[j]
            bk = BK.acq()
            mms = [(HQ2[tau_ % 2][:, ki * 512 + tc * 128:
                                  ki * 512 + (tc + 1) * 128],
                    IDT[:, ki * 1024 + dg * 512:ki * 1024 + (dg + 1) * 512])
                   for ki in range(NIC)]
            tdn = mm_group(pb[bk][:], mms, waits=waits)
            return (bk, tdn)

        def f2_evac(tau_, j, bk, tdn):
            tc, dg = groupsF2[j]
            w1 = [(sT, tdn)] + ([(sX2L[tau_ % 2], ldx2s[tau_])]
                                if j == 0 else [])
            ew("vector", lambda e, s=pb[bk][:], dg=dg, p=tau_ % 2:
               e.tensor_mul(T512F[:], s, CBL[p][:, dg * 512:(dg + 1) * 512]),
               waits=w1)
            BK.rel(bk, sV, sV.n)
            sl = j % 2
            vo2 = ew("vector", lambda e, tc=tc, dg=dg, sl=sl, p=tau_ % 2:
                     e.tensor_add(OUT5[sl][:],
                                  X2L[p][:, tc * 1024 + dg * 512:
                                         tc * 1024 + (dg + 1) * 512],
                                  T512F[:]),
                     waits=([(sOUT[sl], out5_guard[sl])]
                            if sl in out5_guard else []))
            out5_guard[sl] = store(
                out_hbm[tau_][tc * 128:(tc + 1) * 128, dg * 512:(dg + 1) * 512],
                OUT5[sl][:], sOUT[sl], waits=[(sV, vo2)])

        def f2_consume(tau_, j):
            bk, tdn = gstate[tau_][j]
            f2_evac(tau_, j, bk, tdn)
            if j < 2:
                gstate[tau_].append(f2_group(tau_, 6 + j))

        ldp = None
        for m in range(NIC):
            ldp = load(PTILE[:, m * 512:(m + 1) * 512], Pd[m, :, 0:S], sPL,
                       waits=[(sPS, sPS.n), (sV, sV.n)] if m == 0 else ())
        ldp_tot[0] = ldp

        for tau in range(C):
            par = tau % 2
            xw = ([(sV, x2_guard[tau - 2])] if tau >= 2 else list(f2bar))
            ldx2s[tau] = load(
                X2L[par][:].rearrange("p (tc d) -> p tc d", tc=NTC),
                x2d[tau * S:(tau + 1) * S, :]
                .rearrange("(tc p) d -> p tc d", p=128), sX2L[par],
                waits=(xw + [(sX2, sX2.n)]) if tau == 0 else xw)
            ldgb = load(BCL[par][:], rowsd[3, tau, :]
                        .rearrange("(o d) -> o d", o=1)
                        .to_broadcast([128, D]), sBC, waits=list(xw))
            ew("vector", lambda e, par=par: e.scalar_tensor_tensor(
                CBL[par][:], BCL[par][:], SXC[:, 3:4], SDB[:],
                ALU.mult, ALU.mult),
                waits=[(sBC, ldgb), (sIG, vIDT), (sA, ab2)])
            vql = []
            for m in range(NIC):
                wsrc = []
                if m == 0:
                    wsrc = [(sPL, ldp_tot[tau])] + (
                        [(sT, out_done[tau - 2])] if tau >= 2 else list(f2bar))
                vq = quant_slab(PTILE[:, m * 512:(m + 1) * 512],
                                HQ2[par][:, m * 512:(m + 1) * 512],
                                SXC[:, 5:6], waits_src=wsrc)
                vql.append(vq)
                if tau >= 1 and m in EV_AT:
                    f2_consume(tau - 1, EV_AT.index(m))
            if tau >= 1:
                f2_consume(tau - 1, 7)
                out_done[tau - 1] = sT.n
                x2_guard[tau - 1] = sV.n
            vqs[tau] = vql
            if tau + 1 < C:
                ldp = None
                for m in range(NIC):
                    ldp = load(PTILE[:, m * 512:(m + 1) * 512],
                               Pd[m, :, (tau + 1) * S:(tau + 2) * S], sPL,
                               waits=[(sV, vql[m])])
                ldp_tot[tau + 1] = ldp
            gstate[tau] = [f2_group(tau, j,
                                    waits=[(sV, vql[-1])] if j == 0 else ())
                           for j in range(6)]
        for j in range(8):
            f2_consume(C - 1, j)
    es_ffn.close()

    with nc.Block() as blk:
        @blk.sync
        def _(e):
            for f in P.q["sync"]:
                f(e)

        @blk.vector
        def _(e):
            for f in P.q["vector"]:
                f(e)

        @blk.scalar
        def _(e):
            for f in P.q["scalar"]:
                f(e)

        @blk.tensor
        def _(e):
            for f in P.q["tensor"]:
                f(e)

        @blk.gpsimd
        def _(e):
            for f in P.q["gpsimd"]:
                f(e)
            for s_ in ALL_DMA_SEMS:
                if s_.n:
                    e.wait_ge(s_.h, s_.n)

    es.close()
    return nc, dbg_names


def make_inputs_per_core(inputs, core):
    x = np.ascontiguousarray(inputs["L0_attn_in_seq"][:, core * S:(core + 1) * S, :])
    eye = np.eye(128, dtype=np.float32)
    return {
        "x": x,
        "tk": np.asarray(inputs["L0_attn_t_k"]),
        "tv": np.asarray(inputs["L0_attn_t_v"]),
        "tq": np.asarray(inputs["L0_attn_t_q"]),
        "W1": np.asarray(inputs["L0_attn_W1"]), "b1": np.asarray(inputs["L0_attn_b1"]),
        "W2": np.asarray(inputs["L0_attn_W2"]), "b2": np.asarray(inputs["L0_attn_b2"]),
        "clip_x": np.asarray(inputs["L0_ffn_clip_x"]),
        "clip_down": np.asarray(inputs["L0_ffn_clip_down"]),
        "wg": np.asarray(inputs["L0_ffn_w_gate"]),
        "wu": np.asarray(inputs["L0_ffn_w_up"]),
        "wd": np.asarray(inputs["L0_ffn_w_down"]),
        "cvec": np.asarray(inputs["L0_c"]), "wmod": np.asarray(inputs["L0_wmod"]),
        "identf": eye,
        "identb": eye.astype(ml_dtypes.bfloat16),
    }


def kernel(**inputs):
    from concourse.bass_utils import run_bass_kernel_spmd
    nc, _ = build_nc(n_steps=8, do_ffn=True, dbg=())
    in_maps = [make_inputs_per_core(inputs, c) for c in range(8)]
    res = run_bass_kernel_spmd(nc, in_maps, core_ids=list(range(8)))
    out = np.empty((C, 4096, D), dtype=np.float32)
    for c in range(8):
        out[:, c * S:(c + 1) * S, :] = np.asarray(res.results[c]["out"])
    return out
